# version 19
# baseline (speedup 1.0000x reference)
import numpy as np
from contextlib import ExitStack
import concourse.bass as bass
import concourse.mybir as mybir
from concourse.bass_utils import run_bass_kernel_spmd

F32 = mybir.dt.float32
BF16 = mybir.dt.bfloat16
AF = mybir.ActivationFunctionType
OP = mybir.AluOpType
AX = mybir.AxisListType

N_SUBLAYERS = 4

B = 2
T = 2048
C = 1024
NCH = 8
L = 256
J = 256
NH = 16
HS = 64
E = 16
FP = 256
NTB = 4
NKT = 16
EPS_LN = 1e-5


class Ctr:
    def __init__(self, sem, name):
        self.sem = sem
        self.name = name
        self.val = 0


class Prog:
    ENGS = ("t", "v", "s", "g", "q")

    def __init__(self):
        self.q = {e: [] for e in self.ENGS}
        self.waited = {}
        self.own = {}

    def emit(self, eng, f, inc=None, n=1):
        own = self.own.get(eng)
        if own is not None:
            self.wait(eng, own)
            if inc is None:
                inc = own
        if inc is not None:
            inc.val += n
            sem, amount = inc.sem, n

            def g(En):
                f(En).then_inc(sem, amount)
        else:
            def g(En):
                f(En)
        self.q[eng].append(g)
        return inc.val if inc is not None else None

    def wait(self, eng, ctr, v=None):
        if v is None:
            v = ctr.val
        assert v <= ctr.val, f"wait {ctr.name}>={v} but only {ctr.val} emitted"
        if v <= 0:
            return
        k = (eng, ctr.name)
        if self.waited.get(k, 0) >= v:
            return
        self.waited[k] = v
        sem = ctr.sem
        self.q[eng].append(lambda En: En.wait_ge(sem, v))

    def waitt(self, eng, tok):
        ctr, v = tok
        self.wait(eng, ctr, v)


class DmaQ:

    def __init__(self, P, eng, ctrs):
        self.P, self.eng, self.ctrs = P, eng, ctrs
        self.i = 0

    def dma(self, out, in_, **kw):
        c = self.ctrs[self.i % len(self.ctrs)]
        self.i += 1
        self.P.wait(self.eng, c)
        c.val += 16
        sem, v = c.sem, c.val

        def g(En):
            En.dma_start(out=out, in_=in_, **kw).then_inc(sem, 16)
        self.P.q[self.eng].append(g)
        return (c, v)


def build_program():
    nc = bass.Bass("TRN2", target_bir_lowering=False)

    xin = nc.dram_tensor("xin", [B, NCH, 128, T], F32, kind="ExternalInput").ap()
    out_fm = nc.dram_tensor("out_fm", [B, NCH, 128, T], F32, kind="ExternalOutput").ap()

    awdqt = nc.dram_tensor("awdqt", [NCH, 128, J], F32, kind="ExternalInput").ap()
    awdq = nc.dram_tensor("awdq", [2, 128, C], F32, kind="ExternalInput").ap()
    awuqt = nc.dram_tensor("awuqt", [2, 128, C], F32, kind="ExternalInput").ap()
    awuq = nc.dram_tensor("awuq", [NCH, 128, J], F32, kind="ExternalInput").ap()
    awuk = nc.dram_tensor("awuk", [NCH, 128, L], F32, kind="ExternalInput").ap()
    awdkvt = nc.dram_tensor("awdkvt", [NCH, 128, L], F32, kind="ExternalInput").ap()
    awuv = nc.dram_tensor("awuv", [NCH, 128, L], F32, kind="ExternalInput").ap()
    awot = nc.dram_tensor("awot", [NCH, 128, C], F32, kind="ExternalInput").ap()

    bwq = nc.dram_tensor("bwq", [NCH, C, 128], F32, kind="ExternalInput").ap()
    bwk = nc.dram_tensor("bwk", [NCH, C, 128], F32, kind="ExternalInput").ap()
    bwv = nc.dram_tensor("bwv", [NCH, C, 128], F32, kind="ExternalInput").ap()
    bwot = nc.dram_tensor("bwot", [NCH, 128, C], F32, kind="ExternalInput").ap()

    fwg = nc.dram_tensor("fwg", [NCH, 128, E], F32, kind="ExternalInput").ap()
    fbg = nc.dram_tensor("fbg", [16, 1], F32, kind="ExternalInput").ap()
    fw1 = nc.dram_tensor("fw1", [E, NCH, 128, FP], F32, kind="ExternalInput").ap()
    fb1 = nc.dram_tensor("fb1", [E, FP], F32, kind="ExternalInput").ap()
    fw2 = nc.dram_tensor("fw2", [E, 2, 128, C], F32, kind="ExternalInput").ap()
    fb2 = nc.dram_tensor("fb2", [E, C], F32, kind="ExternalInput").ap()
    fgam = nc.dram_tensor("fgam", [NCH, 128], F32, kind="ExternalInput").ap()
    fbet = nc.dram_tensor("fbet", [NCH, 128], F32, kind="ExternalInput").ap()

    ident16 = nc.dram_tensor("ident16", [16, 16], F32, kind="ExternalInput").ap()
    identb = nc.dram_tensor("identb", [128, 128], BF16, kind="ExternalInput").ap()
    tri_incl = nc.dram_tensor("tri_incl", [128, 128], BF16, kind="ExternalInput").ap()
    tri_slow = nc.dram_tensor("tri_slow", [128, 128], BF16, kind="ExternalInput").ap()
    onesf = nc.dram_tensor("onesf", [128, 128], F32, kind="ExternalInput").ap()
    onesb = nc.dram_tensor("onesb", [128, 128], BF16, kind="ExternalInput").ap()
    onehot = nc.dram_tensor("onehot", [16, 2048], BF16, kind="ExternalInput").ap()
    zerob = nc.dram_tensor("zerob", [128, 64], BF16, kind="ExternalInput").ap()

    xres = nc.dram_tensor("xres", [B, NCH, 128, T], F32).ap()
    xnh_d = nc.dram_tensor("xnh_d", [B, NCH, 128, T], BF16).ap()
    xnl_d = nc.dram_tensor("xnl_d", [B, NCH, 128, T], BF16).ap()

    ctx = ExitStack()

    def sbuf(name, shape, dt=F32):
        return ctx.enter_context(nc.sbuf_tensor(name, shape, dt))

    c_ident16 = sbuf("c_ident16", [16, 16])
    c_identb = sbuf("c_identb", [128, 128], BF16)
    c_tri = sbuf("c_tri", [128, 128], BF16)
    c_tril = sbuf("c_tril", [128, 128], BF16)
    c_onesf = sbuf("c_onesf", [128, 128])
    c_onesb = sbuf("c_onesb", [128, 128], BF16)
    c_onehot = sbuf("c_onehot", [16, 2048], BF16)
    c_zerob = sbuf("c_zerob", [128, 64], BF16)
    gam_col = sbuf("gam_col", [128, NCH])
    bet_col = sbuf("bet_col", [128, NCH])
    bg_col = sbuf("bg_col", [16, 1])
    b1_col = sbuf("b1_col", [128, 2 * E])
    b2_hi = sbuf("b2_hi", [16, C], BF16)
    b2_lo = sbuf("b2_lo", [16, C], BF16)
    wgf = sbuf("wgf", [128, NCH * E])

    wstage = sbuf("wstage", [128, 2 * C])

    xtile = sbuf("xtile", [128, NCH * 512])
    xsq = sbuf("xsq", [128, 512])
    xnf = sbuf("xnf", [128, 2 * 512])
    xn_hi = sbuf("xn_hi", [128, NCH * 512], BF16)
    xn_lo = sbuf("xn_lo", [128, NCH * 512], BF16)
    rrep = sbuf("rrep", [128, 512])
    mrep = sbuf("mrep", [128, 512])
    rowsb_r = sbuf("rowsb_r", [1, 512])
    rowsb_m = sbuf("rowsb_m", [1, 512])
    xadd = sbuf("xadd", [128, 2 * 512])

    PBCOLS = 61440
    PFCOLS = 4864
    pool_b = sbuf("pool_b", [128, PBCOLS], BF16)
    pool_f = sbuf("pool_f", [128, PFCOLS])

    AW = 0
    aw_wdqt_h, aw_wdqt_l = 0, 2048
    aw_wuqt_h, aw_wuqt_l = 4096, 6144
    aw_wdkvt_h, aw_wdkvt_l = 8192, 10240
    aw_keff_h, aw_keff_l = 12288, 14336
    aw_veff_h, aw_veff_l = 16384, 18432
    AW_END = 20480

    PB = [ctx.enter_context(nc.psum_tensor(f"pb{i}", [128, 512], F32)) for i in range(8)]

    cT = Ctr(ctx.enter_context(nc.semaphore("cT")), "cT")
    cV = Ctr(ctx.enter_context(nc.semaphore("cV")), "cV")
    cS = Ctr(ctx.enter_context(nc.semaphore("cS")), "cS")
    cG = Ctr(ctx.enter_context(nc.semaphore("cG")), "cG")
    QCT = [Ctr(ctx.enter_context(nc.semaphore(f"cQ{i}")), f"cQ{i}") for i in range(8)]
    WCT = [Ctr(ctx.enter_context(nc.semaphore(f"cW{i}")), f"cW{i}") for i in range(8)]
    ALLC = [cT, cV, cS, cG] + QCT + WCT

    P = Prog()
    P.own = {"v": cV, "s": cS, "g": cG}
    QD = DmaQ(P, "q", QCT)
    GD = DmaQ(P, "g", WCT)

    def pbm(a, n):
        return pool_b[:, a:a + n]

    def pfm(a, n):
        return pool_f[:, a:a + n]

    wdqt_hi = pbm(aw_wdqt_h, 2048); wdqt_lo = pbm(aw_wdqt_l, 2048)
    wuqt_hi = pbm(aw_wuqt_h, 2048); wuqt_lo = pbm(aw_wuqt_l, 2048)
    wdkvt_hi = pbm(aw_wdkvt_h, 2048); wdkvt_lo = pbm(aw_wdkvt_l, 2048)
    keff_hi = pbm(aw_keff_h, 2048); keff_lo = pbm(aw_keff_l, 2048)
    veff_hi = pbm(aw_veff_h, 2048); veff_lo = pbm(aw_veff_l, 2048)

    bank_deps = [[] for _ in range(8)]

    def bank_acquire(i):
        for (c, v) in bank_deps[i]:
            P.wait("t", c, v)
        bank_deps[i] = []

    def bank_consumed(i, *ctrs):
        bank_deps[i] = [(c, c.val) for c in ctrs]

    def stage_barrier():
        for e in Prog.ENGS:
            for c in ALLC:
                P.wait(e, c)

    grp_open = {}
    import traceback as _tb

    def gtrack(out, start, stop):
        name = out.tensor.name if hasattr(out, "tensor") else "?"
        if start:
            assert not grp_open.get(name), f"group restart on open {name}"
            grp_open[name] = True
        else:
            assert grp_open.get(name), f"accumulate into unopened group {name}"
        if stop:
            grp_open[name] = False

    class TEng:

        def __init__(self, E):
            self.E = E

        def matmul(self, out, lh, rh, **kw):
            return self.E.matmul(out, lh, rh, **kw)

    _orig_emit = P.emit

    def emit_t_checked(eng, f, inc=None, n=1):
        if eng == "t":
            class Rec:
                def __init__(self):
                    self.calls = []

                def matmul(self, out, lh, rh, start=None, stop=None, **kw):
                    self.calls.append((out, bool(start), bool(stop)))
                    class _I:
                        def then_inc(self, *a):
                            return self
                    return _I()

                def __getattr__(self, k):
                    def f2(*a, **kw):
                        class _I:
                            def then_inc(self, *a2):
                                return self
                        return _I()
                    return f2
            r = Rec()
            try:
                f(r)
                for (out, st, sp) in r.calls:
                    if hasattr(out, "tensor") and "pb" in str(getattr(out.tensor, "name", "")):
                        gtrack(out, st, sp)
            except AssertionError:
                raise
            except Exception:
                pass
        return _orig_emit(eng, f, inc=inc, n=n)

    P.emit = emit_t_checked

    def mm3(out, lh, ll, rh, rl, start, stop, tp=None, inc=True):
        kw = {} if tp is None else {"tile_position": tp}
        P.emit("t", lambda E: E.matmul(out, lh, rh, start=start, stop=False, **kw))
        P.emit("t", lambda E: E.matmul(out, lh, rl, start=False, stop=False, **kw))
        return P.emit("t", lambda E: E.matmul(out, ll, rh, start=False, stop=stop, **kw),
                      inc=cT if inc else None)

    def split_psum(psum_ap, hi_ap, lo_ap, tval):
        P.wait("s", cT, tval)
        sv = P.emit("s", lambda E: E.activation(hi_ap, psum_ap, AF.Copy), inc=cS)
        P.wait("v", cT, tval)
        P.wait("v", cS, sv)
        vv = P.emit("v", lambda E: E.tensor_tensor(lo_ap, psum_ap, hi_ap, OP.subtract), inc=cV)
        return sv, vv

    def r3(sb_ap, c, f):
        return sb_ap.rearrange("p (c f) -> p c f", c=c, f=f)

    def load_split(dram_re, hi_ap, lo_ap, stage_ap, c=None, f=None):
        if c is not None:
            t1 = GD.dma(r3(hi_ap, c, f), dram_re)
            t2 = GD.dma(r3(stage_ap, c, f), dram_re)
        else:
            t1 = GD.dma(hi_ap, dram_re)
            t2 = GD.dma(stage_ap, dram_re)
        P.waitt("v", t1)
        P.waitt("v", t2)
        return P.emit("v", lambda E: E.tensor_tensor(lo_ap, stage_ap, hi_ap, OP.subtract), inc=cV)

    GD.dma(c_ident16[:], ident16[:, :])
    GD.dma(c_identb[:], identb[:, :])
    GD.dma(c_tri[:], tri_incl[:, :])
    GD.dma(c_tril[:], tri_slow[:, :])
    GD.dma(c_onesf[:], onesf[:, :])
    GD.dma(c_onesb[:], onesb[:, :])
    GD.dma(c_onehot[:], onehot[:, :])
    GD.dma(c_zerob[:], zerob[:, :])
    GD.dma(gam_col[:], fgam.rearrange("c p -> p c"), allow_slow_non_contiguous=True)
    GD.dma(bet_col[:], fbet.rearrange("c p -> p c"), allow_slow_non_contiguous=True)
    GD.dma(bg_col[:], fbg[:, :])
    GD.dma(b1_col.ap().rearrange("p (e c) -> p e c", e=E, c=2), fb1.rearrange("e (c p) -> p e c", p=128), allow_slow_non_contiguous=True)
    v_ = load_split(fb2[:, :], b2_hi[:], b2_lo[:], wstage[0:16, 0:C])
    P.wait("g", cV, v_)
    GD.dma(wgf.ap().rearrange("p (c f) -> p c f", c=NCH, f=E), fwg.rearrange("c p f -> p c f"))

    v_ = load_split(awdqt.rearrange("c p f -> p c f"), wdqt_hi, wdqt_lo, wstage[:, 0:NCH * J], c=NCH, f=J)
    P.wait("g", cV, v_)
    v_ = load_split(awdkvt.rearrange("c p f -> p c f"), wdkvt_hi, wdkvt_lo, wstage[:, 0:NCH * L], c=NCH, f=L)
    P.wait("g", cV, v_)

    base = AW_END
    s_wuq, s_wuq_l = base, base + NCH * J
    base += 2 * NCH * J
    s_wuk, s_wuk_l = base, base + NCH * L
    base += 2 * NCH * L
    s_wdq, s_wdq_l = base, base + 2 * C
    base += 4 * C
    s_A, s_A_l = base, base + 2 * L
    base += 4 * L
    s_wuv, s_wuv_l = base, base + NCH * L
    base += 2 * NCH * L
    s_wot, s_wot_l = base, base + NCH * C
    assert base + 2 * NCH * C <= PBCOLS

    v_ = load_split(awuq.rearrange("c p f -> p c f"), pbm(s_wuq, NCH * J), pbm(s_wuq_l, NCH * J),
                    wstage[:, 0:NCH * J], c=NCH, f=J)
    P.wait("g", cV, v_)
    v_ = load_split(awuk.rearrange("c p f -> p c f"), pbm(s_wuk, NCH * L), pbm(s_wuk_l, NCH * L),
                    wstage[:, 0:NCH * L], c=NCH, f=L)
    P.wait("g", cV, v_)
    v_ = load_split(awdq.rearrange("c p f -> p c f"), pbm(s_wdq, 2 * C), pbm(s_wdq_l, 2 * C),
                    wstage[:, 0:2 * C], c=2, f=C)
    P.wait("g", cV, v_)
    v_ = load_split(awuv.rearrange("c p f -> p c f"), pbm(s_wuv, NCH * L), pbm(s_wuv_l, NCH * L),
                    wstage[:, 0:NCH * L], c=NCH, f=L)
    P.wait("g", cV, v_)
    for qtr in range(4):
        cols = 2 * C
        v_ = load_split(awot[2 * qtr:2 * qtr + 2].rearrange("c p f -> p c f"),
                        pbm(s_wot + qtr * cols, cols), pbm(s_wot_l + qtr * cols, cols),
                        wstage[:, 0:cols], c=2, f=C)
        P.wait("g", cV, v_)

    P.wait("t", cV)
    for jc in range(2):
        bank_acquire(jc)
        for kc in range(NCH):
            mm3(PB[jc][:, 0:L],
                pbm(s_wuq + kc * J + jc * 128, 128), pbm(s_wuq_l + kc * J + jc * 128, 128),
                pbm(s_wuk + kc * L, L), pbm(s_wuk_l + kc * L, L),
                start=(kc == 0), stop=(kc == NCH - 1), inc=(kc == NCH - 1))
        sv, vv = split_psum(PB[jc][:, 0:L], pbm(s_A + jc * L, L), pbm(s_A_l + jc * L, L), cT.val)
        bank_consumed(jc, cS, cV)
    vA = cV.val

    P.wait("t", cV, vA)
    for cc in range(NCH):
        bk = 2 + (cc % 2)
        bank_acquire(bk)
        for jc in range(2):
            mm3(PB[bk][:, 0:L],
                pbm(s_wdq + jc * C + cc * 128, 128), pbm(s_wdq_l + jc * C + cc * 128, 128),
                pbm(s_A + jc * L, L), pbm(s_A_l + jc * L, L),
                start=(jc == 0), stop=(jc == 1), inc=(jc == 1))
        split_psum(PB[bk][:, 0:L], keff_hi[:, cc * L:(cc + 1) * L], keff_lo[:, cc * L:(cc + 1) * L], cT.val)
        bank_consumed(bk, cS, cV)

    for lc in range(2):
        for mb in range(2):
            bk = 4 + mb
            bank_acquire(bk)
            for kc in range(NCH):
                mm3(PB[bk][:, 0:512],
                    pbm(s_wuv + kc * L + lc * 128, 128), pbm(s_wuv_l + kc * L + lc * 128, 128),
                    pbm(s_wot + kc * C + mb * 512, 512), pbm(s_wot_l + kc * C + mb * 512, 512),
                    start=(kc == 0), stop=(kc == NCH - 1), inc=(kc == NCH - 1))
            split_psum(PB[bk][:, 0:512], veff_hi[:, lc * C + mb * 512: lc * C + (mb + 1) * 512],
                       veff_lo[:, lc * C + mb * 512: lc * C + (mb + 1) * 512], cT.val)
            bank_consumed(bk, cS, cV)

    def ln_stats(chunk_fn, twait=None, blk=0):
        A = 6 + (blk % 2)
        bank_acquire(A)
        bank_acquire(5)
        tch = {0: 0, 1: 0}
        xsqs = (xsq.ap(), xadd[:, 512:1024])
        for ch in range(NCH):
            src = chunk_fn(ch)
            xq = xsqs[ch % 2]
            if twait is not None:
                twait("s")
            P.wait("s", cT, tch[ch % 2])
            sv = P.emit("s", lambda E, src=src, xq=xq: E.activation(xq, src, AF.Square), inc=cS)
            if twait is not None:
                twait("t")
            P.wait("t", cS, sv)
            P.emit("t", lambda E, src=src, ch=ch, A=A: E.matmul(
                PB[A][0:1, 0:512], c_onesf[:, 0:1], src,
                start=(ch == 0), stop=(ch == NCH - 1)))
            P.emit("t", lambda E, xq=xq, ch=ch: E.matmul(
                PB[5][0:1, 0:512], c_onesf[:, 0:1], xq,
                start=(ch == 0), stop=(ch == NCH - 1)), inc=cT)
            tch[ch % 2] = cT.val
        tstat = cT.val
        P.wait("v", cT, tstat)
        P.emit("v", lambda E, A=A: E.tensor_scalar_mul(rowsb_m[:], PB[A][0:1, 0:512], 1.0 / C))
        P.emit("v", lambda E: E.tensor_tensor(rowsb_r[:], rowsb_m[:], rowsb_m[:], OP.mult))
        vv = P.emit("v", lambda E: E.scalar_tensor_tensor(
            rowsb_r[:], PB[5][0:1, 0:512], 1.0 / C, rowsb_r[:],
            op0=OP.mult, op1=OP.subtract), inc=cV)
        bank_consumed(A, cV)
        bank_consumed(5, cV)
        vv = P.emit("v", lambda E: E.tensor_scalar_add(rowsb_r[:], rowsb_r[:], EPS_LN), inc=cV)
        P.wait("s", cV, vv)
        sv = P.emit("s", lambda E: E.activation(rowsb_r[:], rowsb_r[:], AF.Sqrt), inc=cS)
        P.wait("v", cS, sv)
        P.emit("v", lambda E: E.reciprocal(rowsb_r[:], rowsb_r[:]))
        vv = P.emit("v", lambda E: E.tensor_tensor(rowsb_m[:], rowsb_m[:], rowsb_r[:],
                                                   OP.mult), inc=cV)
        bank_acquire(A); bank_acquire(5)
        P.wait("t", cV, vv)
        P.emit("t", lambda E, A=A: E.matmul(PB[A][:, 0:512], c_onesf[0:1, :], rowsb_r[0:1, :],
                                            start=True, stop=True))
        P.emit("t", lambda E: E.matmul(PB[5][:, 0:512], c_onesf[0:1, :], rowsb_m[0:1, :],
                                       start=True, stop=True), inc=cT)
        trep = cT.val
        P.wait("v", cT, trep)
        P.emit("v", lambda E, A=A: E.tensor_copy(rrep[:], PB[A][:, 0:512]))
        vv = P.emit("v", lambda E: E.tensor_copy(mrep[:], PB[5][:, 0:512]), inc=cV)
        bank_consumed(A, cV); bank_consumed(5, cV)
        return tstat, vv

    lnpre = [None]
    xtile_free = {"tv": (0, 0, 0)}

    def ln_prefetch(next_src):
        if next_src is None:
            return
        t_, s_, v_ = xtile_free["tv"]
        P.wait("q", cT, t_)
        P.wait("q", cS, s_)
        P.wait("q", cV, v_)
        lnpre[0] = QD.dma(xtile.ap().rearrange("p (c t) -> p c t", c=NCH, t=512),
                          next_src[:, :, 0:512].rearrange("c p t -> p c t"))

    def ln_pass(xsrc_b, sink, write_xn=None, gate=False):
        sink_t = 0
        lnbufs = (xtile.ap(), pool_f[:, 768:768 + NCH * 512])
        prev_stat = [0, 0]
        prev_aff = [0, 0]
        prev_xn_toks = []
        tgate = {}
        for blkq in range(NTB):
            t0 = blkq * 512
            a = blkq % 2
            xt = lnbufs[a]
            if blkq == 0 and lnpre[0] is not None:
                tq = lnpre[0]
                lnpre[0] = None
            else:
                P.wait("q", cT, prev_stat[a])
                P.wait("q", cV, prev_aff[a])
                tq = QD.dma(xt.rearrange("p (c t) -> p c t", c=NCH, t=512),
                            xsrc_b[:, :, t0:t0 + 512].rearrange("c p t -> p c t"))
            tstat, _ = ln_stats(lambda ch: xt[:, ch * 512:(ch + 1) * 512],
                                twait=lambda e: P.waitt(e, tq), blk=blkq)
            prev_stat[a] = tstat
            P.wait("v", cT, sink_t)
            P.wait("s", cT, sink_t)
            for tk in prev_xn_toks:
                P.waitt("v", tk)
                P.waitt("s", tk)
            for ch in range(NCH):
                a2 = ch % 2
                if gate and ch >= 2:
                    P.wait("v", cT, tgate.get(ch - 2, 0))
                P.emit("v", lambda E, ch=ch, a2=a2, xt=xt: E.tensor_tensor(
                    xnf[:, a2 * 512:(a2 + 1) * 512], xt[:, ch * 512:(ch + 1) * 512], rrep[:], OP.mult))
                vv = P.emit("v", lambda E, ch=ch, a2=a2: E.tensor_tensor(
                    xnf[:, a2 * 512:(a2 + 1) * 512], xnf[:, a2 * 512:(a2 + 1) * 512], mrep[:],
                    OP.subtract), inc=cV)
                if gate:
                    if ch == 0:
                        bank_acquire(4)
                    P.wait("t", cV, vv)
                    P.emit("t", lambda En, ch=ch: En.matmul(
                        PB[4][0:16, 0:512], wgf[:, ch * 16:(ch + 1) * 16],
                        xnf[:, (ch % 2) * 512:((ch % 2) + 1) * 512],
                        start=(ch == 0), stop=(ch == NCH - 1)),
                        inc=cT)
                    tgate[ch] = cT.val
                P.wait("s", cV, vv)
                sv = P.emit("s", lambda E, ch=ch, a2=a2: E.activation(
                    xn_hi[:, ch * 512:(ch + 1) * 512], xnf[:, a2 * 512:(a2 + 1) * 512], AF.Copy), inc=cS)
                P.wait("v", cS, sv)
                P.emit("v", lambda E, ch=ch, a2=a2: E.tensor_tensor(
                    xn_lo[:, ch * 512:(ch + 1) * 512], xnf[:, a2 * 512:(a2 + 1) * 512],
                    xn_hi[:, ch * 512:(ch + 1) * 512], OP.subtract), inc=cV)
            vsplit = cV.val
            prev_aff[a] = vsplit
            if write_xn is not None:
                hi_d, lo_d = write_xn
                P.wait("q", cV, vsplit)
                tk1 = QD.dma(hi_d[:, :, t0:t0 + 512].rearrange("c p t -> p c t"),
                             xn_hi.ap().rearrange("p (c t) -> p c t", c=NCH, t=512))
                tk2 = QD.dma(lo_d[:, :, t0:t0 + 512].rearrange("c p t -> p c t"),
                             xn_lo.ap().rearrange("p (c t) -> p c t", c=NCH, t=512))
                prev_xn_toks = [tk1, tk2]
            sink_t = sink(blkq, vsplit)
            if blkq == 2:
                xtile_free["tv"] = (cT.val, cS.val, cV.val)
        return sink_t

    xadd_st = [None, None]

    def attn_a(b, xsrc, xdst, nxt=None):
        A0 = AW_END
        ar_ckvT_h, ar_ckvT_l = A0, A0 + 2 * T
        ar_ckv_h, ar_ckv_l = A0 + 4 * T, A0 + 6 * T
        ar_q1_h, ar_q1_l = A0 + 8 * T, A0 + 10 * T
        ar_qlT_h, ar_qlT_l = A0 + 12 * T, A0 + 14 * T
        base_a = A0 + 16 * T
        ar_wuq_h, ar_wuq_l = base_a, base_a + 256
        ar_M_h, ar_M_l = base_a + 512, base_a + 1536
        ar_pt_h, ar_pt_l = base_a + 2560, base_a + 3584
        ar_ctx_h, ar_ctx_l = base_a + 4608, base_a + 5632
        assert base_a + 6656 <= 59904

        def pth_a(pb_):
            return ar_pt_h + pb_ * 512 if pb_ < 2 else 59904

        def ptl_a(pb_):
            return ar_pt_l + pb_ * 512 if pb_ < 2 else 60416
        pf_ybuf = 0
        pf_ctxf = 2048
        pf_rsum = 3072
        pf_rra = 3584
        pf_wuq = 4096

        def proj_sink(blkq, vsplit):
            t0 = blkq * 512
            P.wait("t", cV, vsplit)
            for lc in range(2):
                bk = lc
                bank_acquire(bk)
                for kc in range(NCH):
                    mm3(PB[bk][:, 0:512],
                        wdkvt_hi[:, kc * L + lc * 128: kc * L + (lc + 1) * 128],
                        wdkvt_lo[:, kc * L + lc * 128: kc * L + (lc + 1) * 128],
                        xn_hi[:, kc * 512:(kc + 1) * 512], xn_lo[:, kc * 512:(kc + 1) * 512],
                        start=(kc == 0), stop=(kc == NCH - 1), inc=(kc == NCH - 1))
                split_psum(PB[bk][:, 0:512], pbm(ar_ckvT_h + lc * T + t0, 512),
                           pbm(ar_ckvT_l + lc * T + t0, 512), cT.val)
                bank_consumed(bk, cS, cV)
            for jc in range(2):
                bk = 2 + jc
                bank_acquire(bk)
                for kc in range(NCH):
                    mm3(PB[bk][:, 0:512],
                        wdqt_hi[:, kc * J + jc * 128: kc * J + (jc + 1) * 128],
                        wdqt_lo[:, kc * J + jc * 128: kc * J + (jc + 1) * 128],
                        xn_hi[:, kc * 512:(kc + 1) * 512], xn_lo[:, kc * 512:(kc + 1) * 512],
                        start=(kc == 0), stop=(kc == NCH - 1), inc=(kc == NCH - 1))
                split_psum(PB[bk][:, 0:512], pbm(ar_q1_h + jc * T + t0, 512),
                           pbm(ar_q1_l + jc * T + t0, 512), cT.val)
                bank_consumed(bk, cS, cV)
            return cT.val

        ln_pass(xsrc, proj_sink)
        ln_prefetch(nxt)

        for lc in range(2):
            for kt in range(NKT):
                bk = 4 + (kt % 2)
                bank_acquire(bk)
                bkb = PB[bk].ap().bitcast(BF16)
                P.emit("t", lambda E, lc=lc, kt=kt, bkb=bkb: E.matmul(
                    bkb[:, 0:128], pbm(ar_ckvT_h + lc * T + kt * 128, 128), c_identb[:],
                    is_transpose=True, start=True, stop=False))
                P.emit("t", lambda E, lc=lc, kt=kt, bkb=bkb: E.matmul(
                    bkb[:, 128:256], pbm(ar_ckvT_l + lc * T + kt * 128, 128), c_identb[:],
                    is_transpose=True, start=False, stop=True), inc=cT)
                tv = cT.val
                P.wait("v", cT, tv)
                P.emit("v", lambda E, lc=lc, kt=kt, bkb=bkb: E.tensor_copy(
                    pbm(ar_ckv_h + kt * L + lc * 128, 128), bkb[:, 0:128]))
                vv = P.emit("v", lambda E, lc=lc, kt=kt, bkb=bkb: E.tensor_copy(
                    pbm(ar_ckv_l + kt * L + lc * 128, 128), bkb[:, 128:256]), inc=cV)
                bank_consumed(bk, cV)

        tqlat_prev = 0
        pend_tail_a = [None]

        def emit_tail_a(qb, h, half, hh):
            p_ = qb % 2
            cbs = (2 + 2 * p_, 3 + 2 * p_)
            rb_ = 6 + p_
            pj_ = h // 2
            P.wait("s", cT)
            rr = pfm(pf_rsum, 512)[0:1, :]
            P.emit("s", lambda E, rr=rr, rb_=rb_: E.activation(
                rr, PB[rb_][0:1, 0:512], AF.Ln))
            sv = P.emit("s", lambda E, rr=rr: E.activation(
                rr, rr, AF.Exp, scale=-1.0), inc=cS)
            bank_consumed(rb_, cS)
            bank_acquire(rb_)
            P.wait("t", cS, sv)
            P.emit("t", lambda E, rr=rr, rb_=rb_: E.matmul(
                PB[rb_][:, 0:512], c_onesf[0:1, :], rr, start=True, stop=True), inc=cT)
            trr = cT.val
            P.wait("s", cT, trr)
            rra = pfm(pf_rra, 512)
            sv = P.emit("s", lambda E, rra=rra, rb_=rb_: E.activation(
                rra, PB[rb_][:, 0:512], AF.Copy), inc=cS)
            bank_consumed(rb_, cS)
            for lc in range(2):
                cf = pfm(pf_ctxf + lc * 512, 512)
                chi = pbm(ar_ctx_h + lc * 512, 512)
                clo = pbm(ar_ctx_l + lc * 512, 512)
                P.wait("v", cS, sv)
                vv2 = P.emit("v", lambda E, cf=cf, cb=cbs[lc], rra=rra: E.tensor_tensor(
                    cf, PB[cb][:, 0:512], rra, OP.mult), inc=cV)
                P.wait("s", cV, vv2)
                sv = P.emit("s", lambda E, chi=chi, cf=cf: E.activation(chi, cf, AF.Copy), inc=cS)
                P.wait("v", cS, sv)
                P.emit("v", lambda E, clo=clo, cf=cf, chi=chi: E.tensor_tensor(
                    clo, cf, chi, OP.subtract), inc=cV)
                bank_consumed(cbs[lc], cV)
            vctx = cV.val
            P.wait("t", cV, vctx)
            bank_acquire(cbs[0])
            for lc in range(2):
                mm3(PB[cbs[0]][half:half + 64, 0:512],
                    veff_hi[:, lc * C + h * 64: lc * C + (h + 1) * 64],
                    veff_lo[:, lc * C + h * 64: lc * C + (h + 1) * 64],
                    pbm(ar_ctx_h + lc * 512, 512), pbm(ar_ctx_l + lc * 512, 512),
                    start=(lc == 0), stop=(lc == 1), tp=(0, half),
                    inc=(lc == 1))
            ty = cT.val
            if hh == 0:
                P.wait("v", cT, ty)
                vv = P.emit("v", lambda E, qb=qb, cb=cbs[0]: E.tensor_copy(
                    pool_f[0:64, pf_ybuf + qb * 512: pf_ybuf + (qb + 1) * 512],
                    PB[cb][0:64, 0:512]), inc=cV)
                bank_consumed(cbs[0], cV)
            else:
                t0 = qb * 512
                a = qb % 2
                if xadd_st[a] is not None:
                    P.waitt("q", xadd_st[a])
                tq = QD.dma(xadd[:, a * 512:(a + 1) * 512],
                            xsrc[pj_, :, t0:t0 + 512])
                P.wait("v", cT, ty)
                P.waitt("v", tq)
                P.emit("v", lambda E, a=a, qb=qb: E.tensor_tensor(
                    xadd[0:64, a * 512:(a + 1) * 512], xadd[0:64, a * 512:(a + 1) * 512],
                    pool_f[0:64, pf_ybuf + qb * 512: pf_ybuf + (qb + 1) * 512], OP.add))
                vv = P.emit("v", lambda E, a=a, cb=cbs[0]: E.tensor_tensor(
                    xadd[64:128, a * 512:(a + 1) * 512], xadd[64:128, a * 512:(a + 1) * 512],
                    PB[cb][64:128, 0:512], OP.add), inc=cV)
                bank_consumed(cbs[0], cV)
                P.wait("q", cV, vv)
                xadd_st[a] = QD.dma(xdst[pj_, :, t0:t0 + 512],
                                    xadd[:, a * 512:(a + 1) * 512])

        P.wait("g", cT)
        P.wait("g", cS)
        P.wait("g", cV)
        wuq_tok = [None, None]
        wuq_vwu = [0, 0]
        wuq_tok[0] = GD.dma(pool_f[:, pf_wuq:pf_wuq + J], awuq[0][:, :])
        for pj in range(8):
            sl = pj % 2
            stg = pool_f[:, pf_wuq + sl * J:pf_wuq + (sl + 1) * J]
            P.wait("s", cT, tqlat_prev)
            P.waitt("s", wuq_tok[sl])
            svw = P.emit("s", lambda E, stg=stg: E.activation(
                pbm(ar_wuq_h, J), stg, AF.Copy), inc=cS)
            P.wait("v", cS, svw)
            vwu = P.emit("v", lambda E, stg=stg: E.tensor_tensor(
                pbm(ar_wuq_l, J), stg, pbm(ar_wuq_h, J), OP.subtract), inc=cV)
            wuq_vwu[sl] = vwu
            if pj < 7:
                nsl = (pj + 1) % 2
                P.wait("g", cV, wuq_vwu[nsl])
                wuq_tok[nsl] = GD.dma(pool_f[:, pf_wuq + nsl * J:pf_wuq + (nsl + 1) * J],
                                      awuq[pj + 1][:, :])

            for hh in range(2):
                h = 2 * pj + hh
                half = hh * 64
                tp = (half, 0)
                P.wait("t", cV, vwu)
                P.wait("t", cT, tqlat_prev)
                for jc in range(2):
                    bk = 2 + jc
                    bank_acquire(bk)
                    for lc in range(2):
                        mm3(PB[bk][:, lc * 128:(lc + 1) * 128],
                            pbm(ar_wuq_h + jc * 128, 128)[half:half + 64, :],
                            pbm(ar_wuq_l + jc * 128, 128)[half:half + 64, :],
                            keff_hi[half:half + 64, pj * L + lc * 128: pj * L + (lc + 1) * 128],
                            keff_lo[half:half + 64, pj * L + lc * 128: pj * L + (lc + 1) * 128],
                            start=(lc == 0), stop=(lc == 1), tp=tp, inc=(lc == 1))
                    split_psum(PB[bk][:, 0:256], pbm(ar_M_h + jc * 256, 256),
                               pbm(ar_M_l + jc * 256, 256), cT.val)
                    bank_consumed(bk, cS, cV)
                vM = cV.val
                P.wait("t", cV, vM)
                for lc in range(2):
                    for tb in range(NTB):
                        bk = (tb % 2)
                        bank_acquire(bk)
                        for jc in range(2):
                            mm3(PB[bk][:, 0:512],
                                pbm(ar_M_h + jc * 256 + lc * 128, 128),
                                pbm(ar_M_l + jc * 256 + lc * 128, 128),
                                pbm(ar_q1_h + jc * T + tb * 512, 512),
                                pbm(ar_q1_l + jc * T + tb * 512, 512),
                                start=(jc == 0), stop=(jc == 1), inc=(jc == 1))
                        split_psum(PB[bk][:, 0:512], pbm(ar_qlT_h + lc * T + tb * 512, 512),
                                   pbm(ar_qlT_l + lc * T + tb * 512, 512), cT.val)
                        bank_consumed(bk, cS, cV)
                tqlat_prev = cT.val
                vql = cV.val
                P.wait("t", cV, vql)

                for qb in range(NTB):
                    nkts = 4 * qb + 4

                    p_ = qb % 2
                    cbs = (2 + 2 * p_, 3 + 2 * p_)
                    rb_ = 6 + p_

                    def consume_kt(ehi, elo, kt, lo, vv, gv, cbs=cbs, rb_=rb_, nkts=nkts):
                        P.wait("t", cV, vv)
                        P.wait("t", cG, gv)
                        if kt == 0:
                            bank_acquire(rb_); bank_acquire(cbs[0]); bank_acquire(cbs[1])
                        P.emit("t", lambda E, ehi=ehi, kt=kt, rb_=rb_, lo=lo: E.matmul(
                            PB[rb_][0:1, lo:512], c_onesb[:, 0:1], ehi[:, lo:512],
                            start=(kt == 0), stop=False))
                        P.emit("t", lambda E, elo=elo, kt=kt, rb_=rb_, lo=lo,
                               e=(kt == nkts - 1): E.matmul(
                            PB[rb_][0:1, lo:512], c_onesb[:, 0:1], elo[:, lo:512],
                            start=False, stop=e))
                        for lc in range(2):
                            mm3(PB[cbs[lc]][:, lo:512],
                                pbm(ar_ckv_h + kt * L + lc * 128, 128),
                                pbm(ar_ckv_l + kt * L + lc * 128, 128),
                                ehi[:, lo:512], elo[:, lo:512],
                                start=(kt == 0), stop=(kt == nkts - 1),
                                inc=(lc == 1))

                    pends = []
                    for kt in range(nkts):
                        sbk = kt % 2
                        pbuf = kt % 3
                        dj = kt - 4 * qb
                        lo = max(dj, 0) * 128
                        w = 512 - lo
                        bank_acquire(sbk)
                        for lc in range(2):
                            mm3(PB[sbk][:, lo:512],
                                pbm(ar_ckvT_h + lc * T + kt * 128, 128),
                                pbm(ar_ckvT_l + lc * T + kt * 128, 128),
                                pbm(ar_qlT_h + lc * T + qb * 512 + lo, w),
                                pbm(ar_qlT_l + lc * T + qb * 512 + lo, w),
                                start=(lc == 0), stop=(lc == 1), inc=(lc == 1))
                        tsc = cT.val
                        ehi = pbm(pth_a(pbuf), 512)
                        elo = pbm(ptl_a(pbuf), 512)
                        ef = xnf[:, (kt % 2) * 512:((kt % 2) + 1) * 512]
                        P.wait("s", cT, tsc)
                        if len(pends) == 2:
                            P.wait("s", cV, pends[0][4])
                        sv2 = P.emit("s", lambda E, sbk=sbk, ef=ef, lo=lo: E.activation(
                            ef[:, lo:512], PB[sbk][:, lo:512], AF.Exp, scale=0.125), inc=cS)
                        bank_consumed(sbk, cS)
                        sv3 = P.emit("s", lambda E, ef=ef, ehi=ehi, lo=lo: E.activation(
                            ehi[:, lo:512], ef[:, lo:512], AF.Copy), inc=cS)
                        P.wait("v", cS, sv3)
                        vv = P.emit("v", lambda E, ef=ef, ehi=ehi, elo=elo, lo=lo: E.tensor_tensor(
                            elo[:, lo:512], ef[:, lo:512], ehi[:, lo:512], OP.subtract), inc=cV)
                        if dj >= 0:
                            P.emit("v", lambda E, ehi=ehi, lo=lo: E.tensor_tensor(
                                ehi[:, lo:lo + 128], ehi[:, lo:lo + 128], c_tri[:], OP.mult))
                            vv = P.emit("v", lambda E, elo=elo, lo=lo: E.tensor_tensor(
                                elo[:, lo:lo + 128], elo[:, lo:lo + 128], c_tri[:], OP.mult), inc=cV)
                        pends.append((ehi, elo, kt, lo, vv, 0))
                        if len(pends) == 3:
                            consume_kt(*pends.pop(0))
                    for p_ in pends:
                        consume_kt(*p_)
                    if pend_tail_a[0] is not None:
                        emit_tail_a(*pend_tail_a[0])
                    pend_tail_a[0] = (qb, h, half, hh)
                emit_tail_a(*pend_tail_a[0])
                pend_tail_a[0] = None
        return

    def ffn(b, xsrc, xdst, nxt=None):
        fa_h, fa_hl = 0, 16384
        fa_w1 = 32768
        fa_w2 = 40960
        fa_mask = 49152
        fa_sel = 49408
        fa_z = 49664
        fa_selT = 49920
        fa_selrow = 51968
        fa_repm16 = 52992
        assert fa_repm16 + 16 * 512 <= PBCOLS
        pfa_lg = 0
        pfa_zf = 768
        assert pfa_zf + 4096 <= PFCOLS

        def ga(a, n):
            return xadd[:, a:a + n]
        ga_lgt, ga_cum, ga_off, ga_K, ga_g1 = 0, 16, 272, 288, 304

        def w1sl(sl, hl, kc):
            base = fa_w1 + sl * 4096 + hl * 2048
            return pbm(base + kc * FP, FP)

        def w2sl(sl, hl, fc):
            base = fa_w2 + sl * 4096 + hl * 2048
            return pbm(base + fc * C, C)

        P.emit("v", lambda E: E.memset(ga(ga_off, 16)[0:1, :], 0.0))

        def gate_sink(blkq, vsplit):
            t0b = blkq * 512
            tlg = cT.val
            P.wait("s", cT, tlg)
            lg = pool_f[0:16, pfa_lg:pfa_lg + 512]
            sv = P.emit("s", lambda E, lg=lg: E.activation(lg, PB[4][0:16, 0:512], AF.Identity,
                                                           bias=bg_col[:]), inc=cS)
            bank_consumed(4, cS)
            for tb in range(4):
                tile = 4 * blkq + tb
                lgt = ga(ga_lgt, 16)
                bank_acquire(5)
                P.wait("t", cS, sv)
                P.emit("t", lambda E, lg=lg, tb=tb: E.matmul(
                    PB[5][:, 0:16], lg[:, tb * 128:(tb + 1) * 128], c_ident16[:],
                    is_transpose=True, start=True, stop=True), inc=cT)
                ttr = cT.val
                P.wait("v", cT, ttr)
                P.emit("v", lambda E, lgt=lgt: E.tensor_copy(lgt, PB[5][:, 0:16]))
                bank_consumed(5, cV)
                g1 = ga(ga_g1, 16)
                g2 = ga(ga_g1 + 16, 16)
                m1 = ga(ga_g1 + 32, 1)
                P.emit("v", lambda E, lgt=lgt, m1=m1: E.tensor_reduce(
                    m1, lgt, axis=AX.X, op=OP.max))
                P.emit("v", lambda E, lgt=lgt, g1=g1, m1=m1: E.tensor_scalar(
                    g1, lgt, m1, None, op0=OP.is_ge))
                P.emit("v", lambda E, g1=g1: E.tensor_scalar_mul(g1, g1, 1e9))
                P.emit("v", lambda E, lgt=lgt, g1=g1, g2=g2: E.tensor_tensor(
                    g2, lgt, g1, OP.subtract))
                P.emit("v", lambda E, g2=g2, m1=m1: E.tensor_reduce(
                    m1, g2, axis=AX.X, op=OP.max))
                mk = pbm(fa_mask + tile * 16, 16)
                zb = pbm(fa_z, 16)
                P.emit("v", lambda E, lgt=lgt, m1=m1, mk=mk: E.tensor_scalar(
                    mk, lgt, m1, None, op0=OP.is_ge))
                vz = P.emit("v", lambda E, mk=mk, zb=zb: E.tensor_scalar(
                    zb, mk, -1.0, 1.0, op0=OP.mult, op1=OP.add))
                bank_acquire(5)
                bank_acquire(4)
                P.wait("t", cV, vz)
                P.emit("t", lambda E, zb=zb: E.matmul(PB[5][:, 0:16], c_tri[:], zb,
                                                      start=True, stop=False))
                P.emit("t", lambda E: E.matmul(PB[5][:, 0:16], c_onesf[0:1, :],
                                               ga(ga_off, 16)[0:1, :], start=False, stop=True))
                P.emit("t", lambda E, zb=zb: E.matmul(PB[4][0:1, 0:16], c_onesb[:, 0:1], zb,
                                                      start=True, stop=True), inc=cT)
                tcs = cT.val
                P.wait("v", cT, tcs)
                P.emit("v", lambda E, tile=tile: E.tensor_copy(ga(ga_cum + tile * 16, 16),
                                                               PB[5][:, 0:16]))
                vo = P.emit("v", lambda E: E.tensor_tensor(ga(ga_off, 16)[0:1, :],
                                                           ga(ga_off, 16)[0:1, :],
                                                           PB[4][0:1, 0:16], OP.add))
                bank_consumed(5, cV)
                bank_consumed(4, cV)
                P.wait("t", cV, vo)
            return cT.val

        P.wait("g", cV)
        P.wait("g", cT)
        tw1 = GD.dma(wstage.ap().rearrange("p (c f) -> p c f", c=NCH, f=FP),
                     fw1[0].rearrange("c p f -> p c f"))
        tw2 = GD.dma(r3(pbm(fa_w1, 2048), NCH, FP),
                     fw1[0].rearrange("c p f -> p c f"))
        P.waitt("v", tw1)
        P.waitt("v", tw2)
        w1pre = P.emit("v", lambda E: E.tensor_tensor(
            pbm(fa_w1 + 2048, 2048), wstage[:, 0:2048],
            pbm(fa_w1, 2048), OP.subtract), inc=cV)

        ln_pass(xsrc, gate_sink, write_xn=(xnh_d[b], xnl_d[b]), gate=True)

        Kr = ga(ga_K, 16)
        vK = P.emit("v", lambda E: E.tensor_scalar_add(ga(ga_g1, 16)[0:1, 0:16],
                                                       ga(ga_off, 16)[0:1, :], -1024.0))
        bank_acquire(5)
        P.wait("t", cV, vK)
        P.emit("t", lambda E: E.matmul(PB[5][:, 0:16], c_onesf[0:1, :],
                                       ga(ga_g1, 16)[0:1, 0:16], start=True, stop=True), inc=cT)
        tKr = cT.val
        P.wait("v", cT, tKr)
        P.emit("v", lambda E, Kr=Kr: E.tensor_copy(Kr, PB[5][:, 0:16]))
        bank_consumed(5, cV)
        for tile in range(NKT):
            mk = pbm(fa_mask + tile * 16, 16)
            sl = pbm(fa_sel + tile * 16, 16)
            zb = pbm(fa_z, 16)
            u = ga(ga_g1, 16)
            P.emit("v", lambda E, mk=mk, zb=zb: E.tensor_scalar(
                zb, mk, -1.0, 1.0, op0=OP.mult, op1=OP.add))
            P.emit("v", lambda E, u=u, tile=tile, Kr=Kr: E.tensor_tensor(
                u, ga(ga_cum + tile * 16, 16), Kr, OP.is_le))
            P.emit("v", lambda E, u=u, zb=zb: E.tensor_tensor(u, u, zb, OP.mult))
            vs = P.emit("v", lambda E, sl=sl, u=u, mk=mk: E.tensor_tensor(sl, u, mk, OP.add))
            bank_acquire(5)
            P.wait("t", cV, vs)
            bkb = PB[5].ap().bitcast(BF16)
            P.emit("t", lambda E, sl=sl, bkb=bkb: E.matmul(bkb[0:16, 0:128], sl, c_identb[:],
                                                           is_transpose=True, start=True,
                                                           stop=True), inc=cT)
            tsT = cT.val
            P.wait("v", cT, tsT)
            P.emit("v", lambda E, tile=tile, bkb=bkb: E.tensor_copy(
                pool_b[0:16, fa_selT + tile * 128: fa_selT + (tile + 1) * 128],
                bkb[0:16, 0:128]), inc=cV)
            bank_consumed(5, cV)

        xst = None
        zaff_pend = None
        wstage_last = [cV.val]

        def emit_zaffine(ztb):
            nonlocal xst
            zt0 = ztb * 512
            if xst is not None:
                P.waitt("q", xst)
            txl = QD.dma(xtile.ap().rearrange("p (c t) -> p c t", c=NCH, t=512),
                         xsrc[:, :, zt0:zt0 + 512].rearrange("c p t -> p c t"))
            for dc in range(8):
                zf = pfm(pfa_zf + dc * 512, 512)
                P.emit("v", lambda E, zf=zf: E.tensor_tensor(zf, zf, rrep[:], OP.mult))
                P.emit("v", lambda E, zf=zf: E.tensor_tensor(zf, zf, mrep[:], OP.subtract))
                P.emit("v", lambda E, zf=zf, dc=dc: E.tensor_scalar(
                    zf, zf, gam_col[:, dc:dc + 1], bet_col[:, dc:dc + 1],
                    op0=OP.mult, op1=OP.add))
                P.waitt("v", txl)
                P.emit("v", lambda E, zf=zf, dc=dc: E.tensor_tensor(
                    xtile[:, dc * 512:(dc + 1) * 512], xtile[:, dc * 512:(dc + 1) * 512],
                    zf, OP.add), inc=cV)
            vfin = cV.val
            P.wait("q", cV, vfin)
            xst = QD.dma(xdst[:, :, zt0:zt0 + 512].rearrange("c p t -> p c t"),
                         xtile.ap().rearrange("p (c t) -> p c t", c=NCH, t=512))
        P.wait("q", cT)
        P.wait("q", cV)
        P.wait("q", cS)
        for c in QCT:
            P.wait("q", c)
        for tb in range(NTB):
            t0b = tb * 512
            th = QD.dma(xn_hi.ap().rearrange("p (c t) -> p c t", c=NCH, t=512),
                        xnh_d[b][:, :, t0b:t0b + 512].rearrange("c p t -> p c t"))
            tl = QD.dma(xn_lo.ap().rearrange("p (c t) -> p c t", c=NCH, t=512),
                        xnl_d[b][:, :, t0b:t0b + 512].rearrange("c p t -> p c t"))
            wtoks = {}

            def load_w1(e, wprev=None):
                sl = e % 2
                if e >= 1:
                    P.wait("g", cV, wtoks[e - 1])
                elif wprev is not None:
                    P.wait("g", cV, wprev)
                t1 = GD.dma(wstage.ap().rearrange("p (c f) -> p c f", c=NCH, f=FP),
                            fw1[e].rearrange("c p f -> p c f"))
                t2 = GD.dma(r3(pbm(fa_w1 + sl * 4096, 2048), NCH, FP),
                            fw1[e].rearrange("c p f -> p c f"))
                P.waitt("v", t1)
                P.waitt("v", t2)
                vv = P.emit("v", lambda E, sl=sl: E.tensor_tensor(
                    pbm(fa_w1 + sl * 4096 + 2048, 2048), wstage[:, 0:2048],
                    pbm(fa_w1 + sl * 4096, 2048), OP.subtract), inc=cV)
                wtoks[e] = vv

            pendr = None
            for e in range(E):
                rb = 2 + (e % 2)
                bank_acquire(rb)
                P.emit("t", lambda E, e=e, rb=rb, tb=tb: E.matmul(
                    PB[rb][:, 0:512], c_onehot[:, e * 128:(e + 1) * 128],
                    pool_b[0:16, fa_selT + tb * 512: fa_selT + (tb + 1) * 512],
                    start=True, stop=True), inc=cT)
                if pendr is not None:
                    pe, ptv, prb = pendr
                    P.wait("v", cT, ptv)
                    P.emit("v", lambda E, pe=pe, prb=prb: E.tensor_copy(
                        pbm(fa_repm16 + pe * 512, 512), PB[prb][:, 0:512]), inc=cV)
                    bank_consumed(prb, cV)
                pendr = (e, cT.val, rb)
            pe, ptv, prb = pendr
            P.wait("v", cT, ptv)
            P.emit("v", lambda E, pe=pe, prb=prb: E.tensor_copy(
                pbm(fa_repm16 + pe * 512, 512), PB[prb][:, 0:512]), inc=cV)
            bank_consumed(prb, cV)

            tproj = {}
            xnf_user = {0: 0, 1: 0}
            if tb == 0:
                wtoks[0] = w1pre
            else:
                load_w1(0, wprev=wstage_last[0])
            for e in range(E):
                if e + 1 < E:
                    if e >= 1:
                        P.wait("g", cT, tproj[e - 1])
                    load_w1(e + 1)
                sl = e % 2
                P.wait("t", cV, wtoks[e])
                P.waitt("t", th)
                P.waitt("t", tl)
                for fc in range(2):
                    bk = (e * 2 + fc) % 4
                    bank_acquire(bk)
                    for kc in range(NCH):
                        mm3(PB[bk][:, 0:512],
                            w1sl(sl, 0, kc)[:, fc * 128:(fc + 1) * 128],
                            w1sl(sl, 1, kc)[:, fc * 128:(fc + 1) * 128],
                            xn_hi[:, kc * 512:(kc + 1) * 512], xn_lo[:, kc * 512:(kc + 1) * 512],
                            start=(kc == 0), stop=(kc == NCH - 1), inc=(kc == NCH - 1))
                    tproj[e] = cT.val
                    hh = pbm(fa_h + (e * 2 + fc) * 512, 512)
                    hl = pbm(fa_hl + (e * 2 + fc) * 512, 512)
                    a = fc
                    P.wait("s", cT, tproj[e])
                    P.wait("s", cV, xnf_user[a])
                    sv2 = P.emit("s", lambda E, a=a, bk=bk, e=e, fc=fc: E.activation(
                        xnf[:, a * 512:(a + 1) * 512], PB[bk][:, 0:512], AF.Gelu,
                        bias=b1_col[:, e * 2 + fc:e * 2 + fc + 1]), inc=cS)
                    P.wait("v", cS, sv2)
                    P.emit("v", lambda E, hh=hh, a=a: E.tensor_copy(
                        hh, xnf[:, a * 512:(a + 1) * 512]))
                    xnf_user[a] = P.emit("v", lambda E, hl=hl, a=a, hh=hh: E.tensor_tensor(
                        hl, xnf[:, a * 512:(a + 1) * 512], hh, OP.subtract), inc=cV)
                    bank_consumed(bk, cS, cV)
                for fc in range(2):
                    hh = pbm(fa_h + (e * 2 + fc) * 512, 512)
                    hl = pbm(fa_hl + (e * 2 + fc) * 512, 512)
                    P.emit("v", lambda E, hh=hh, e=e: E.tensor_tensor(
                        hh, hh, pbm(fa_repm16 + e * 512, 512), OP.mult))
                    P.emit("v", lambda E, hl=hl, e=e: E.tensor_tensor(
                        hl, hl, pbm(fa_repm16 + e * 512, 512), OP.mult), inc=cV)
            vhall = cV.val

            if zaff_pend is not None:
                emit_zaffine(zaff_pend)
                zaff_pend = None

            w2toks = {}

            def load_w2(e, wprev=None):
                sl = e % 2
                if e >= 1:
                    P.wait("g", cV, w2toks[e - 1])
                elif wprev is not None:
                    P.wait("g", cV, wprev)
                t1 = GD.dma(wstage.ap().rearrange("p (c f) -> p c f", c=2, f=C),
                            fw2[e].rearrange("c p f -> p c f"))
                t2 = GD.dma(r3(pbm(fa_w2 + sl * 4096, 2048), 2, C),
                            fw2[e].rearrange("c p f -> p c f"))
                P.waitt("v", t1)
                P.waitt("v", t2)
                vv = P.emit("v", lambda E, sl=sl: E.tensor_tensor(
                    pbm(fa_w2 + sl * 4096 + 2048, 2048), wstage[:, 0:2048],
                    pbm(fa_w2 + sl * 4096, 2048), OP.subtract), inc=cV)
                w2toks[e] = vv

            tw2 = {}
            load_w2(0, wprev=wtoks[E - 1])
            load_w2(1)
            wstage_last[0] = w2toks[1]
            P.wait("t", cV, vhall)
            for dc in range(8):
                bank_acquire(dc)
            for e in range(E):
                if e + 1 < E and e >= 1:
                    P.wait("g", cT, tw2[e - 1])
                    load_w2(e + 1)
                    wstage_last[0] = w2toks[e + 1]
                sl = e % 2
                P.wait("t", cV, w2toks[e])
                for dc in range(8):
                    if e == 0:
                        P.emit("t", lambda E, dc=dc, tb=tb: E.matmul(
                            PB[dc][:, 0:512], b2_hi[:, dc * 128:(dc + 1) * 128],
                            pool_b[0:16, fa_selT + tb * 512: fa_selT + (tb + 1) * 512],
                            start=True, stop=False))
                        P.emit("t", lambda E, dc=dc, tb=tb: E.matmul(
                            PB[dc][:, 0:512], b2_lo[:, dc * 128:(dc + 1) * 128],
                            pool_b[0:16, fa_selT + tb * 512: fa_selT + (tb + 1) * 512],
                            start=False, stop=False))
                    for fc in range(2):
                        hh = pbm(fa_h + (e * 2 + fc) * 512, 512)
                        hl = pbm(fa_hl + (e * 2 + fc) * 512, 512)
                        mm3(PB[dc][:, 0:512],
                            w2sl(sl, 0, fc)[:, dc * 128:(dc + 1) * 128],
                            w2sl(sl, 1, fc)[:, dc * 128:(dc + 1) * 128],
                            hh, hl,
                            start=False, stop=(e == E - 1 and fc == 1),
                            inc=(fc == 1))
                tw2[e] = cT.val
            tout = cT.val

            for dc in (2, 3, 0, 1, 4, 5, 6, 7):
                zf = pfm(pfa_zf + dc * 512, 512)
                P.waitt("v", th)
                P.waitt("v", tl)
                P.emit("v", lambda E, zf=zf, dc=dc: E.tensor_tensor(
                    zf, xn_hi[:, dc * 512:(dc + 1) * 512], xn_lo[:, dc * 512:(dc + 1) * 512],
                    OP.add))
                P.wait("v", cT, tout)
                P.emit("v", lambda E, zf=zf, dc=dc: E.tensor_tensor(
                    zf, zf, PB[dc][:, 0:512], OP.add), inc=cV)
                bank_consumed(dc, cV)
            vz = cV.val
            ln_stats(lambda ch: pfm(pfa_zf + ch * 512, 512),
                     twait=lambda e: P.wait(e, cV, vz), blk=tb)
            zaff_pend = tb
            P.wait("q", cT, tout)
            P.wait("q", cV, vz)
        if zaff_pend is not None:
            emit_zaffine(zaff_pend)
        if nxt is not None:
            if xst is not None:
                P.waitt("q", xst)
            lnpre[0] = QD.dma(xtile.ap().rearrange("p (c t) -> p c t", c=NCH, t=512),
                              nxt[:, :, 0:512].rearrange("c p t -> p c t"))
        return

    def attn_b(b, xsrc, xdst, nxt=None):
        ba_y, ba_yl = 0, 16384
        ba_w = 32768
        ba_q, ba_ql = 38912, 40960
        ba_k, ba_kl = 43008, 45056
        ba_vT, ba_vTl = 47104, 47616
        ba_vt, ba_vtl = 48128, 50176
        ba_pt, ba_ptl = 52224, 53248

        def pth(pb_):
            return ba_pt + pb_ * 512 if pb_ < 2 else 60416

        def ptl(pb_):
            return ba_ptl + pb_ * 512 if pb_ < 2 else 60928
        pfb_rs = 0
        pfb_ef = 768
        pfb_yf = 2816
        pfb_rr = 3328
        pfb_ef3 = 3840

        def noop_sink(blkq, vsplit):
            return 0

        def wsl(base, hl, kc):
            return pbm(base + hl * 1024 + kc * 128, 128)

        def load_wqkv(pj, wprev):
            slot = ba_w if pj % 2 == 0 else 54272
            toks = []
            for wi, wd in enumerate((bwq, bwk, bwv)):
                base = slot + wi * 2048
                if wprev is not None:
                    P.wait("g", cV, wprev)
                t1 = GD.dma(wstage[:, 0:1024].rearrange("p (k f) -> p k f", k=NCH, f=128),
                            wd[pj].rearrange("(k p) f -> p k f", p=128))
                t2 = GD.dma(r3(pbm(base, 1024), NCH, 128),
                            wd[pj].rearrange("(k p) f -> p k f", p=128))
                P.waitt("v", t1)
                P.waitt("v", t2)
                wprev = P.emit("v", lambda E, base=base: E.tensor_tensor(
                    pbm(base + 1024, 1024), wstage[:, 0:1024], pbm(base, 1024), OP.subtract),
                    inc=cV)
                toks.append(wprev)
            return toks

        P.wait("g", cV)
        P.wait("g", cT)
        wvtoks = load_wqkv(0, None)

        ln_pass(xsrc, noop_sink, write_xn=(xnh_d[b], xnl_d[b]))
        ln_prefetch(nxt)
        for c in QCT:
            P.wait("q", c)
        tproj_prev = 0
        for pj in range(8):
            slot = ba_w if pj % 2 == 0 else 54272
            for tb in range(NTB):
                t0b = tb * 512
                P.wait("q", cT, tproj_prev)
                th = QD.dma(xn_hi.ap().rearrange("p (c t) -> p c t", c=NCH, t=512),
                            xnh_d[b][:, :, t0b:t0b + 512].rearrange("c p t -> p c t"))
                tl = QD.dma(xn_lo.ap().rearrange("p (c t) -> p c t", c=NCH, t=512),
                            xnl_d[b][:, :, t0b:t0b + 512].rearrange("c p t -> p c t"))
                P.waitt("t", th)
                P.waitt("t", tl)
                P.wait("t", cV, wvtoks[2])
                svv = None
                for wi, (dsth, dstl) in enumerate(((ba_q, ba_ql), (ba_k, ba_kl), (ba_vT, ba_vTl))):
                    bk = wi
                    base = slot + wi * 2048
                    bank_acquire(bk)
                    for kc in range(NCH):
                        mm3(PB[bk][:, 0:512],
                            wsl(base, 0, kc), wsl(base, 1, kc),
                            xn_hi[:, kc * 512:(kc + 1) * 512], xn_lo[:, kc * 512:(kc + 1) * 512],
                            start=(kc == 0), stop=(kc == NCH - 1), inc=(kc == NCH - 1))
                    if wi < 2:
                        svv = split_psum(PB[bk][:, 0:512], pbm(dsth + t0b, 512),
                                         pbm(dstl + t0b, 512), cT.val)
                    else:
                        svv = split_psum(PB[bk][:, 0:512], pbm(ba_vT, 512), pbm(ba_vTl, 512),
                                         cT.val)
                    bank_consumed(bk, cS, cV)
                tproj_prev = cT.val
                P.wait("t", cS, svv[0])
                P.wait("t", cV, svv[1])
                for tt in range(4):
                    kt = 4 * tb + tt
                    bk = 3 + (tt % 2)
                    bank_acquire(bk)
                    bkb = PB[bk].ap().bitcast(BF16)
                    P.emit("t", lambda E, tt=tt, bkb=bkb: E.matmul(
                        bkb[:, 0:128], pbm(ba_vT + tt * 128, 128), c_identb[:],
                        is_transpose=True, start=True, stop=False))
                    P.emit("t", lambda E, tt=tt, bkb=bkb: E.matmul(
                        bkb[:, 128:256], pbm(ba_vTl + tt * 128, 128), c_identb[:],
                        is_transpose=True, start=False, stop=True), inc=cT)
                    tv = cT.val
                    P.wait("v", cT, tv)
                    P.emit("v", lambda E, kt=kt, bkb=bkb: E.tensor_copy(
                        pbm(ba_vt + kt * 128, 128), bkb[:, 0:128]))
                    vv = P.emit("v", lambda E, kt=kt, bkb=bkb: E.tensor_copy(
                        pbm(ba_vtl + kt * 128, 128), bkb[:, 128:256]), inc=cV)
                    bank_consumed(bk, cV)
            vproj = cV.val
            if pj + 1 < 8:
                wvtoks = load_wqkv(pj + 1, wvtoks[2])
            for hh in range(2):
                half = hh * 64
                tp = (half, 0)
                P.wait("t", cV, vproj)
                gk = 0

                def win(qb, kt):
                    rel = kt - 4 * qb
                    jlo = max(rel, 0)
                    jhi = min(rel + 1, 3)
                    return jlo * 128, (jhi + 1) * 128, rel

                def consume_bkt(ehi, elo, qb, kt, first, last, vv):
                    rb = 2 + (qb % 2)
                    yb = 6 + (qb % 2)
                    lo, hi, rel = win(qb, kt)
                    P.wait("t", cV, vv)
                    if first:
                        bank_acquire(rb)
                        bank_acquire(yb)
                        P.emit("t", lambda E, rb=rb, ehi=ehi: E.matmul(
                            PB[rb][0:1, 0:512], c_zerob[:, 0:1], ehi, start=True, stop=False))
                        P.emit("t", lambda E, yb=yb, ehi=ehi, half=half: E.matmul(
                            PB[yb][half:half + 64, 0:512], c_zerob[:, 0:64], ehi,
                            start=True, stop=False, tile_position=(0, half)))
                    P.emit("t", lambda E, ehi=ehi, rb=rb, lo=lo, hi=hi: E.matmul(
                        PB[rb][0:1, lo:hi], c_onesb[:, 0:1], ehi[:, lo:hi],
                        start=False, stop=False))
                    P.emit("t", lambda E, elo=elo, rb=rb, lo=lo, hi=hi, last=last: E.matmul(
                        PB[rb][0:1, lo:hi], c_onesb[:, 0:1], elo[:, lo:hi],
                        start=False, stop=last))
                    mm3(PB[yb][half:half + 64, lo:hi],
                        pbm(ba_vt + kt * 128, 128)[:, half:half + 64],
                        pbm(ba_vtl + kt * 128, 128)[:, half:half + 64],
                        ehi[:, lo:hi], elo[:, lo:hi],
                        start=False, stop=last, tp=(0, half), inc=last)

                def emit_tail(qb, sv):
                    yb = 6 + (qb % 2)
                    rs = pool_f[0:1, pfb_rs:pfb_rs + 512]
                    bank_acquire(4)
                    P.wait("t", cS, sv)
                    P.emit("t", lambda E, rs=rs, half=half: E.matmul(
                        PB[4][half:half + 64, 0:512], c_onesf[0:1, 0:64], rs,
                        start=True, stop=True, tile_position=(0, half)), inc=cT)
                    trr = cT.val
                    yh = pbm(ba_y + pj * 2048 + qb * 512, 512)[half:half + 64, :]
                    yl = pbm(ba_yl + pj * 2048 + qb * 512, 512)[half:half + 64, :]
                    yf = pool_f[half:half + 64, pfb_yf:pfb_yf + 512]
                    rrb = pool_f[half:half + 64, pfb_rr:pfb_rr + 512]
                    P.wait("s", cT, trr)
                    sv = P.emit("s", lambda E, rrb=rrb, half=half: E.activation(
                        rrb, PB[4][half:half + 64, 0:512], AF.Copy), inc=cS)
                    bank_consumed(4, cS)
                    P.wait("v", cS, sv)
                    vv = P.emit("v", lambda E, yf=yf, rrb=rrb, yb=yb, half=half: E.tensor_tensor(
                        yf, PB[yb][half:half + 64, 0:512], rrb, OP.mult), inc=cV)
                    bank_consumed(yb, cV)
                    P.wait("s", cV, vv)
                    sv = P.emit("s", lambda E, yh=yh, yf=yf: E.activation(yh, yf, AF.Copy), inc=cS)
                    P.wait("v", cS, sv)
                    P.emit("v", lambda E, yl=yl, yf=yf, yh=yh: E.tensor_tensor(
                        yl, yf, yh, OP.subtract), inc=cV)

                pends = []
                pend_tail = None
                efvv = {0: 0, 1: 0, 2: 0}

                def pop_one():
                    nonlocal pend_tail
                    ent = pends.pop(0)
                    consume_bkt(*ent)
                    if ent[5]:
                        if pend_tail is not None:
                            emit_tail(*pend_tail)
                        qb_ = ent[2]
                        rb = 2 + (qb_ % 2)
                        ty = cT.val
                        P.wait("s", cT, ty)
                        rs = pool_f[0:1, pfb_rs:pfb_rs + 512]
                        P.emit("s", lambda E, rs=rs, rb=rb: E.activation(
                            rs, PB[rb][0:1, 0:512], AF.Ln))
                        sv = P.emit("s", lambda E, rs=rs: E.activation(
                            rs, rs, AF.Exp, scale=-1.0), inc=cS)
                        bank_consumed(rb, cS)
                        pend_tail = (qb_, sv)
                for qb in range(NTB):
                    kts = [k for k in range(4 * qb - 1, 4 * qb + 4) if k >= 0]
                    for ki, kt in enumerate(kts):
                        lo, hi, rel = win(qb, kt)
                        w = hi - lo
                        sbk = gk % 2
                        pb3 = gk % 3
                        gk += 1
                        bank_acquire(sbk)
                        mm3(PB[sbk][:, lo:hi],
                            pbm(ba_k + kt * 128, 128)[half:half + 64, :],
                            pbm(ba_kl + kt * 128, 128)[half:half + 64, :],
                            pbm(ba_q + qb * 512 + lo, w)[half:half + 64, :],
                            pbm(ba_ql + qb * 512 + lo, w)[half:half + 64, :],
                            start=True, stop=True, tp=tp)
                        tsc = cT.val
                        ehi = pbm(pth(pb3), 512)
                        elo = pbm(ptl(pb3), 512)
                        ef = pfm(pfb_ef + pb3 * 1024 if pb3 < 2 else pfb_ef3, 512)
                        P.wait("s", cT, tsc)
                        P.wait("s", cV, efvv[pb3])
                        sv2 = P.emit("s", lambda E, ef=ef, sbk=sbk, lo=lo, hi=hi: E.activation(
                            ef[:, lo:hi], PB[sbk][:, lo:hi], AF.Exp, scale=0.125), inc=cS)
                        bank_consumed(sbk, cS)
                        sv3 = P.emit("s", lambda E, ef=ef, ehi=ehi, lo=lo, hi=hi: E.activation(
                            ehi[:, lo:hi], ef[:, lo:hi], AF.Copy), inc=cS)
                        P.wait("v", cS, sv3)
                        vv = P.emit("v", lambda E, ef=ef, ehi=ehi, elo=elo, lo=lo, hi=hi:
                                    E.tensor_tensor(elo[:, lo:hi], ef[:, lo:hi], ehi[:, lo:hi],
                                                    OP.subtract), inc=cV)
                        efvv[pb3] = vv
                        for jj in (rel, rel + 1):
                            if 0 <= jj <= 3:
                                msk = c_tri if jj == rel else c_tril
                                P.emit("v", lambda E, ehi=ehi, jj=jj, msk=msk: E.tensor_tensor(
                                    ehi[:, jj * 128:(jj + 1) * 128],
                                    ehi[:, jj * 128:(jj + 1) * 128], msk[:], OP.mult))
                                vv = P.emit("v", lambda E, elo=elo, jj=jj, msk=msk: E.tensor_tensor(
                                    elo[:, jj * 128:(jj + 1) * 128],
                                    elo[:, jj * 128:(jj + 1) * 128], msk[:], OP.mult), inc=cV)
                        pends.append((ehi, elo, qb, kt, ki == 0, ki == len(kts) - 1, vv))
                        if len(pends) == 3:
                            pop_one()
                while pends:
                    pop_one()
                emit_tail(*pend_tail)
        P.wait("g", cV, wvtoks[2])
        P.wait("g", cT, tproj_prev)
        wotoks = []
        for pj in range(8):
            base = ba_w + pj * 2048
            if pj >= 1:
                P.wait("g", cV, wotoks[pj - 1])
            t1 = GD.dma(wstage[:, 0:1024], bwot[pj][:, :])
            t2 = GD.dma(pbm(base, 1024), bwot[pj][:, :])
            P.waitt("v", t1)
            P.waitt("v", t2)
            vv = P.emit("v", lambda E, base=base: E.tensor_tensor(
                pbm(base + 1024, 1024), wstage[:, 0:1024], pbm(base, 1024), OP.subtract), inc=cV)
            wotoks.append(vv)
        vwo = cV.val
        P.wait("t", cV, vwo)
        xst = None
        for fc in range(NCH):
            for tb in range(NTB):
                t0b = tb * 512
                bk = (fc * NTB + tb) % 2
                bank_acquire(bk)
                for pj in range(8):
                    mm3(PB[bk][:, 0:512],
                        pbm(ba_w + pj * 2048 + fc * 128, 128),
                        pbm(ba_w + pj * 2048 + 1024 + fc * 128, 128),
                        pbm(ba_y + pj * 2048 + t0b, 512), pbm(ba_yl + pj * 2048 + t0b, 512),
                        start=(pj == 0), stop=(pj == 7), inc=(pj == 7))
                to = cT.val
                a = (fc * NTB + tb) % 2
                if xst is not None:
                    P.waitt("q", xst)
                txl = QD.dma(xadd[:, a * 512:(a + 1) * 512], xsrc[fc, :, t0b:t0b + 512])
                P.waitt("v", txl)
                P.wait("v", cT, to)
                vv = P.emit("v", lambda E, a=a, bk=bk: E.tensor_tensor(
                    xadd[:, a * 512:(a + 1) * 512], xadd[:, a * 512:(a + 1) * 512],
                    PB[bk][:, 0:512], OP.add), inc=cV)
                bank_consumed(bk, cV)
                P.wait("q", cV, vv)
                xst = QD.dma(xdst[fc, :, t0b:t0b + 512], xadd[:, a * 512:(a + 1) * 512])
        return

    srcs = [xin, xres, xres, xres]
    dsts = [xres, xres, xres, xres]
    dsts[N_SUBLAYERS - 1] = out_fm

    seq = [(attn_a, b, srcs[0][b], dsts[0][b]) for b in range(B)]
    if N_SUBLAYERS > 1:
        seq += [(ffn, b, srcs[1][b], dsts[1][b]) for b in range(B)]
    if N_SUBLAYERS > 2:
        seq += [(attn_b, b, srcs[2][b], dsts[2][b]) for b in range(B)]
    if N_SUBLAYERS > 3:
        seq += [(ffn, b, srcs[3][b], dsts[3][b]) for b in range(B)]
    for i, (fn, b, s_, d_) in enumerate(seq):
        fn(b, s_, d_, nxt=(seq[i + 1][2] if i + 1 < len(seq) else None))
        stage_barrier()

    for c in QCT + WCT:
        P.wait("q", c)
        P.wait("g", c)

    with nc.Block() as blk:
        blk.sync(lambda E: [f(E) for f in P.q["q"]])
        blk.tensor(lambda E: [f(E) for f in P.q["t"]])
        blk.vector(lambda E: [f(E) for f in P.q["v"]])
        blk.scalar(lambda E: [f(E) for f in P.q["s"]])
        blk.gpsimd(lambda E: [f(E) for f in P.q["g"]])

    ctx.close()
    return nc


def host_inputs(inputs, core):
    x = inputs["L0_attn_a_x"]
    m = {}
    xb = np.ascontiguousarray(x[2 * core:2 * core + 2].transpose(0, 2, 1))
    m["xin"] = xb.reshape(B, NCH, 128, T)
    m["awdqt"] = np.ascontiguousarray(inputs["L0_attn_a_W_dq"].T).reshape(NCH, 128, J)
    m["awdq"] = inputs["L0_attn_a_W_dq"].reshape(2, 128, C)
    m["awuqt"] = np.ascontiguousarray(inputs["L0_attn_a_W_uq"].T).reshape(2, 128, C)
    m["awuq"] = inputs["L0_attn_a_W_uq"].reshape(NCH, 128, J)
    m["awuk"] = inputs["L0_attn_a_W_uk"].reshape(NCH, 128, L)
    m["awdkvt"] = np.ascontiguousarray(inputs["L0_attn_a_W_dkv"].T).reshape(NCH, 128, L)
    m["awuv"] = inputs["L0_attn_a_W_uv"].reshape(NCH, 128, L)
    m["awot"] = np.ascontiguousarray(inputs["L0_attn_a_W_o"].T).reshape(NCH, 128, C)
    m["bwq"] = np.ascontiguousarray(inputs["L0_attn_b_Wq"].T.reshape(C, NCH, 128).transpose(1, 0, 2))
    m["bwk"] = np.ascontiguousarray(inputs["L0_attn_b_Wk"].T.reshape(C, NCH, 128).transpose(1, 0, 2))
    m["bwv"] = np.ascontiguousarray(inputs["L0_attn_b_Wv"].T.reshape(C, NCH, 128).transpose(1, 0, 2))
    m["bwot"] = np.ascontiguousarray(inputs["L0_attn_b_Wo"].T).reshape(NCH, 128, C)
    m["fwg"] = inputs["L0_ffn_wg"].reshape(NCH, 128, E)
    m["fbg"] = inputs["L0_ffn_bg"].reshape(16, 1)
    m["fw1"] = inputs["L0_ffn_w1"].reshape(E, NCH, 128, FP)
    m["fb1"] = inputs["L0_ffn_b1"]
    m["fw2"] = inputs["L0_ffn_w2"].reshape(E, 2, 128, C)
    m["fb2"] = inputs["L0_ffn_b2"]
    m["fgam"] = inputs["L0_ffn_gamma"].reshape(NCH, 128)
    m["fbet"] = inputs["L0_ffn_beta"].reshape(NCH, 128)
    m["ident16"] = np.eye(16, dtype=np.float32)
    import ml_dtypes
    m["identb"] = np.eye(128, dtype=ml_dtypes.bfloat16)
    ii = np.arange(128)
    m["tri_incl"] = (ii[None, :] >= ii[:, None]).astype(ml_dtypes.bfloat16)
    m["tri_slow"] = (ii[None, :] < ii[:, None]).astype(ml_dtypes.bfloat16)
    m["onesf"] = np.ones((128, 128), dtype=np.float32)
    m["onesb"] = np.ones((128, 128), dtype=ml_dtypes.bfloat16)
    oh = np.zeros((16, 2048), dtype=ml_dtypes.bfloat16)
    for e in range(16):
        oh[e, e * 128:(e + 1) * 128] = 1
    m["onehot"] = oh
    m["zerob"] = np.zeros((128, 64), dtype=ml_dtypes.bfloat16)
    return m


_nc_cache = {}


def kernel(**inputs):
    key = N_SUBLAYERS
    if key not in _nc_cache:
        _nc_cache[key] = build_program()
    nc = _nc_cache[key]
    in_maps = [host_inputs(inputs, core) for core in range(8)]
    res = run_bass_kernel_spmd(nc, in_maps, core_ids=list(range(8)))
    outs = []
    for core in range(8):
        o = res.results[core]["out_fm"].reshape(B, C, T)
        outs.append(np.ascontiguousarray(o.transpose(0, 2, 1)))
    return np.concatenate(outs, axis=0)


# version 20
# speedup vs baseline: 1.0155x; 1.0155x over previous
import numpy as np
from contextlib import ExitStack
import concourse.bass as bass
import concourse.mybir as mybir
from concourse.bass_utils import run_bass_kernel_spmd

F32 = mybir.dt.float32
BF16 = mybir.dt.bfloat16
AF = mybir.ActivationFunctionType
OP = mybir.AluOpType
AX = mybir.AxisListType

N_SUBLAYERS = 4

B = 2
T = 2048
C = 1024
NCH = 8
L = 256
J = 256
NH = 16
HS = 64
E = 16
FP = 256
NTB = 4
NKT = 16
EPS_LN = 1e-5


class Ctr:
    def __init__(self, sem, name):
        self.sem = sem
        self.name = name
        self.val = 0


class Prog:
    ENGS = ("t", "v", "s", "g", "q")

    def __init__(self):
        self.q = {e: [] for e in self.ENGS}
        self.waited = {}
        self.own = {}

    def emit(self, eng, f, inc=None, n=1):
        own = self.own.get(eng)
        if own is not None:
            self.wait(eng, own)
            if inc is None:
                inc = own
        if inc is not None:
            inc.val += n
            sem, amount = inc.sem, n

            def g(En):
                f(En).then_inc(sem, amount)
        else:
            def g(En):
                f(En)
        self.q[eng].append(g)
        return inc.val if inc is not None else None

    def wait(self, eng, ctr, v=None):
        if v is None:
            v = ctr.val
        assert v <= ctr.val, f"wait {ctr.name}>={v} but only {ctr.val} emitted"
        if v <= 0:
            return
        k = (eng, ctr.name)
        if self.waited.get(k, 0) >= v:
            return
        self.waited[k] = v
        sem = ctr.sem
        self.q[eng].append(lambda En: En.wait_ge(sem, v))

    def waitt(self, eng, tok):
        ctr, v = tok
        self.wait(eng, ctr, v)


class DmaQ:

    def __init__(self, P, eng, ctrs):
        self.P, self.eng, self.ctrs = P, eng, ctrs
        self.i = 0

    def dma(self, out, in_, **kw):
        c = self.ctrs[self.i % len(self.ctrs)]
        self.i += 1
        self.P.wait(self.eng, c)
        c.val += 16
        sem, v = c.sem, c.val

        def g(En):
            En.dma_start(out=out, in_=in_, **kw).then_inc(sem, 16)
        self.P.q[self.eng].append(g)
        return (c, v)


def build_program():
    nc = bass.Bass("TRN2", target_bir_lowering=False)

    xin = nc.dram_tensor("xin", [B, NCH, 128, T], F32, kind="ExternalInput").ap()
    out_fm = nc.dram_tensor("out_fm", [B, NCH, 128, T], F32, kind="ExternalOutput").ap()

    awdqt = nc.dram_tensor("awdqt", [NCH, 128, J], F32, kind="ExternalInput").ap()
    awdq = nc.dram_tensor("awdq", [2, 128, C], F32, kind="ExternalInput").ap()
    awuqt = nc.dram_tensor("awuqt", [2, 128, C], F32, kind="ExternalInput").ap()
    awuq = nc.dram_tensor("awuq", [NCH, 128, J], F32, kind="ExternalInput").ap()
    awuk = nc.dram_tensor("awuk", [NCH, 128, L], F32, kind="ExternalInput").ap()
    awdkvt = nc.dram_tensor("awdkvt", [NCH, 128, L], F32, kind="ExternalInput").ap()
    awuv = nc.dram_tensor("awuv", [NCH, 128, L], F32, kind="ExternalInput").ap()
    awot = nc.dram_tensor("awot", [NCH, 128, C], F32, kind="ExternalInput").ap()

    bwq = nc.dram_tensor("bwq", [NCH, C, 128], F32, kind="ExternalInput").ap()
    bwk = nc.dram_tensor("bwk", [NCH, C, 128], F32, kind="ExternalInput").ap()
    bwv = nc.dram_tensor("bwv", [NCH, C, 128], F32, kind="ExternalInput").ap()
    bwot = nc.dram_tensor("bwot", [NCH, 128, C], F32, kind="ExternalInput").ap()

    fwg = nc.dram_tensor("fwg", [NCH, 128, E], F32, kind="ExternalInput").ap()
    fbg = nc.dram_tensor("fbg", [16, 1], F32, kind="ExternalInput").ap()
    fw1 = nc.dram_tensor("fw1", [E, NCH, 128, FP], F32, kind="ExternalInput").ap()
    fb1 = nc.dram_tensor("fb1", [E, FP], F32, kind="ExternalInput").ap()
    fw2 = nc.dram_tensor("fw2", [E, 2, 128, C], F32, kind="ExternalInput").ap()
    fb2 = nc.dram_tensor("fb2", [E, C], F32, kind="ExternalInput").ap()
    fgam = nc.dram_tensor("fgam", [NCH, 128], F32, kind="ExternalInput").ap()
    fbet = nc.dram_tensor("fbet", [NCH, 128], F32, kind="ExternalInput").ap()

    ident16 = nc.dram_tensor("ident16", [16, 16], F32, kind="ExternalInput").ap()
    identb = nc.dram_tensor("identb", [128, 128], BF16, kind="ExternalInput").ap()
    tri_incl = nc.dram_tensor("tri_incl", [128, 128], BF16, kind="ExternalInput").ap()
    ntri = nc.dram_tensor("ntri", [128, 128], BF16, kind="ExternalInput").ap()
    ntril = nc.dram_tensor("ntril", [128, 128], BF16, kind="ExternalInput").ap()
    onesf = nc.dram_tensor("onesf", [128, 128], F32, kind="ExternalInput").ap()
    onesb = nc.dram_tensor("onesb", [128, 128], BF16, kind="ExternalInput").ap()
    onehot = nc.dram_tensor("onehot", [16, 2048], BF16, kind="ExternalInput").ap()
    zerob = nc.dram_tensor("zerob", [128, 64], BF16, kind="ExternalInput").ap()

    xres = nc.dram_tensor("xres", [B, NCH, 128, T], F32).ap()
    xnh_d = nc.dram_tensor("xnh_d", [B, NCH, 128, T], BF16).ap()
    xnl_d = nc.dram_tensor("xnl_d", [B, NCH, 128, T], BF16).ap()

    ctx = ExitStack()

    def sbuf(name, shape, dt=F32):
        return ctx.enter_context(nc.sbuf_tensor(name, shape, dt))

    c_ident16 = sbuf("c_ident16", [16, 16])
    c_identb = sbuf("c_identb", [128, 128], BF16)
    c_tri = sbuf("c_tri", [128, 128], BF16)
    c_ntri = sbuf("c_ntri", [128, 128], BF16)
    c_ntril = sbuf("c_ntril", [128, 128], BF16)
    c_onesf = sbuf("c_onesf", [128, 128])
    c_onesb = sbuf("c_onesb", [128, 128], BF16)
    c_onehot = sbuf("c_onehot", [16, 2048], BF16)
    c_zerob = sbuf("c_zerob", [128, 64], BF16)
    gam_col = sbuf("gam_col", [128, NCH])
    bet_col = sbuf("bet_col", [128, NCH])
    bg_col = sbuf("bg_col", [16, 1])
    b1_col = sbuf("b1_col", [128, 2 * E])
    b2_hi = sbuf("b2_hi", [16, C], BF16)
    b2_lo = sbuf("b2_lo", [16, C], BF16)
    wgf = sbuf("wgf", [128, NCH * E])

    wstage = sbuf("wstage", [128, 2 * C])

    xtile = sbuf("xtile", [128, NCH * 512])
    xsq = sbuf("xsq", [128, 512])
    xnf = sbuf("xnf", [128, 2 * 512])
    xn_hi = sbuf("xn_hi", [128, NCH * 512], BF16)
    xn_lo = sbuf("xn_lo", [128, NCH * 512], BF16)
    rrep = sbuf("rrep", [128, 512])
    mrep = sbuf("mrep", [128, 512])
    rowsb_r = sbuf("rowsb_r", [1, 512])
    rowsb_m = sbuf("rowsb_m", [1, 512])
    xadd = sbuf("xadd", [128, 2 * 512])

    PBCOLS = 61440
    PFCOLS = 4864
    pool_b = sbuf("pool_b", [128, PBCOLS], BF16)
    pool_f = sbuf("pool_f", [128, PFCOLS])

    AW = 0
    aw_wdqt_h, aw_wdqt_l = 0, 2048
    aw_wuqt_h, aw_wuqt_l = 4096, 6144
    aw_wdkvt_h, aw_wdkvt_l = 8192, 10240
    aw_keff_h, aw_keff_l = 12288, 14336
    aw_veff_h, aw_veff_l = 16384, 18432
    AW_END = 20480

    PB = [ctx.enter_context(nc.psum_tensor(f"pb{i}", [128, 512], F32)) for i in range(8)]

    cT = Ctr(ctx.enter_context(nc.semaphore("cT")), "cT")
    cV = Ctr(ctx.enter_context(nc.semaphore("cV")), "cV")
    cS = Ctr(ctx.enter_context(nc.semaphore("cS")), "cS")
    cG = Ctr(ctx.enter_context(nc.semaphore("cG")), "cG")
    QCT = [Ctr(ctx.enter_context(nc.semaphore(f"cQ{i}")), f"cQ{i}") for i in range(8)]
    WCT = [Ctr(ctx.enter_context(nc.semaphore(f"cW{i}")), f"cW{i}") for i in range(8)]
    ALLC = [cT, cV, cS, cG] + QCT + WCT

    P = Prog()
    P.own = {"v": cV, "s": cS, "g": cG}
    QD = DmaQ(P, "q", QCT)
    GD = DmaQ(P, "g", WCT)

    def pbm(a, n):
        return pool_b[:, a:a + n]

    def pfm(a, n):
        return pool_f[:, a:a + n]

    wdqt_hi = pbm(aw_wdqt_h, 2048); wdqt_lo = pbm(aw_wdqt_l, 2048)
    wuqt_hi = pbm(aw_wuqt_h, 2048); wuqt_lo = pbm(aw_wuqt_l, 2048)
    wdkvt_hi = pbm(aw_wdkvt_h, 2048); wdkvt_lo = pbm(aw_wdkvt_l, 2048)
    keff_hi = pbm(aw_keff_h, 2048); keff_lo = pbm(aw_keff_l, 2048)
    veff_hi = pbm(aw_veff_h, 2048); veff_lo = pbm(aw_veff_l, 2048)

    bank_deps = [[] for _ in range(8)]

    def bank_acquire(i):
        for (c, v) in bank_deps[i]:
            P.wait("t", c, v)
        bank_deps[i] = []

    def bank_consumed(i, *ctrs):
        bank_deps[i] = [(c, c.val) for c in ctrs]

    def stage_barrier():
        for e in Prog.ENGS:
            for c in ALLC:
                P.wait(e, c)

    grp_open = {}
    import traceback as _tb

    def gtrack(out, start, stop):
        name = out.tensor.name if hasattr(out, "tensor") else "?"
        if start:
            assert not grp_open.get(name), f"group restart on open {name}"
            grp_open[name] = True
        else:
            assert grp_open.get(name), f"accumulate into unopened group {name}"
        if stop:
            grp_open[name] = False

    class TEng:

        def __init__(self, E):
            self.E = E

        def matmul(self, out, lh, rh, **kw):
            return self.E.matmul(out, lh, rh, **kw)

    _orig_emit = P.emit

    def emit_t_checked(eng, f, inc=None, n=1):
        if eng == "t":
            class Rec:
                def __init__(self):
                    self.calls = []

                def matmul(self, out, lh, rh, start=None, stop=None, **kw):
                    self.calls.append((out, bool(start), bool(stop)))
                    class _I:
                        def then_inc(self, *a):
                            return self
                    return _I()

                def __getattr__(self, k):
                    def f2(*a, **kw):
                        class _I:
                            def then_inc(self, *a2):
                                return self
                        return _I()
                    return f2
            r = Rec()
            try:
                f(r)
                for (out, st, sp) in r.calls:
                    if hasattr(out, "tensor") and "pb" in str(getattr(out.tensor, "name", "")):
                        gtrack(out, st, sp)
            except AssertionError:
                raise
            except Exception:
                pass
        return _orig_emit(eng, f, inc=inc, n=n)

    P.emit = emit_t_checked

    def mm3(out, lh, ll, rh, rl, start, stop, tp=None, inc=True):
        kw = {} if tp is None else {"tile_position": tp}
        P.emit("t", lambda E: E.matmul(out, lh, rh, start=start, stop=False, **kw))
        P.emit("t", lambda E: E.matmul(out, lh, rl, start=False, stop=False, **kw))
        return P.emit("t", lambda E: E.matmul(out, ll, rh, start=False, stop=stop, **kw),
                      inc=cT if inc else None)

    def split_psum(psum_ap, hi_ap, lo_ap, tval):
        P.wait("s", cT, tval)
        sv = P.emit("s", lambda E: E.activation(hi_ap, psum_ap, AF.Copy), inc=cS)
        P.wait("v", cT, tval)
        P.wait("v", cS, sv)
        vv = P.emit("v", lambda E: E.tensor_tensor(lo_ap, psum_ap, hi_ap, OP.subtract), inc=cV)
        return sv, vv

    def r3(sb_ap, c, f):
        return sb_ap.rearrange("p (c f) -> p c f", c=c, f=f)

    def load_split(dram_re, hi_ap, lo_ap, stage_ap, c=None, f=None):
        if c is not None:
            t1 = GD.dma(r3(hi_ap, c, f), dram_re)
            t2 = GD.dma(r3(stage_ap, c, f), dram_re)
        else:
            t1 = GD.dma(hi_ap, dram_re)
            t2 = GD.dma(stage_ap, dram_re)
        P.waitt("v", t1)
        P.waitt("v", t2)
        return P.emit("v", lambda E: E.tensor_tensor(lo_ap, stage_ap, hi_ap, OP.subtract), inc=cV)

    GD.dma(c_ident16[:], ident16[:, :])
    GD.dma(c_identb[:], identb[:, :])
    GD.dma(c_tri[:], tri_incl[:, :])
    GD.dma(c_ntri[:], ntri[:, :])
    GD.dma(c_ntril[:], ntril[:, :])
    GD.dma(c_onesf[:], onesf[:, :])
    GD.dma(c_onesb[:], onesb[:, :])
    GD.dma(c_onehot[:], onehot[:, :])
    GD.dma(c_zerob[:], zerob[:, :])
    GD.dma(gam_col[:], fgam.rearrange("c p -> p c"), allow_slow_non_contiguous=True)
    GD.dma(bet_col[:], fbet.rearrange("c p -> p c"), allow_slow_non_contiguous=True)
    GD.dma(bg_col[:], fbg[:, :])
    GD.dma(b1_col.ap().rearrange("p (e c) -> p e c", e=E, c=2), fb1.rearrange("e (c p) -> p e c", p=128), allow_slow_non_contiguous=True)
    v_ = load_split(fb2[:, :], b2_hi[:], b2_lo[:], wstage[0:16, 0:C])
    P.wait("g", cV, v_)
    GD.dma(wgf.ap().rearrange("p (c f) -> p c f", c=NCH, f=E), fwg.rearrange("c p f -> p c f"))

    v_ = load_split(awdqt.rearrange("c p f -> p c f"), wdqt_hi, wdqt_lo, wstage[:, 0:NCH * J], c=NCH, f=J)
    P.wait("g", cV, v_)
    v_ = load_split(awdkvt.rearrange("c p f -> p c f"), wdkvt_hi, wdkvt_lo, wstage[:, 0:NCH * L], c=NCH, f=L)
    P.wait("g", cV, v_)

    base = AW_END
    s_wuq, s_wuq_l = base, base + NCH * J
    base += 2 * NCH * J
    s_wuk, s_wuk_l = base, base + NCH * L
    base += 2 * NCH * L
    s_wdq, s_wdq_l = base, base + 2 * C
    base += 4 * C
    s_A, s_A_l = base, base + 2 * L
    base += 4 * L
    s_wuv, s_wuv_l = base, base + NCH * L
    base += 2 * NCH * L
    s_wot, s_wot_l = base, base + NCH * C
    assert base + 2 * NCH * C <= PBCOLS

    v_ = load_split(awuq.rearrange("c p f -> p c f"), pbm(s_wuq, NCH * J), pbm(s_wuq_l, NCH * J),
                    wstage[:, 0:NCH * J], c=NCH, f=J)
    P.wait("g", cV, v_)
    v_ = load_split(awuk.rearrange("c p f -> p c f"), pbm(s_wuk, NCH * L), pbm(s_wuk_l, NCH * L),
                    wstage[:, 0:NCH * L], c=NCH, f=L)
    P.wait("g", cV, v_)
    v_ = load_split(awdq.rearrange("c p f -> p c f"), pbm(s_wdq, 2 * C), pbm(s_wdq_l, 2 * C),
                    wstage[:, 0:2 * C], c=2, f=C)
    P.wait("g", cV, v_)
    v_ = load_split(awuv.rearrange("c p f -> p c f"), pbm(s_wuv, NCH * L), pbm(s_wuv_l, NCH * L),
                    wstage[:, 0:NCH * L], c=NCH, f=L)
    P.wait("g", cV, v_)
    for qtr in range(4):
        cols = 2 * C
        v_ = load_split(awot[2 * qtr:2 * qtr + 2].rearrange("c p f -> p c f"),
                        pbm(s_wot + qtr * cols, cols), pbm(s_wot_l + qtr * cols, cols),
                        wstage[:, 0:cols], c=2, f=C)
        P.wait("g", cV, v_)

    P.wait("t", cV)
    for jc in range(2):
        bank_acquire(jc)
        for kc in range(NCH):
            mm3(PB[jc][:, 0:L],
                pbm(s_wuq + kc * J + jc * 128, 128), pbm(s_wuq_l + kc * J + jc * 128, 128),
                pbm(s_wuk + kc * L, L), pbm(s_wuk_l + kc * L, L),
                start=(kc == 0), stop=(kc == NCH - 1), inc=(kc == NCH - 1))
        sv, vv = split_psum(PB[jc][:, 0:L], pbm(s_A + jc * L, L), pbm(s_A_l + jc * L, L), cT.val)
        bank_consumed(jc, cS, cV)
    vA = cV.val

    P.wait("t", cV, vA)
    for cc in range(NCH):
        bk = 2 + (cc % 2)
        bank_acquire(bk)
        for jc in range(2):
            mm3(PB[bk][:, 0:L],
                pbm(s_wdq + jc * C + cc * 128, 128), pbm(s_wdq_l + jc * C + cc * 128, 128),
                pbm(s_A + jc * L, L), pbm(s_A_l + jc * L, L),
                start=(jc == 0), stop=(jc == 1), inc=(jc == 1))
        split_psum(PB[bk][:, 0:L], keff_hi[:, cc * L:(cc + 1) * L], keff_lo[:, cc * L:(cc + 1) * L], cT.val)
        bank_consumed(bk, cS, cV)

    for lc in range(2):
        for mb in range(2):
            bk = 4 + mb
            bank_acquire(bk)
            for kc in range(NCH):
                mm3(PB[bk][:, 0:512],
                    pbm(s_wuv + kc * L + lc * 128, 128), pbm(s_wuv_l + kc * L + lc * 128, 128),
                    pbm(s_wot + kc * C + mb * 512, 512), pbm(s_wot_l + kc * C + mb * 512, 512),
                    start=(kc == 0), stop=(kc == NCH - 1), inc=(kc == NCH - 1))
            split_psum(PB[bk][:, 0:512], veff_hi[:, lc * C + mb * 512: lc * C + (mb + 1) * 512],
                       veff_lo[:, lc * C + mb * 512: lc * C + (mb + 1) * 512], cT.val)
            bank_consumed(bk, cS, cV)

    def ln_stats(chunk_fn, twait=None, blk=0):
        A = 6 + (blk % 2)
        bank_acquire(A)
        bank_acquire(5)
        tch = {0: 0, 1: 0}
        xsqs = (xsq.ap(), xadd[:, 512:1024])
        for ch in range(NCH):
            src = chunk_fn(ch)
            xq = xsqs[ch % 2]
            if twait is not None:
                twait("s")
            P.wait("s", cT, tch[ch % 2])
            sv = P.emit("s", lambda E, src=src, xq=xq: E.activation(xq, src, AF.Square), inc=cS)
            if twait is not None:
                twait("t")
            P.wait("t", cS, sv)
            P.emit("t", lambda E, src=src, ch=ch, A=A: E.matmul(
                PB[A][0:1, 0:512], c_onesf[:, 0:1], src,
                start=(ch == 0), stop=(ch == NCH - 1)))
            P.emit("t", lambda E, xq=xq, ch=ch: E.matmul(
                PB[5][0:1, 0:512], c_onesf[:, 0:1], xq,
                start=(ch == 0), stop=(ch == NCH - 1)), inc=cT)
            tch[ch % 2] = cT.val
        tstat = cT.val
        P.wait("v", cT, tstat)
        P.emit("v", lambda E, A=A: E.tensor_scalar_mul(rowsb_m[:], PB[A][0:1, 0:512], 1.0 / C))
        P.emit("v", lambda E: E.tensor_tensor(rowsb_r[:], rowsb_m[:], rowsb_m[:], OP.mult))
        vv = P.emit("v", lambda E: E.scalar_tensor_tensor(
            rowsb_r[:], PB[5][0:1, 0:512], 1.0 / C, rowsb_r[:],
            op0=OP.mult, op1=OP.subtract), inc=cV)
        bank_consumed(A, cV)
        bank_consumed(5, cV)
        vv = P.emit("v", lambda E: E.tensor_scalar_add(rowsb_r[:], rowsb_r[:], EPS_LN), inc=cV)
        P.wait("s", cV, vv)
        sv = P.emit("s", lambda E: E.activation(rowsb_r[:], rowsb_r[:], AF.Sqrt), inc=cS)
        P.wait("v", cS, sv)
        P.emit("v", lambda E: E.reciprocal(rowsb_r[:], rowsb_r[:]))
        vv = P.emit("v", lambda E: E.tensor_tensor(rowsb_m[:], rowsb_m[:], rowsb_r[:],
                                                   OP.mult), inc=cV)
        bank_acquire(A); bank_acquire(5)
        P.wait("t", cV, vv)
        P.emit("t", lambda E, A=A: E.matmul(PB[A][:, 0:512], c_onesf[0:1, :], rowsb_r[0:1, :],
                                            start=True, stop=True))
        P.emit("t", lambda E: E.matmul(PB[5][:, 0:512], c_onesf[0:1, :], rowsb_m[0:1, :],
                                       start=True, stop=True), inc=cT)
        trep = cT.val
        P.wait("v", cT, trep)
        P.emit("v", lambda E, A=A: E.tensor_copy(rrep[:], PB[A][:, 0:512]))
        vv = P.emit("v", lambda E: E.tensor_copy(mrep[:], PB[5][:, 0:512]), inc=cV)
        bank_consumed(A, cV); bank_consumed(5, cV)
        return tstat, vv

    lnpre = [None]
    xtile_free = {"tv": (0, 0, 0)}

    def ln_prefetch(next_src):
        if next_src is None:
            return
        t_, s_, v_ = xtile_free["tv"]
        P.wait("q", cT, t_)
        P.wait("q", cS, s_)
        P.wait("q", cV, v_)
        lnpre[0] = QD.dma(xtile.ap().rearrange("p (c t) -> p c t", c=NCH, t=512),
                          next_src[:, :, 0:512].rearrange("c p t -> p c t"))

    def ln_pass(xsrc_b, sink, write_xn=None, gate=False):
        sink_t = 0
        lnbufs = (xtile.ap(), pool_f[:, 768:768 + NCH * 512])
        prev_stat = [0, 0]
        prev_aff = [0, 0]
        prev_xn_toks = []
        tgate = {}
        for blkq in range(NTB):
            t0 = blkq * 512
            a = blkq % 2
            xt = lnbufs[a]
            if blkq == 0 and lnpre[0] is not None:
                tq = lnpre[0]
                lnpre[0] = None
            else:
                P.wait("q", cT, prev_stat[a])
                P.wait("q", cV, prev_aff[a])
                tq = QD.dma(xt.rearrange("p (c t) -> p c t", c=NCH, t=512),
                            xsrc_b[:, :, t0:t0 + 512].rearrange("c p t -> p c t"))
            tstat, _ = ln_stats(lambda ch: xt[:, ch * 512:(ch + 1) * 512],
                                twait=lambda e: P.waitt(e, tq), blk=blkq)
            prev_stat[a] = tstat
            P.wait("v", cT, sink_t)
            P.wait("s", cT, sink_t)
            for tk in prev_xn_toks:
                P.waitt("v", tk)
                P.waitt("s", tk)
            for ch in range(NCH):
                a2 = ch % 2
                if gate and ch >= 2:
                    P.wait("v", cT, tgate.get(ch - 2, 0))
                P.emit("v", lambda E, ch=ch, a2=a2, xt=xt: E.tensor_tensor(
                    xnf[:, a2 * 512:(a2 + 1) * 512], xt[:, ch * 512:(ch + 1) * 512], rrep[:], OP.mult))
                vv = P.emit("v", lambda E, ch=ch, a2=a2: E.tensor_tensor(
                    xnf[:, a2 * 512:(a2 + 1) * 512], xnf[:, a2 * 512:(a2 + 1) * 512], mrep[:],
                    OP.subtract), inc=cV)
                if gate:
                    if ch == 0:
                        bank_acquire(4)
                    P.wait("t", cV, vv)
                    P.emit("t", lambda En, ch=ch: En.matmul(
                        PB[4][0:16, 0:512], wgf[:, ch * 16:(ch + 1) * 16],
                        xnf[:, (ch % 2) * 512:((ch % 2) + 1) * 512],
                        start=(ch == 0), stop=(ch == NCH - 1)),
                        inc=cT)
                    tgate[ch] = cT.val
                P.wait("s", cV, vv)
                sv = P.emit("s", lambda E, ch=ch, a2=a2: E.activation(
                    xn_hi[:, ch * 512:(ch + 1) * 512], xnf[:, a2 * 512:(a2 + 1) * 512], AF.Copy), inc=cS)
                P.wait("v", cS, sv)
                P.emit("v", lambda E, ch=ch, a2=a2: E.tensor_tensor(
                    xn_lo[:, ch * 512:(ch + 1) * 512], xnf[:, a2 * 512:(a2 + 1) * 512],
                    xn_hi[:, ch * 512:(ch + 1) * 512], OP.subtract), inc=cV)
            vsplit = cV.val
            prev_aff[a] = vsplit
            if write_xn is not None:
                hi_d, lo_d = write_xn
                P.wait("q", cV, vsplit)
                tk1 = QD.dma(hi_d[:, :, t0:t0 + 512].rearrange("c p t -> p c t"),
                             xn_hi.ap().rearrange("p (c t) -> p c t", c=NCH, t=512))
                tk2 = QD.dma(lo_d[:, :, t0:t0 + 512].rearrange("c p t -> p c t"),
                             xn_lo.ap().rearrange("p (c t) -> p c t", c=NCH, t=512))
                prev_xn_toks = [tk1, tk2]
            sink_t = sink(blkq, vsplit)
            if blkq == 2:
                xtile_free["tv"] = (cT.val, cS.val, cV.val)
        return sink_t

    xadd_st = [None, None]

    def attn_a(b, xsrc, xdst, nxt=None):
        A0 = AW_END
        ar_ckvT_h, ar_ckvT_l = A0, A0 + 2 * T
        ar_ckv_h, ar_ckv_l = A0 + 4 * T, A0 + 6 * T
        ar_q1_h, ar_q1_l = A0 + 8 * T, A0 + 10 * T
        ar_qlT_h, ar_qlT_l = A0 + 12 * T, A0 + 14 * T
        base_a = A0 + 16 * T
        ar_wuq_h, ar_wuq_l = base_a, base_a + 256
        ar_M_h, ar_M_l = base_a + 512, base_a + 1536
        ar_pt_h, ar_pt_l = base_a + 2560, base_a + 3584
        ar_ctx_h, ar_ctx_l = base_a + 4608, base_a + 5632
        assert base_a + 6656 <= 59904

        def pth_a(pb_):
            return ar_pt_h + pb_ * 512 if pb_ < 2 else 59904

        def ptl_a(pb_):
            return ar_pt_l + pb_ * 512 if pb_ < 2 else 60416
        pf_ybuf = 0
        pf_ctxf = 2048
        pf_rsum = 3072
        pf_rra = 3584
        pf_wuq = 4096

        def proj_sink(blkq, vsplit):
            t0 = blkq * 512
            P.wait("t", cV, vsplit)
            for lc in range(2):
                bk = lc
                bank_acquire(bk)
                for kc in range(NCH):
                    mm3(PB[bk][:, 0:512],
                        wdkvt_hi[:, kc * L + lc * 128: kc * L + (lc + 1) * 128],
                        wdkvt_lo[:, kc * L + lc * 128: kc * L + (lc + 1) * 128],
                        xn_hi[:, kc * 512:(kc + 1) * 512], xn_lo[:, kc * 512:(kc + 1) * 512],
                        start=(kc == 0), stop=(kc == NCH - 1), inc=(kc == NCH - 1))
                split_psum(PB[bk][:, 0:512], pbm(ar_ckvT_h + lc * T + t0, 512),
                           pbm(ar_ckvT_l + lc * T + t0, 512), cT.val)
                bank_consumed(bk, cS, cV)
            for jc in range(2):
                bk = 2 + jc
                bank_acquire(bk)
                for kc in range(NCH):
                    mm3(PB[bk][:, 0:512],
                        wdqt_hi[:, kc * J + jc * 128: kc * J + (jc + 1) * 128],
                        wdqt_lo[:, kc * J + jc * 128: kc * J + (jc + 1) * 128],
                        xn_hi[:, kc * 512:(kc + 1) * 512], xn_lo[:, kc * 512:(kc + 1) * 512],
                        start=(kc == 0), stop=(kc == NCH - 1), inc=(kc == NCH - 1))
                split_psum(PB[bk][:, 0:512], pbm(ar_q1_h + jc * T + t0, 512),
                           pbm(ar_q1_l + jc * T + t0, 512), cT.val)
                bank_consumed(bk, cS, cV)
            return cT.val

        ln_pass(xsrc, proj_sink)
        ln_prefetch(nxt)

        for lc in range(2):
            for kt in range(NKT):
                bk = 4 + (kt % 2)
                bank_acquire(bk)
                bkb = PB[bk].ap().bitcast(BF16)
                P.emit("t", lambda E, lc=lc, kt=kt, bkb=bkb: E.matmul(
                    bkb[:, 0:128], pbm(ar_ckvT_h + lc * T + kt * 128, 128), c_identb[:],
                    is_transpose=True, start=True, stop=False))
                P.emit("t", lambda E, lc=lc, kt=kt, bkb=bkb: E.matmul(
                    bkb[:, 128:256], pbm(ar_ckvT_l + lc * T + kt * 128, 128), c_identb[:],
                    is_transpose=True, start=False, stop=True), inc=cT)
                tv = cT.val
                P.wait("v", cT, tv)
                P.emit("v", lambda E, lc=lc, kt=kt, bkb=bkb: E.tensor_copy(
                    pbm(ar_ckv_h + kt * L + lc * 128, 128), bkb[:, 0:128]))
                vv = P.emit("v", lambda E, lc=lc, kt=kt, bkb=bkb: E.tensor_copy(
                    pbm(ar_ckv_l + kt * L + lc * 128, 128), bkb[:, 128:256]), inc=cV)
                bank_consumed(bk, cV)

        tqlat_prev = 0
        pend_tail_a = [None]

        def emit_tail_a(qb, h, half, hh):
            p_ = qb % 2
            cbs = (2 + 2 * p_, 3 + 2 * p_)
            rb_ = 6 + p_
            pj_ = h // 2
            P.wait("s", cT)
            rr = pfm(pf_rsum, 512)[0:1, :]
            P.emit("s", lambda E, rr=rr, rb_=rb_: E.activation(
                rr, PB[rb_][0:1, 0:512], AF.Ln))
            sv = P.emit("s", lambda E, rr=rr: E.activation(
                rr, rr, AF.Exp, scale=-1.0), inc=cS)
            bank_consumed(rb_, cS)
            bank_acquire(rb_)
            P.wait("t", cS, sv)
            P.emit("t", lambda E, rr=rr, rb_=rb_: E.matmul(
                PB[rb_][:, 0:512], c_onesf[0:1, :], rr, start=True, stop=True), inc=cT)
            trr = cT.val
            P.wait("s", cT, trr)
            rra = pfm(pf_rra, 512)
            sv = P.emit("s", lambda E, rra=rra, rb_=rb_: E.activation(
                rra, PB[rb_][:, 0:512], AF.Copy), inc=cS)
            bank_consumed(rb_, cS)
            for lc in range(2):
                cf = pfm(pf_ctxf + lc * 512, 512)
                chi = pbm(ar_ctx_h + lc * 512, 512)
                clo = pbm(ar_ctx_l + lc * 512, 512)
                P.wait("v", cS, sv)
                vv2 = P.emit("v", lambda E, cf=cf, cb=cbs[lc], rra=rra: E.tensor_tensor(
                    cf, PB[cb][:, 0:512], rra, OP.mult), inc=cV)
                P.wait("s", cV, vv2)
                sv = P.emit("s", lambda E, chi=chi, cf=cf: E.activation(chi, cf, AF.Copy), inc=cS)
                P.wait("v", cS, sv)
                P.emit("v", lambda E, clo=clo, cf=cf, chi=chi: E.tensor_tensor(
                    clo, cf, chi, OP.subtract), inc=cV)
                bank_consumed(cbs[lc], cV)
            vctx = cV.val
            P.wait("t", cV, vctx)
            bank_acquire(cbs[0])
            for lc in range(2):
                mm3(PB[cbs[0]][half:half + 64, 0:512],
                    veff_hi[:, lc * C + h * 64: lc * C + (h + 1) * 64],
                    veff_lo[:, lc * C + h * 64: lc * C + (h + 1) * 64],
                    pbm(ar_ctx_h + lc * 512, 512), pbm(ar_ctx_l + lc * 512, 512),
                    start=(lc == 0), stop=(lc == 1), tp=(0, half),
                    inc=(lc == 1))
            ty = cT.val
            if hh == 0:
                P.wait("v", cT, ty)
                vv = P.emit("v", lambda E, qb=qb, cb=cbs[0]: E.tensor_copy(
                    pool_f[0:64, pf_ybuf + qb * 512: pf_ybuf + (qb + 1) * 512],
                    PB[cb][0:64, 0:512]), inc=cV)
                bank_consumed(cbs[0], cV)
            else:
                t0 = qb * 512
                a = qb % 2
                if xadd_st[a] is not None:
                    P.waitt("q", xadd_st[a])
                tq = QD.dma(xadd[:, a * 512:(a + 1) * 512],
                            xsrc[pj_, :, t0:t0 + 512])
                P.wait("v", cT, ty)
                P.waitt("v", tq)
                P.emit("v", lambda E, a=a, qb=qb: E.tensor_tensor(
                    xadd[0:64, a * 512:(a + 1) * 512], xadd[0:64, a * 512:(a + 1) * 512],
                    pool_f[0:64, pf_ybuf + qb * 512: pf_ybuf + (qb + 1) * 512], OP.add))
                vv = P.emit("v", lambda E, a=a, cb=cbs[0]: E.tensor_tensor(
                    xadd[64:128, a * 512:(a + 1) * 512], xadd[64:128, a * 512:(a + 1) * 512],
                    PB[cb][64:128, 0:512], OP.add), inc=cV)
                bank_consumed(cbs[0], cV)
                P.wait("q", cV, vv)
                xadd_st[a] = QD.dma(xdst[pj_, :, t0:t0 + 512],
                                    xadd[:, a * 512:(a + 1) * 512])

        P.wait("g", cT)
        P.wait("g", cS)
        P.wait("g", cV)
        wuq_tok = [None, None]
        wuq_vwu = [0, 0]
        wuq_tok[0] = GD.dma(pool_f[:, pf_wuq:pf_wuq + J], awuq[0][:, :])
        for pj in range(8):
            sl = pj % 2
            stg = pool_f[:, pf_wuq + sl * J:pf_wuq + (sl + 1) * J]
            P.wait("s", cT, tqlat_prev)
            P.waitt("s", wuq_tok[sl])
            svw = P.emit("s", lambda E, stg=stg: E.activation(
                pbm(ar_wuq_h, J), stg, AF.Copy), inc=cS)
            P.wait("v", cS, svw)
            vwu = P.emit("v", lambda E, stg=stg: E.tensor_tensor(
                pbm(ar_wuq_l, J), stg, pbm(ar_wuq_h, J), OP.subtract), inc=cV)
            wuq_vwu[sl] = vwu
            if pj < 7:
                nsl = (pj + 1) % 2
                P.wait("g", cV, wuq_vwu[nsl])
                wuq_tok[nsl] = GD.dma(pool_f[:, pf_wuq + nsl * J:pf_wuq + (nsl + 1) * J],
                                      awuq[pj + 1][:, :])

            for hh in range(2):
                h = 2 * pj + hh
                half = hh * 64
                tp = (half, 0)
                P.wait("t", cV, vwu)
                P.wait("t", cT, tqlat_prev)
                for jc in range(2):
                    bk = 2 + jc
                    bank_acquire(bk)
                    for lc in range(2):
                        mm3(PB[bk][:, lc * 128:(lc + 1) * 128],
                            pbm(ar_wuq_h + jc * 128, 128)[half:half + 64, :],
                            pbm(ar_wuq_l + jc * 128, 128)[half:half + 64, :],
                            keff_hi[half:half + 64, pj * L + lc * 128: pj * L + (lc + 1) * 128],
                            keff_lo[half:half + 64, pj * L + lc * 128: pj * L + (lc + 1) * 128],
                            start=(lc == 0), stop=(lc == 1), tp=tp, inc=(lc == 1))
                    split_psum(PB[bk][:, 0:256], pbm(ar_M_h + jc * 256, 256),
                               pbm(ar_M_l + jc * 256, 256), cT.val)
                    bank_consumed(bk, cS, cV)
                vM = cV.val
                P.wait("t", cV, vM)
                for lc in range(2):
                    for tb in range(NTB):
                        bk = (tb % 2)
                        bank_acquire(bk)
                        for jc in range(2):
                            mm3(PB[bk][:, 0:512],
                                pbm(ar_M_h + jc * 256 + lc * 128, 128),
                                pbm(ar_M_l + jc * 256 + lc * 128, 128),
                                pbm(ar_q1_h + jc * T + tb * 512, 512),
                                pbm(ar_q1_l + jc * T + tb * 512, 512),
                                start=(jc == 0), stop=(jc == 1), inc=(jc == 1))
                        split_psum(PB[bk][:, 0:512], pbm(ar_qlT_h + lc * T + tb * 512, 512),
                                   pbm(ar_qlT_l + lc * T + tb * 512, 512), cT.val)
                        bank_consumed(bk, cS, cV)
                tqlat_prev = cT.val
                vql = cV.val
                P.wait("t", cV, vql)

                for qb in range(NTB):
                    nkts = 4 * qb + 4

                    p_ = qb % 2
                    cbs = (2 + 2 * p_, 3 + 2 * p_)
                    rb_ = 6 + p_

                    def consume_kt(ehi, elo, kt, lo, vv, gv, cbs=cbs, rb_=rb_, nkts=nkts):
                        P.wait("t", cV, vv)
                        P.wait("t", cG, gv)
                        if kt == 0:
                            bank_acquire(rb_); bank_acquire(cbs[0]); bank_acquire(cbs[1])
                        P.emit("t", lambda E, ehi=ehi, kt=kt, rb_=rb_, lo=lo: E.matmul(
                            PB[rb_][0:1, lo:512], c_onesb[:, 0:1], ehi[:, lo:512],
                            start=(kt == 0), stop=False))
                        P.emit("t", lambda E, elo=elo, kt=kt, rb_=rb_, lo=lo,
                               e=(kt == nkts - 1): E.matmul(
                            PB[rb_][0:1, lo:512], c_onesb[:, 0:1], elo[:, lo:512],
                            start=False, stop=e))
                        for lc in range(2):
                            mm3(PB[cbs[lc]][:, lo:512],
                                pbm(ar_ckv_h + kt * L + lc * 128, 128),
                                pbm(ar_ckv_l + kt * L + lc * 128, 128),
                                ehi[:, lo:512], elo[:, lo:512],
                                start=(kt == 0), stop=(kt == nkts - 1),
                                inc=(lc == 1))

                    pends = []
                    for kt in range(nkts):
                        sbk = kt % 2
                        pbuf = kt % 3
                        dj = kt - 4 * qb
                        lo = max(dj, 0) * 128
                        w = 512 - lo
                        bank_acquire(sbk)
                        for lc in range(2):
                            mm3(PB[sbk][:, lo:512],
                                pbm(ar_ckvT_h + lc * T + kt * 128, 128),
                                pbm(ar_ckvT_l + lc * T + kt * 128, 128),
                                pbm(ar_qlT_h + lc * T + qb * 512 + lo, w),
                                pbm(ar_qlT_l + lc * T + qb * 512 + lo, w),
                                start=(lc == 0), stop=(lc == 1 and dj < 0),
                                inc=(lc == 1 and dj < 0))
                        if dj >= 0:
                            P.emit("t", lambda E, sbk=sbk, lo=lo: E.matmul(
                                PB[sbk][:, lo:lo + 128], c_identb[:], c_ntri[:],
                                start=False, stop=True), inc=cT)
                        tsc = cT.val
                        ehi = pbm(pth_a(pbuf), 512)
                        elo = pbm(ptl_a(pbuf), 512)
                        ef = xnf[:, (kt % 2) * 512:((kt % 2) + 1) * 512]
                        P.wait("s", cT, tsc)
                        if len(pends) == 2:
                            P.wait("s", cV, pends[0][4])
                        sv2 = P.emit("s", lambda E, sbk=sbk, ef=ef, lo=lo: E.activation(
                            ef[:, lo:512], PB[sbk][:, lo:512], AF.Exp, scale=0.125), inc=cS)
                        bank_consumed(sbk, cS)
                        sv3 = P.emit("s", lambda E, ef=ef, ehi=ehi, lo=lo: E.activation(
                            ehi[:, lo:512], ef[:, lo:512], AF.Copy), inc=cS)
                        P.wait("v", cS, sv3)
                        vv = P.emit("v", lambda E, ef=ef, ehi=ehi, elo=elo, lo=lo: E.tensor_tensor(
                            elo[:, lo:512], ef[:, lo:512], ehi[:, lo:512], OP.subtract), inc=cV)
                        pends.append((ehi, elo, kt, lo, vv, 0))
                        if len(pends) == 3:
                            consume_kt(*pends.pop(0))
                    for p_ in pends:
                        consume_kt(*p_)
                    if pend_tail_a[0] is not None:
                        emit_tail_a(*pend_tail_a[0])
                    pend_tail_a[0] = (qb, h, half, hh)
                emit_tail_a(*pend_tail_a[0])
                pend_tail_a[0] = None
        return

    def ffn(b, xsrc, xdst, nxt=None):
        fa_h, fa_hl = 0, 16384
        fa_w1 = 32768
        fa_w2 = 40960
        fa_mask = 49152
        fa_sel = 49408
        fa_z = 49664
        fa_selT = 49920
        fa_selrow = 51968
        fa_repm16 = 52992
        assert fa_repm16 + 16 * 512 <= PBCOLS
        pfa_lg = 0
        pfa_zf = 768
        assert pfa_zf + 4096 <= PFCOLS

        def ga(a, n):
            return xadd[:, a:a + n]
        ga_lgt, ga_cum, ga_off, ga_K, ga_g1 = 0, 16, 272, 288, 304

        def w1sl(sl, hl, kc):
            base = fa_w1 + sl * 4096 + hl * 2048
            return pbm(base + kc * FP, FP)

        def w2sl(sl, hl, fc):
            base = fa_w2 + sl * 4096 + hl * 2048
            return pbm(base + fc * C, C)

        P.emit("v", lambda E: E.memset(ga(ga_off, 16)[0:1, :], 0.0))

        def gate_sink(blkq, vsplit):
            t0b = blkq * 512
            tlg = cT.val
            P.wait("s", cT, tlg)
            lg = pool_f[0:16, pfa_lg:pfa_lg + 512]
            sv = P.emit("s", lambda E, lg=lg: E.activation(lg, PB[4][0:16, 0:512], AF.Identity,
                                                           bias=bg_col[:]), inc=cS)
            bank_consumed(4, cS)
            for tb in range(4):
                tile = 4 * blkq + tb
                lgt = ga(ga_lgt, 16)
                bank_acquire(5)
                P.wait("t", cS, sv)
                P.emit("t", lambda E, lg=lg, tb=tb: E.matmul(
                    PB[5][:, 0:16], lg[:, tb * 128:(tb + 1) * 128], c_ident16[:],
                    is_transpose=True, start=True, stop=True), inc=cT)
                ttr = cT.val
                P.wait("v", cT, ttr)
                P.emit("v", lambda E, lgt=lgt: E.tensor_copy(lgt, PB[5][:, 0:16]))
                bank_consumed(5, cV)
                g1 = ga(ga_g1, 16)
                g2 = ga(ga_g1 + 16, 16)
                m1 = ga(ga_g1 + 32, 1)
                P.emit("v", lambda E, lgt=lgt, m1=m1: E.tensor_reduce(
                    m1, lgt, axis=AX.X, op=OP.max))
                P.emit("v", lambda E, lgt=lgt, g1=g1, m1=m1: E.tensor_scalar(
                    g1, lgt, m1, None, op0=OP.is_ge))
                P.emit("v", lambda E, g1=g1: E.tensor_scalar_mul(g1, g1, 1e9))
                P.emit("v", lambda E, lgt=lgt, g1=g1, g2=g2: E.tensor_tensor(
                    g2, lgt, g1, OP.subtract))
                P.emit("v", lambda E, g2=g2, m1=m1: E.tensor_reduce(
                    m1, g2, axis=AX.X, op=OP.max))
                mk = pbm(fa_mask + tile * 16, 16)
                zb = pbm(fa_z, 16)
                P.emit("v", lambda E, lgt=lgt, m1=m1, mk=mk: E.tensor_scalar(
                    mk, lgt, m1, None, op0=OP.is_ge))
                vz = P.emit("v", lambda E, mk=mk, zb=zb: E.tensor_scalar(
                    zb, mk, -1.0, 1.0, op0=OP.mult, op1=OP.add))
                bank_acquire(5)
                bank_acquire(4)
                P.wait("t", cV, vz)
                P.emit("t", lambda E, zb=zb: E.matmul(PB[5][:, 0:16], c_tri[:], zb,
                                                      start=True, stop=False))
                P.emit("t", lambda E: E.matmul(PB[5][:, 0:16], c_onesf[0:1, :],
                                               ga(ga_off, 16)[0:1, :], start=False, stop=True))
                P.emit("t", lambda E, zb=zb: E.matmul(PB[4][0:1, 0:16], c_onesb[:, 0:1], zb,
                                                      start=True, stop=True), inc=cT)
                tcs = cT.val
                P.wait("v", cT, tcs)
                P.emit("v", lambda E, tile=tile: E.tensor_copy(ga(ga_cum + tile * 16, 16),
                                                               PB[5][:, 0:16]))
                vo = P.emit("v", lambda E: E.tensor_tensor(ga(ga_off, 16)[0:1, :],
                                                           ga(ga_off, 16)[0:1, :],
                                                           PB[4][0:1, 0:16], OP.add))
                bank_consumed(5, cV)
                bank_consumed(4, cV)
                P.wait("t", cV, vo)
            return cT.val

        ln_pass(xsrc, gate_sink, write_xn=(xnh_d[b], xnl_d[b]), gate=True)

        Kr = ga(ga_K, 16)
        vK = P.emit("v", lambda E: E.tensor_scalar_add(ga(ga_g1, 16)[0:1, 0:16],
                                                       ga(ga_off, 16)[0:1, :], -1024.0))
        bank_acquire(5)
        P.wait("t", cV, vK)
        P.emit("t", lambda E: E.matmul(PB[5][:, 0:16], c_onesf[0:1, :],
                                       ga(ga_g1, 16)[0:1, 0:16], start=True, stop=True), inc=cT)
        tKr = cT.val
        P.wait("v", cT, tKr)
        P.emit("v", lambda E, Kr=Kr: E.tensor_copy(Kr, PB[5][:, 0:16]))
        bank_consumed(5, cV)
        for tile in range(NKT):
            mk = pbm(fa_mask + tile * 16, 16)
            sl = pbm(fa_sel + tile * 16, 16)
            zb = pbm(fa_z, 16)
            u = ga(ga_g1, 16)
            P.emit("v", lambda E, mk=mk, zb=zb: E.tensor_scalar(
                zb, mk, -1.0, 1.0, op0=OP.mult, op1=OP.add))
            P.emit("v", lambda E, u=u, tile=tile, Kr=Kr: E.tensor_tensor(
                u, ga(ga_cum + tile * 16, 16), Kr, OP.is_le))
            P.emit("v", lambda E, u=u, zb=zb: E.tensor_tensor(u, u, zb, OP.mult))
            vs = P.emit("v", lambda E, sl=sl, u=u, mk=mk: E.tensor_tensor(sl, u, mk, OP.add))
            bank_acquire(5)
            P.wait("t", cV, vs)
            bkb = PB[5].ap().bitcast(BF16)
            P.emit("t", lambda E, sl=sl, bkb=bkb: E.matmul(bkb[0:16, 0:128], sl, c_identb[:],
                                                           is_transpose=True, start=True,
                                                           stop=True), inc=cT)
            tsT = cT.val
            P.wait("v", cT, tsT)
            P.emit("v", lambda E, tile=tile, bkb=bkb: E.tensor_copy(
                pool_b[0:16, fa_selT + tile * 128: fa_selT + (tile + 1) * 128],
                bkb[0:16, 0:128]), inc=cV)
            bank_consumed(5, cV)

        xst = None
        zaff_pend = None
        wstage_last = [cV.val]

        def emit_zaffine(ztb):
            nonlocal xst
            zt0 = ztb * 512
            if xst is not None:
                P.waitt("q", xst)
            txl = QD.dma(xtile.ap().rearrange("p (c t) -> p c t", c=NCH, t=512),
                         xsrc[:, :, zt0:zt0 + 512].rearrange("c p t -> p c t"))
            for dc in range(8):
                zf = pfm(pfa_zf + dc * 512, 512)
                P.emit("v", lambda E, zf=zf: E.tensor_tensor(zf, zf, rrep[:], OP.mult))
                P.emit("v", lambda E, zf=zf: E.tensor_tensor(zf, zf, mrep[:], OP.subtract))
                P.emit("v", lambda E, zf=zf, dc=dc: E.tensor_scalar(
                    zf, zf, gam_col[:, dc:dc + 1], bet_col[:, dc:dc + 1],
                    op0=OP.mult, op1=OP.add))
                P.waitt("v", txl)
                P.emit("v", lambda E, zf=zf, dc=dc: E.tensor_tensor(
                    xtile[:, dc * 512:(dc + 1) * 512], xtile[:, dc * 512:(dc + 1) * 512],
                    zf, OP.add), inc=cV)
            vfin = cV.val
            P.wait("q", cV, vfin)
            xst = QD.dma(xdst[:, :, zt0:zt0 + 512].rearrange("c p t -> p c t"),
                         xtile.ap().rearrange("p (c t) -> p c t", c=NCH, t=512))
        P.wait("q", cT)
        P.wait("q", cV)
        P.wait("q", cS)
        for c in QCT:
            P.wait("q", c)
        for tb in range(NTB):
            t0b = tb * 512
            th = QD.dma(xn_hi.ap().rearrange("p (c t) -> p c t", c=NCH, t=512),
                        xnh_d[b][:, :, t0b:t0b + 512].rearrange("c p t -> p c t"))
            tl = QD.dma(xn_lo.ap().rearrange("p (c t) -> p c t", c=NCH, t=512),
                        xnl_d[b][:, :, t0b:t0b + 512].rearrange("c p t -> p c t"))
            wtoks = {}

            def load_w1(e, wprev=None):
                sl = e % 2
                if e >= 1:
                    P.wait("g", cV, wtoks[e - 1])
                elif wprev is not None:
                    P.wait("g", cV, wprev)
                t1 = GD.dma(wstage.ap().rearrange("p (c f) -> p c f", c=NCH, f=FP),
                            fw1[e].rearrange("c p f -> p c f"))
                t2 = GD.dma(r3(pbm(fa_w1 + sl * 4096, 2048), NCH, FP),
                            fw1[e].rearrange("c p f -> p c f"))
                P.waitt("v", t1)
                P.waitt("v", t2)
                vv = P.emit("v", lambda E, sl=sl: E.tensor_tensor(
                    pbm(fa_w1 + sl * 4096 + 2048, 2048), wstage[:, 0:2048],
                    pbm(fa_w1 + sl * 4096, 2048), OP.subtract), inc=cV)
                wtoks[e] = vv

            pendr = None
            for e in range(E):
                rb = 2 + (e % 2)
                bank_acquire(rb)
                P.emit("t", lambda E, e=e, rb=rb, tb=tb: E.matmul(
                    PB[rb][:, 0:512], c_onehot[:, e * 128:(e + 1) * 128],
                    pool_b[0:16, fa_selT + tb * 512: fa_selT + (tb + 1) * 512],
                    start=True, stop=True), inc=cT)
                if pendr is not None:
                    pe, ptv, prb = pendr
                    P.wait("v", cT, ptv)
                    P.emit("v", lambda E, pe=pe, prb=prb: E.tensor_copy(
                        pbm(fa_repm16 + pe * 512, 512), PB[prb][:, 0:512]), inc=cV)
                    bank_consumed(prb, cV)
                pendr = (e, cT.val, rb)
            pe, ptv, prb = pendr
            P.wait("v", cT, ptv)
            P.emit("v", lambda E, pe=pe, prb=prb: E.tensor_copy(
                pbm(fa_repm16 + pe * 512, 512), PB[prb][:, 0:512]), inc=cV)
            bank_consumed(prb, cV)

            tproj = {}
            xnf_user = {0: 0, 1: 0}
            load_w1(0, wprev=wstage_last[0])
            for e in range(E):
                if e + 1 < E:
                    if e >= 1:
                        P.wait("g", cT, tproj[e - 1])
                    load_w1(e + 1)
                sl = e % 2
                P.wait("t", cV, wtoks[e])
                P.waitt("t", th)
                P.waitt("t", tl)
                for fc in range(2):
                    bk = (e * 2 + fc) % 4
                    bank_acquire(bk)
                    for kc in range(NCH):
                        mm3(PB[bk][:, 0:512],
                            w1sl(sl, 0, kc)[:, fc * 128:(fc + 1) * 128],
                            w1sl(sl, 1, kc)[:, fc * 128:(fc + 1) * 128],
                            xn_hi[:, kc * 512:(kc + 1) * 512], xn_lo[:, kc * 512:(kc + 1) * 512],
                            start=(kc == 0), stop=(kc == NCH - 1), inc=(kc == NCH - 1))
                    tproj[e] = cT.val
                    hh = pbm(fa_h + (e * 2 + fc) * 512, 512)
                    hl = pbm(fa_hl + (e * 2 + fc) * 512, 512)
                    a = fc
                    P.wait("s", cT, tproj[e])
                    P.wait("s", cV, xnf_user[a])
                    sv2 = P.emit("s", lambda E, a=a, bk=bk, e=e, fc=fc: E.activation(
                        xnf[:, a * 512:(a + 1) * 512], PB[bk][:, 0:512], AF.Gelu,
                        bias=b1_col[:, e * 2 + fc:e * 2 + fc + 1]), inc=cS)
                    P.wait("v", cS, sv2)
                    P.emit("v", lambda E, hh=hh, a=a: E.tensor_copy(
                        hh, xnf[:, a * 512:(a + 1) * 512]))
                    xnf_user[a] = P.emit("v", lambda E, hl=hl, a=a, hh=hh: E.tensor_tensor(
                        hl, xnf[:, a * 512:(a + 1) * 512], hh, OP.subtract), inc=cV)
                    bank_consumed(bk, cS, cV)
                for fc in range(2):
                    hh = pbm(fa_h + (e * 2 + fc) * 512, 512)
                    hl = pbm(fa_hl + (e * 2 + fc) * 512, 512)
                    P.emit("v", lambda E, hh=hh, e=e: E.tensor_tensor(
                        hh, hh, pbm(fa_repm16 + e * 512, 512), OP.mult))
                    P.emit("v", lambda E, hl=hl, e=e: E.tensor_tensor(
                        hl, hl, pbm(fa_repm16 + e * 512, 512), OP.mult), inc=cV)
            vhall = cV.val

            if zaff_pend is not None:
                emit_zaffine(zaff_pend)
                zaff_pend = None

            w2toks = {}

            def load_w2(e, wprev=None):
                sl = e % 2
                if e >= 1:
                    P.wait("g", cV, w2toks[e - 1])
                elif wprev is not None:
                    P.wait("g", cV, wprev)
                t1 = GD.dma(wstage.ap().rearrange("p (c f) -> p c f", c=2, f=C),
                            fw2[e].rearrange("c p f -> p c f"))
                t2 = GD.dma(r3(pbm(fa_w2 + sl * 4096, 2048), 2, C),
                            fw2[e].rearrange("c p f -> p c f"))
                P.waitt("v", t1)
                P.waitt("v", t2)
                vv = P.emit("v", lambda E, sl=sl: E.tensor_tensor(
                    pbm(fa_w2 + sl * 4096 + 2048, 2048), wstage[:, 0:2048],
                    pbm(fa_w2 + sl * 4096, 2048), OP.subtract), inc=cV)
                w2toks[e] = vv

            tw2 = {}
            load_w2(0, wprev=wtoks[E - 1])
            load_w2(1)
            wstage_last[0] = w2toks[1]
            P.wait("t", cV, vhall)
            for dc in range(8):
                bank_acquire(dc)
            for e in range(E):
                if e + 1 < E and e >= 1:
                    P.wait("g", cT, tw2[e - 1])
                    load_w2(e + 1)
                    wstage_last[0] = w2toks[e + 1]
                sl = e % 2
                P.wait("t", cV, w2toks[e])
                for dc in range(8):
                    if e == 0:
                        P.emit("t", lambda E, dc=dc, tb=tb: E.matmul(
                            PB[dc][:, 0:512], b2_hi[:, dc * 128:(dc + 1) * 128],
                            pool_b[0:16, fa_selT + tb * 512: fa_selT + (tb + 1) * 512],
                            start=True, stop=False))
                        P.emit("t", lambda E, dc=dc, tb=tb: E.matmul(
                            PB[dc][:, 0:512], b2_lo[:, dc * 128:(dc + 1) * 128],
                            pool_b[0:16, fa_selT + tb * 512: fa_selT + (tb + 1) * 512],
                            start=False, stop=False))
                    for fc in range(2):
                        hh = pbm(fa_h + (e * 2 + fc) * 512, 512)
                        hl = pbm(fa_hl + (e * 2 + fc) * 512, 512)
                        mm3(PB[dc][:, 0:512],
                            w2sl(sl, 0, fc)[:, dc * 128:(dc + 1) * 128],
                            w2sl(sl, 1, fc)[:, dc * 128:(dc + 1) * 128],
                            hh, hl,
                            start=False, stop=(e == E - 1 and fc == 1),
                            inc=(fc == 1))
                tw2[e] = cT.val
            tout = cT.val

            for dc in (2, 3, 0, 1, 4, 5, 6, 7):
                zf = pfm(pfa_zf + dc * 512, 512)
                P.waitt("v", th)
                P.waitt("v", tl)
                P.emit("v", lambda E, zf=zf, dc=dc: E.tensor_tensor(
                    zf, xn_hi[:, dc * 512:(dc + 1) * 512], xn_lo[:, dc * 512:(dc + 1) * 512],
                    OP.add))
                P.wait("v", cT, tout)
                P.emit("v", lambda E, zf=zf, dc=dc: E.tensor_tensor(
                    zf, zf, PB[dc][:, 0:512], OP.add), inc=cV)
                bank_consumed(dc, cV)
            vz = cV.val
            ln_stats(lambda ch: pfm(pfa_zf + ch * 512, 512),
                     twait=lambda e: P.wait(e, cV, vz), blk=tb)
            zaff_pend = tb
            P.wait("q", cT, tout)
            P.wait("q", cV, vz)
        if zaff_pend is not None:
            emit_zaffine(zaff_pend)
        if nxt is not None:
            if xst is not None:
                P.waitt("q", xst)
            lnpre[0] = QD.dma(xtile.ap().rearrange("p (c t) -> p c t", c=NCH, t=512),
                              nxt[:, :, 0:512].rearrange("c p t -> p c t"))
        return

    def attn_b(b, xsrc, xdst, nxt=None):
        ba_y, ba_yl = 0, 16384
        ba_w = 32768
        ba_q, ba_ql = 38912, 40960
        ba_k, ba_kl = 43008, 45056
        ba_vT, ba_vTl = 47104, 47616
        ba_vt, ba_vtl = 48128, 50176
        ba_pt, ba_ptl = 52224, 53248

        def pth(pb_):
            return ba_pt + pb_ * 512 if pb_ < 2 else 60416

        def ptl(pb_):
            return ba_ptl + pb_ * 512 if pb_ < 2 else 60928
        pfb_rs = 0
        pfb_ef = 768
        pfb_yf = 2816
        pfb_rr = 3328
        pfb_ef3 = 3840

        def noop_sink(blkq, vsplit):
            return 0

        ln_pass(xsrc, noop_sink, write_xn=(xnh_d[b], xnl_d[b]))
        ln_prefetch(nxt)
        for c in QCT:
            P.wait("q", c)

        def wsl(base, hl, kc):
            return pbm(base + hl * 1024 + kc * 128, 128)

        def load_wqkv(pj, wprev):
            slot = ba_w if pj % 2 == 0 else 54272
            toks = []
            for wi, wd in enumerate((bwq, bwk, bwv)):
                base = slot + wi * 2048
                if wprev is not None:
                    P.wait("g", cV, wprev)
                t1 = GD.dma(wstage[:, 0:1024].rearrange("p (k f) -> p k f", k=NCH, f=128),
                            wd[pj].rearrange("(k p) f -> p k f", p=128))
                t2 = GD.dma(r3(pbm(base, 1024), NCH, 128),
                            wd[pj].rearrange("(k p) f -> p k f", p=128))
                P.waitt("v", t1)
                P.waitt("v", t2)
                wprev = P.emit("v", lambda E, base=base: E.tensor_tensor(
                    pbm(base + 1024, 1024), wstage[:, 0:1024], pbm(base, 1024), OP.subtract),
                    inc=cV)
                toks.append(wprev)
            return toks

        P.wait("g", cV)
        P.wait("g", cT)
        wvtoks = load_wqkv(0, None)
        tproj_prev = 0
        for pj in range(8):
            slot = ba_w if pj % 2 == 0 else 54272
            for tb in range(NTB):
                t0b = tb * 512
                P.wait("q", cT, tproj_prev)
                th = QD.dma(xn_hi.ap().rearrange("p (c t) -> p c t", c=NCH, t=512),
                            xnh_d[b][:, :, t0b:t0b + 512].rearrange("c p t -> p c t"))
                tl = QD.dma(xn_lo.ap().rearrange("p (c t) -> p c t", c=NCH, t=512),
                            xnl_d[b][:, :, t0b:t0b + 512].rearrange("c p t -> p c t"))
                P.waitt("t", th)
                P.waitt("t", tl)
                P.wait("t", cV, wvtoks[2])
                svv = None
                for wi, (dsth, dstl) in enumerate(((ba_q, ba_ql), (ba_k, ba_kl), (ba_vT, ba_vTl))):
                    bk = wi
                    base = slot + wi * 2048
                    bank_acquire(bk)
                    for kc in range(NCH):
                        mm3(PB[bk][:, 0:512],
                            wsl(base, 0, kc), wsl(base, 1, kc),
                            xn_hi[:, kc * 512:(kc + 1) * 512], xn_lo[:, kc * 512:(kc + 1) * 512],
                            start=(kc == 0), stop=(kc == NCH - 1), inc=(kc == NCH - 1))
                    if wi < 2:
                        svv = split_psum(PB[bk][:, 0:512], pbm(dsth + t0b, 512),
                                         pbm(dstl + t0b, 512), cT.val)
                    else:
                        svv = split_psum(PB[bk][:, 0:512], pbm(ba_vT, 512), pbm(ba_vTl, 512),
                                         cT.val)
                    bank_consumed(bk, cS, cV)
                tproj_prev = cT.val
                P.wait("t", cS, svv[0])
                P.wait("t", cV, svv[1])
                for tt in range(4):
                    kt = 4 * tb + tt
                    bk = 3 + (tt % 2)
                    bank_acquire(bk)
                    bkb = PB[bk].ap().bitcast(BF16)
                    P.emit("t", lambda E, tt=tt, bkb=bkb: E.matmul(
                        bkb[:, 0:128], pbm(ba_vT + tt * 128, 128), c_identb[:],
                        is_transpose=True, start=True, stop=False))
                    P.emit("t", lambda E, tt=tt, bkb=bkb: E.matmul(
                        bkb[:, 128:256], pbm(ba_vTl + tt * 128, 128), c_identb[:],
                        is_transpose=True, start=False, stop=True), inc=cT)
                    tv = cT.val
                    P.wait("v", cT, tv)
                    P.emit("v", lambda E, kt=kt, bkb=bkb: E.tensor_copy(
                        pbm(ba_vt + kt * 128, 128), bkb[:, 0:128]))
                    vv = P.emit("v", lambda E, kt=kt, bkb=bkb: E.tensor_copy(
                        pbm(ba_vtl + kt * 128, 128), bkb[:, 128:256]), inc=cV)
                    bank_consumed(bk, cV)
            vproj = cV.val
            if pj + 1 < 8:
                wvtoks = load_wqkv(pj + 1, wvtoks[2])
            for hh in range(2):
                half = hh * 64
                tp = (half, 0)
                P.wait("t", cV, vproj)
                gk = 0

                def win(qb, kt):
                    rel = kt - 4 * qb
                    jlo = max(rel, 0)
                    jhi = min(rel + 1, 3)
                    return jlo * 128, (jhi + 1) * 128, rel

                def consume_bkt(ehi, elo, qb, kt, first, last, vv):
                    rb = 2 + (qb % 2)
                    yb = 6 + (qb % 2)
                    lo, hi, rel = win(qb, kt)
                    P.wait("t", cV, vv)
                    if first:
                        bank_acquire(rb)
                        bank_acquire(yb)
                        P.emit("t", lambda E, rb=rb, ehi=ehi: E.matmul(
                            PB[rb][0:1, 0:512], c_zerob[:, 0:1], ehi, start=True, stop=False))
                        P.emit("t", lambda E, yb=yb, ehi=ehi, half=half: E.matmul(
                            PB[yb][half:half + 64, 0:512], c_zerob[:, 0:64], ehi,
                            start=True, stop=False, tile_position=(0, half)))
                    P.emit("t", lambda E, ehi=ehi, rb=rb, lo=lo, hi=hi: E.matmul(
                        PB[rb][0:1, lo:hi], c_onesb[:, 0:1], ehi[:, lo:hi],
                        start=False, stop=False))
                    P.emit("t", lambda E, elo=elo, rb=rb, lo=lo, hi=hi, last=last: E.matmul(
                        PB[rb][0:1, lo:hi], c_onesb[:, 0:1], elo[:, lo:hi],
                        start=False, stop=last))
                    mm3(PB[yb][half:half + 64, lo:hi],
                        pbm(ba_vt + kt * 128, 128)[:, half:half + 64],
                        pbm(ba_vtl + kt * 128, 128)[:, half:half + 64],
                        ehi[:, lo:hi], elo[:, lo:hi],
                        start=False, stop=last, tp=(0, half), inc=last)

                def emit_tail(qb, sv):
                    yb = 6 + (qb % 2)
                    rs = pool_f[0:1, pfb_rs:pfb_rs + 512]
                    bank_acquire(4)
                    P.wait("t", cS, sv)
                    P.emit("t", lambda E, rs=rs, half=half: E.matmul(
                        PB[4][half:half + 64, 0:512], c_onesf[0:1, 0:64], rs,
                        start=True, stop=True, tile_position=(0, half)), inc=cT)
                    trr = cT.val
                    yh = pbm(ba_y + pj * 2048 + qb * 512, 512)[half:half + 64, :]
                    yl = pbm(ba_yl + pj * 2048 + qb * 512, 512)[half:half + 64, :]
                    yf = pool_f[half:half + 64, pfb_yf:pfb_yf + 512]
                    rrb = pool_f[half:half + 64, pfb_rr:pfb_rr + 512]
                    P.wait("s", cT, trr)
                    sv = P.emit("s", lambda E, rrb=rrb, half=half: E.activation(
                        rrb, PB[4][half:half + 64, 0:512], AF.Copy), inc=cS)
                    bank_consumed(4, cS)
                    P.wait("v", cS, sv)
                    vv = P.emit("v", lambda E, yf=yf, rrb=rrb, yb=yb, half=half: E.tensor_tensor(
                        yf, PB[yb][half:half + 64, 0:512], rrb, OP.mult), inc=cV)
                    bank_consumed(yb, cV)
                    P.wait("s", cV, vv)
                    sv = P.emit("s", lambda E, yh=yh, yf=yf: E.activation(yh, yf, AF.Copy), inc=cS)
                    P.wait("v", cS, sv)
                    P.emit("v", lambda E, yl=yl, yf=yf, yh=yh: E.tensor_tensor(
                        yl, yf, yh, OP.subtract), inc=cV)

                pends = []
                pend_tail = None
                efvv = {0: 0, 1: 0, 2: 0}

                def pop_one():
                    nonlocal pend_tail
                    ent = pends.pop(0)
                    consume_bkt(*ent)
                    if ent[5]:
                        if pend_tail is not None:
                            emit_tail(*pend_tail)
                        qb_ = ent[2]
                        rb = 2 + (qb_ % 2)
                        ty = cT.val
                        P.wait("s", cT, ty)
                        rs = pool_f[0:1, pfb_rs:pfb_rs + 512]
                        P.emit("s", lambda E, rs=rs, rb=rb: E.activation(
                            rs, PB[rb][0:1, 0:512], AF.Ln))
                        sv = P.emit("s", lambda E, rs=rs: E.activation(
                            rs, rs, AF.Exp, scale=-1.0), inc=cS)
                        bank_consumed(rb, cS)
                        pend_tail = (qb_, sv)
                for qb in range(NTB):
                    kts = [k for k in range(4 * qb - 1, 4 * qb + 4) if k >= 0]
                    for ki, kt in enumerate(kts):
                        lo, hi, rel = win(qb, kt)
                        w = hi - lo
                        sbk = gk % 2
                        pb3 = gk % 3
                        gk += 1
                        bank_acquire(sbk)
                        mm3(PB[sbk][:, lo:hi],
                            pbm(ba_k + kt * 128, 128)[half:half + 64, :],
                            pbm(ba_kl + kt * 128, 128)[half:half + 64, :],
                            pbm(ba_q + qb * 512 + lo, w)[half:half + 64, :],
                            pbm(ba_ql + qb * 512 + lo, w)[half:half + 64, :],
                            start=True, stop=False, tp=tp, inc=False)
                        mjs = [jj for jj in (rel, rel + 1) if 0 <= jj <= 3]
                        for mi, jj in enumerate(mjs):
                            msk = c_ntri if jj == rel else c_ntril
                            P.emit("t", lambda E, sbk=sbk, jj=jj, msk=msk,
                                   e=(mi == len(mjs) - 1): E.matmul(
                                PB[sbk][:, jj * 128:(jj + 1) * 128], c_identb[:], msk[:],
                                start=False, stop=e),
                                inc=cT if mi == len(mjs) - 1 else None)
                        tsc = cT.val
                        ehi = pbm(pth(pb3), 512)
                        elo = pbm(ptl(pb3), 512)
                        ef = pfm(pfb_ef + pb3 * 1024 if pb3 < 2 else pfb_ef3, 512)
                        P.wait("s", cT, tsc)
                        P.wait("s", cV, efvv[pb3])
                        sv2 = P.emit("s", lambda E, ef=ef, sbk=sbk, lo=lo, hi=hi: E.activation(
                            ef[:, lo:hi], PB[sbk][:, lo:hi], AF.Exp, scale=0.125), inc=cS)
                        bank_consumed(sbk, cS)
                        sv3 = P.emit("s", lambda E, ef=ef, ehi=ehi, lo=lo, hi=hi: E.activation(
                            ehi[:, lo:hi], ef[:, lo:hi], AF.Copy), inc=cS)
                        P.wait("v", cS, sv3)
                        vv = P.emit("v", lambda E, ef=ef, ehi=ehi, elo=elo, lo=lo, hi=hi:
                                    E.tensor_tensor(elo[:, lo:hi], ef[:, lo:hi], ehi[:, lo:hi],
                                                    OP.subtract), inc=cV)
                        efvv[pb3] = vv
                        pends.append((ehi, elo, qb, kt, ki == 0, ki == len(kts) - 1, vv))
                        if len(pends) == 3:
                            pop_one()
                while pends:
                    pop_one()
                emit_tail(*pend_tail)
        P.wait("g", cV, wvtoks[2])
        P.wait("g", cT, tproj_prev)
        wotoks = []
        for pj in range(8):
            base = ba_w + pj * 2048
            if pj >= 1:
                P.wait("g", cV, wotoks[pj - 1])
            t1 = GD.dma(wstage[:, 0:1024], bwot[pj][:, :])
            t2 = GD.dma(pbm(base, 1024), bwot[pj][:, :])
            P.waitt("v", t1)
            P.waitt("v", t2)
            vv = P.emit("v", lambda E, base=base: E.tensor_tensor(
                pbm(base + 1024, 1024), wstage[:, 0:1024], pbm(base, 1024), OP.subtract), inc=cV)
            wotoks.append(vv)
        vwo = cV.val
        P.wait("t", cV, vwo)
        xst = None
        for fc in range(NCH):
            for tb in range(NTB):
                t0b = tb * 512
                bk = (fc * NTB + tb) % 2
                bank_acquire(bk)
                for pj in range(8):
                    mm3(PB[bk][:, 0:512],
                        pbm(ba_w + pj * 2048 + fc * 128, 128),
                        pbm(ba_w + pj * 2048 + 1024 + fc * 128, 128),
                        pbm(ba_y + pj * 2048 + t0b, 512), pbm(ba_yl + pj * 2048 + t0b, 512),
                        start=(pj == 0), stop=(pj == 7), inc=(pj == 7))
                to = cT.val
                a = (fc * NTB + tb) % 2
                if xst is not None:
                    P.waitt("q", xst)
                txl = QD.dma(xadd[:, a * 512:(a + 1) * 512], xsrc[fc, :, t0b:t0b + 512])
                P.waitt("v", txl)
                P.wait("v", cT, to)
                vv = P.emit("v", lambda E, a=a, bk=bk: E.tensor_tensor(
                    xadd[:, a * 512:(a + 1) * 512], xadd[:, a * 512:(a + 1) * 512],
                    PB[bk][:, 0:512], OP.add), inc=cV)
                bank_consumed(bk, cV)
                P.wait("q", cV, vv)
                xst = QD.dma(xdst[fc, :, t0b:t0b + 512], xadd[:, a * 512:(a + 1) * 512])
        return

    srcs = [xin, xres, xres, xres]
    dsts = [xres, xres, xres, xres]
    dsts[N_SUBLAYERS - 1] = out_fm

    seq = [(attn_a, b, srcs[0][b], dsts[0][b]) for b in range(B)]
    if N_SUBLAYERS > 1:
        seq += [(ffn, b, srcs[1][b], dsts[1][b]) for b in range(B)]
    if N_SUBLAYERS > 2:
        seq += [(attn_b, b, srcs[2][b], dsts[2][b]) for b in range(B)]
    if N_SUBLAYERS > 3:
        seq += [(ffn, b, srcs[3][b], dsts[3][b]) for b in range(B)]
    for i, (fn, b, s_, d_) in enumerate(seq):
        fn(b, s_, d_, nxt=(seq[i + 1][2] if i + 1 < len(seq) else None))
        stage_barrier()

    for c in QCT + WCT:
        P.wait("q", c)
        P.wait("g", c)

    with nc.Block() as blk:
        blk.sync(lambda E: [f(E) for f in P.q["q"]])
        blk.tensor(lambda E: [f(E) for f in P.q["t"]])
        blk.vector(lambda E: [f(E) for f in P.q["v"]])
        blk.scalar(lambda E: [f(E) for f in P.q["s"]])
        blk.gpsimd(lambda E: [f(E) for f in P.q["g"]])

    ctx.close()
    return nc


def host_inputs(inputs, core):
    x = inputs["L0_attn_a_x"]
    m = {}
    xb = np.ascontiguousarray(x[2 * core:2 * core + 2].transpose(0, 2, 1))
    m["xin"] = xb.reshape(B, NCH, 128, T)
    m["awdqt"] = np.ascontiguousarray(inputs["L0_attn_a_W_dq"].T).reshape(NCH, 128, J)
    m["awdq"] = inputs["L0_attn_a_W_dq"].reshape(2, 128, C)
    m["awuqt"] = np.ascontiguousarray(inputs["L0_attn_a_W_uq"].T).reshape(2, 128, C)
    m["awuq"] = inputs["L0_attn_a_W_uq"].reshape(NCH, 128, J)
    m["awuk"] = inputs["L0_attn_a_W_uk"].reshape(NCH, 128, L)
    m["awdkvt"] = np.ascontiguousarray(inputs["L0_attn_a_W_dkv"].T).reshape(NCH, 128, L)
    m["awuv"] = inputs["L0_attn_a_W_uv"].reshape(NCH, 128, L)
    m["awot"] = np.ascontiguousarray(inputs["L0_attn_a_W_o"].T).reshape(NCH, 128, C)
    m["bwq"] = np.ascontiguousarray(inputs["L0_attn_b_Wq"].T.reshape(C, NCH, 128).transpose(1, 0, 2))
    m["bwk"] = np.ascontiguousarray(inputs["L0_attn_b_Wk"].T.reshape(C, NCH, 128).transpose(1, 0, 2))
    m["bwv"] = np.ascontiguousarray(inputs["L0_attn_b_Wv"].T.reshape(C, NCH, 128).transpose(1, 0, 2))
    m["bwot"] = np.ascontiguousarray(inputs["L0_attn_b_Wo"].T).reshape(NCH, 128, C)
    m["fwg"] = inputs["L0_ffn_wg"].reshape(NCH, 128, E)
    m["fbg"] = inputs["L0_ffn_bg"].reshape(16, 1)
    m["fw1"] = inputs["L0_ffn_w1"].reshape(E, NCH, 128, FP)
    m["fb1"] = inputs["L0_ffn_b1"]
    m["fw2"] = inputs["L0_ffn_w2"].reshape(E, 2, 128, C)
    m["fb2"] = inputs["L0_ffn_b2"]
    m["fgam"] = inputs["L0_ffn_gamma"].reshape(NCH, 128)
    m["fbet"] = inputs["L0_ffn_beta"].reshape(NCH, 128)
    m["ident16"] = np.eye(16, dtype=np.float32)
    import ml_dtypes
    m["identb"] = np.eye(128, dtype=ml_dtypes.bfloat16)
    ii = np.arange(128)
    m["tri_incl"] = (ii[None, :] >= ii[:, None]).astype(ml_dtypes.bfloat16)
    m["ntri"] = (((ii[None, :] >= ii[:, None]).astype(np.float32) - 1.0)
                 * 1e9).astype(ml_dtypes.bfloat16)
    m["ntril"] = (((ii[None, :] < ii[:, None]).astype(np.float32) - 1.0)
                  * 1e9).astype(ml_dtypes.bfloat16)
    m["onesf"] = np.ones((128, 128), dtype=np.float32)
    m["onesb"] = np.ones((128, 128), dtype=ml_dtypes.bfloat16)
    oh = np.zeros((16, 2048), dtype=ml_dtypes.bfloat16)
    for e in range(16):
        oh[e, e * 128:(e + 1) * 128] = 1
    m["onehot"] = oh
    m["zerob"] = np.zeros((128, 64), dtype=ml_dtypes.bfloat16)
    return m


_nc_cache = {}


def kernel(**inputs):
    key = N_SUBLAYERS
    if key not in _nc_cache:
        _nc_cache[key] = build_program()
    nc = _nc_cache[key]
    in_maps = [host_inputs(inputs, core) for core in range(8)]
    res = run_bass_kernel_spmd(nc, in_maps, core_ids=list(range(8)))
    outs = []
    for core in range(8):
        o = res.results[core]["out_fm"].reshape(B, C, T)
        outs.append(np.ascontiguousarray(o.transpose(0, 2, 1)))
    return np.concatenate(outs, axis=0)


# version 21 (final)
# speedup vs baseline: 1.0165x; 1.0010x over previous
import numpy as np
from contextlib import ExitStack
import concourse.bass as bass
import concourse.mybir as mybir
from concourse.bass_utils import run_bass_kernel_spmd

F32 = mybir.dt.float32
BF16 = mybir.dt.bfloat16
AF = mybir.ActivationFunctionType
OP = mybir.AluOpType
AX = mybir.AxisListType

N_SUBLAYERS = 4

B = 2
T = 2048
C = 1024
NCH = 8
L = 256
J = 256
NH = 16
HS = 64
E = 16
FP = 256
NTB = 4
NKT = 16
EPS_LN = 1e-5


class Ctr:
    def __init__(self, sem, name):
        self.sem = sem
        self.name = name
        self.val = 0


class Prog:
    ENGS = ("t", "v", "s", "g", "q")

    def __init__(self):
        self.q = {e: [] for e in self.ENGS}
        self.waited = {}
        self.own = {}

    def emit(self, eng, f, inc=None, n=1):
        own = self.own.get(eng)
        if own is not None:
            self.wait(eng, own)
            if inc is None:
                inc = own
        if inc is not None:
            inc.val += n
            sem, amount = inc.sem, n

            def g(En):
                f(En).then_inc(sem, amount)
        else:
            def g(En):
                f(En)
        self.q[eng].append(g)
        return inc.val if inc is not None else None

    def wait(self, eng, ctr, v=None):
        if v is None:
            v = ctr.val
        assert v <= ctr.val, f"wait {ctr.name}>={v} but only {ctr.val} emitted"
        if v <= 0:
            return
        k = (eng, ctr.name)
        if self.waited.get(k, 0) >= v:
            return
        self.waited[k] = v
        sem = ctr.sem
        self.q[eng].append(lambda En: En.wait_ge(sem, v))

    def waitt(self, eng, tok):
        ctr, v = tok
        self.wait(eng, ctr, v)


class DmaQ:

    def __init__(self, P, eng, ctrs):
        self.P, self.eng, self.ctrs = P, eng, ctrs
        self.i = 0

    def dma(self, out, in_, **kw):
        c = self.ctrs[self.i % len(self.ctrs)]
        self.i += 1
        self.P.wait(self.eng, c)
        c.val += 16
        sem, v = c.sem, c.val

        def g(En):
            En.dma_start(out=out, in_=in_, **kw).then_inc(sem, 16)
        self.P.q[self.eng].append(g)
        return (c, v)


def build_program():
    nc = bass.Bass("TRN2", target_bir_lowering=False)

    xin = nc.dram_tensor("xin", [B, NCH, 128, T], F32, kind="ExternalInput").ap()
    out_fm = nc.dram_tensor("out_fm", [B, NCH, 128, T], F32, kind="ExternalOutput").ap()

    awdqt = nc.dram_tensor("awdqt", [NCH, 128, J], F32, kind="ExternalInput").ap()
    awdq = nc.dram_tensor("awdq", [2, 128, C], F32, kind="ExternalInput").ap()
    awuqt = nc.dram_tensor("awuqt", [2, 128, C], F32, kind="ExternalInput").ap()
    awuq = nc.dram_tensor("awuq", [NCH, 128, J], F32, kind="ExternalInput").ap()
    awuk = nc.dram_tensor("awuk", [NCH, 128, L], F32, kind="ExternalInput").ap()
    awdkvt = nc.dram_tensor("awdkvt", [NCH, 128, L], F32, kind="ExternalInput").ap()
    awuv = nc.dram_tensor("awuv", [NCH, 128, L], F32, kind="ExternalInput").ap()
    awot = nc.dram_tensor("awot", [NCH, 128, C], F32, kind="ExternalInput").ap()

    bwq = nc.dram_tensor("bwq", [NCH, C, 128], F32, kind="ExternalInput").ap()
    bwk = nc.dram_tensor("bwk", [NCH, C, 128], F32, kind="ExternalInput").ap()
    bwv = nc.dram_tensor("bwv", [NCH, C, 128], F32, kind="ExternalInput").ap()
    bwot = nc.dram_tensor("bwot", [NCH, 128, C], F32, kind="ExternalInput").ap()

    fwg = nc.dram_tensor("fwg", [NCH, 128, E], F32, kind="ExternalInput").ap()
    fbg = nc.dram_tensor("fbg", [16, 1], F32, kind="ExternalInput").ap()
    fw1 = nc.dram_tensor("fw1", [E, NCH, 128, FP], F32, kind="ExternalInput").ap()
    fb1 = nc.dram_tensor("fb1", [E, FP], F32, kind="ExternalInput").ap()
    fw2 = nc.dram_tensor("fw2", [E, 2, 128, C], F32, kind="ExternalInput").ap()
    fb2 = nc.dram_tensor("fb2", [E, C], F32, kind="ExternalInput").ap()
    fgam = nc.dram_tensor("fgam", [NCH, 128], F32, kind="ExternalInput").ap()
    fbet = nc.dram_tensor("fbet", [NCH, 128], F32, kind="ExternalInput").ap()

    ident16 = nc.dram_tensor("ident16", [16, 16], F32, kind="ExternalInput").ap()
    identb = nc.dram_tensor("identb", [128, 128], BF16, kind="ExternalInput").ap()
    tri_incl = nc.dram_tensor("tri_incl", [128, 128], BF16, kind="ExternalInput").ap()
    ntri = nc.dram_tensor("ntri", [128, 128], BF16, kind="ExternalInput").ap()
    ntril = nc.dram_tensor("ntril", [128, 128], BF16, kind="ExternalInput").ap()
    onesf = nc.dram_tensor("onesf", [128, 128], F32, kind="ExternalInput").ap()
    onesb = nc.dram_tensor("onesb", [128, 128], BF16, kind="ExternalInput").ap()
    onehot = nc.dram_tensor("onehot", [16, 2048], BF16, kind="ExternalInput").ap()
    zerob = nc.dram_tensor("zerob", [128, 64], BF16, kind="ExternalInput").ap()

    xres = nc.dram_tensor("xres", [B, NCH, 128, T], F32).ap()
    xnh_d = nc.dram_tensor("xnh_d", [B, NCH, 128, T], BF16).ap()
    xnl_d = nc.dram_tensor("xnl_d", [B, NCH, 128, T], BF16).ap()

    ctx = ExitStack()

    def sbuf(name, shape, dt=F32):
        return ctx.enter_context(nc.sbuf_tensor(name, shape, dt))

    c_ident16 = sbuf("c_ident16", [16, 16])
    c_identb = sbuf("c_identb", [128, 128], BF16)
    c_tri = sbuf("c_tri", [128, 128], BF16)
    c_ntri = sbuf("c_ntri", [128, 128], BF16)
    c_ntril = sbuf("c_ntril", [128, 128], BF16)
    c_onesf = sbuf("c_onesf", [128, 128])
    c_onesb = sbuf("c_onesb", [128, 128], BF16)
    c_onehot = sbuf("c_onehot", [16, 2048], BF16)
    c_zerob = sbuf("c_zerob", [128, 64], BF16)
    gam_col = sbuf("gam_col", [128, NCH])
    bet_col = sbuf("bet_col", [128, NCH])
    bg_col = sbuf("bg_col", [16, 1])
    b1_col = sbuf("b1_col", [128, 2 * E])
    b2_hi = sbuf("b2_hi", [16, C], BF16)
    b2_lo = sbuf("b2_lo", [16, C], BF16)
    wgf = sbuf("wgf", [128, NCH * E])

    wstage = sbuf("wstage", [128, 2 * C])

    xtile = sbuf("xtile", [128, NCH * 512])
    xsq = sbuf("xsq", [128, 512])
    xnf = sbuf("xnf", [128, 2 * 512])
    xn_hi = sbuf("xn_hi", [128, NCH * 512], BF16)
    xn_lo = sbuf("xn_lo", [128, NCH * 512], BF16)
    rrep = sbuf("rrep", [128, 512])
    mrep = sbuf("mrep", [128, 512])
    rowsb_r = sbuf("rowsb_r", [1, 512])
    rowsb_m = sbuf("rowsb_m", [1, 512])
    xadd = sbuf("xadd", [128, 2 * 512])

    PBCOLS = 61440
    PFCOLS = 4864
    pool_b = sbuf("pool_b", [128, PBCOLS], BF16)
    pool_f = sbuf("pool_f", [128, PFCOLS])

    AW = 0
    aw_wdqt_h, aw_wdqt_l = 0, 2048
    aw_wuqt_h, aw_wuqt_l = 4096, 6144
    aw_wdkvt_h, aw_wdkvt_l = 8192, 10240
    aw_keff_h, aw_keff_l = 12288, 14336
    aw_veff_h, aw_veff_l = 16384, 18432
    AW_END = 20480

    PB = [ctx.enter_context(nc.psum_tensor(f"pb{i}", [128, 512], F32)) for i in range(8)]

    cT = Ctr(ctx.enter_context(nc.semaphore("cT")), "cT")
    cV = Ctr(ctx.enter_context(nc.semaphore("cV")), "cV")
    cS = Ctr(ctx.enter_context(nc.semaphore("cS")), "cS")
    cG = Ctr(ctx.enter_context(nc.semaphore("cG")), "cG")
    QCT = [Ctr(ctx.enter_context(nc.semaphore(f"cQ{i}")), f"cQ{i}") for i in range(8)]
    WCT = [Ctr(ctx.enter_context(nc.semaphore(f"cW{i}")), f"cW{i}") for i in range(8)]
    ALLC = [cT, cV, cS, cG] + QCT + WCT

    P = Prog()
    P.own = {"v": cV, "s": cS, "g": cG}
    QD = DmaQ(P, "q", QCT)
    GD = DmaQ(P, "g", WCT)

    def pbm(a, n):
        return pool_b[:, a:a + n]

    def pfm(a, n):
        return pool_f[:, a:a + n]

    wdqt_hi = pbm(aw_wdqt_h, 2048); wdqt_lo = pbm(aw_wdqt_l, 2048)
    wuqt_hi = pbm(aw_wuqt_h, 2048); wuqt_lo = pbm(aw_wuqt_l, 2048)
    wdkvt_hi = pbm(aw_wdkvt_h, 2048); wdkvt_lo = pbm(aw_wdkvt_l, 2048)
    keff_hi = pbm(aw_keff_h, 2048); keff_lo = pbm(aw_keff_l, 2048)
    veff_hi = pbm(aw_veff_h, 2048); veff_lo = pbm(aw_veff_l, 2048)

    bank_deps = [[] for _ in range(8)]

    def bank_acquire(i):
        for (c, v) in bank_deps[i]:
            P.wait("t", c, v)
        bank_deps[i] = []

    def bank_consumed(i, *ctrs):
        bank_deps[i] = [(c, c.val) for c in ctrs]

    def stage_barrier():
        for e in Prog.ENGS:
            for c in ALLC:
                P.wait(e, c)

    grp_open = {}
    import traceback as _tb

    def gtrack(out, start, stop):
        name = out.tensor.name if hasattr(out, "tensor") else "?"
        if start:
            assert not grp_open.get(name), f"group restart on open {name}"
            grp_open[name] = True
        else:
            assert grp_open.get(name), f"accumulate into unopened group {name}"
        if stop:
            grp_open[name] = False

    class TEng:

        def __init__(self, E):
            self.E = E

        def matmul(self, out, lh, rh, **kw):
            return self.E.matmul(out, lh, rh, **kw)

    _orig_emit = P.emit

    def emit_t_checked(eng, f, inc=None, n=1):
        if eng == "t":
            class Rec:
                def __init__(self):
                    self.calls = []

                def matmul(self, out, lh, rh, start=None, stop=None, **kw):
                    self.calls.append((out, bool(start), bool(stop)))
                    class _I:
                        def then_inc(self, *a):
                            return self
                    return _I()

                def __getattr__(self, k):
                    def f2(*a, **kw):
                        class _I:
                            def then_inc(self, *a2):
                                return self
                        return _I()
                    return f2
            r = Rec()
            try:
                f(r)
                for (out, st, sp) in r.calls:
                    if hasattr(out, "tensor") and "pb" in str(getattr(out.tensor, "name", "")):
                        gtrack(out, st, sp)
            except AssertionError:
                raise
            except Exception:
                pass
        return _orig_emit(eng, f, inc=inc, n=n)

    P.emit = emit_t_checked

    def mm3(out, lh, ll, rh, rl, start, stop, tp=None, inc=True):
        kw = {} if tp is None else {"tile_position": tp}
        P.emit("t", lambda E: E.matmul(out, lh, rh, start=start, stop=False, **kw))
        P.emit("t", lambda E: E.matmul(out, lh, rl, start=False, stop=False, **kw))
        return P.emit("t", lambda E: E.matmul(out, ll, rh, start=False, stop=stop, **kw),
                      inc=cT if inc else None)

    def split_psum(psum_ap, hi_ap, lo_ap, tval):
        P.wait("s", cT, tval)
        sv = P.emit("s", lambda E: E.activation(hi_ap, psum_ap, AF.Copy), inc=cS)
        P.wait("v", cT, tval)
        P.wait("v", cS, sv)
        vv = P.emit("v", lambda E: E.tensor_tensor(lo_ap, psum_ap, hi_ap, OP.subtract), inc=cV)
        return sv, vv

    def r3(sb_ap, c, f):
        return sb_ap.rearrange("p (c f) -> p c f", c=c, f=f)

    def load_split(dram_re, hi_ap, lo_ap, stage_ap, c=None, f=None):
        if c is not None:
            t1 = GD.dma(r3(hi_ap, c, f), dram_re)
            t2 = GD.dma(r3(stage_ap, c, f), dram_re)
        else:
            t1 = GD.dma(hi_ap, dram_re)
            t2 = GD.dma(stage_ap, dram_re)
        P.waitt("v", t1)
        P.waitt("v", t2)
        return P.emit("v", lambda E: E.tensor_tensor(lo_ap, stage_ap, hi_ap, OP.subtract), inc=cV)

    GD.dma(c_ident16[:], ident16[:, :])
    GD.dma(c_identb[:], identb[:, :])
    GD.dma(c_tri[:], tri_incl[:, :])
    GD.dma(c_ntri[:], ntri[:, :])
    GD.dma(c_ntril[:], ntril[:, :])
    GD.dma(c_onesf[:], onesf[:, :])
    GD.dma(c_onesb[:], onesb[:, :])
    GD.dma(c_onehot[:], onehot[:, :])
    GD.dma(c_zerob[:], zerob[:, :])
    GD.dma(gam_col[:], fgam.rearrange("c p -> p c"), allow_slow_non_contiguous=True)
    GD.dma(bet_col[:], fbet.rearrange("c p -> p c"), allow_slow_non_contiguous=True)
    GD.dma(bg_col[:], fbg[:, :])
    GD.dma(b1_col.ap().rearrange("p (e c) -> p e c", e=E, c=2), fb1.rearrange("e (c p) -> p e c", p=128), allow_slow_non_contiguous=True)
    v_ = load_split(fb2[:, :], b2_hi[:], b2_lo[:], wstage[0:16, 0:C])
    P.wait("g", cV, v_)
    GD.dma(wgf.ap().rearrange("p (c f) -> p c f", c=NCH, f=E), fwg.rearrange("c p f -> p c f"))

    v_ = load_split(awdqt.rearrange("c p f -> p c f"), wdqt_hi, wdqt_lo, wstage[:, 0:NCH * J], c=NCH, f=J)
    P.wait("g", cV, v_)
    v_ = load_split(awdkvt.rearrange("c p f -> p c f"), wdkvt_hi, wdkvt_lo, wstage[:, 0:NCH * L], c=NCH, f=L)
    P.wait("g", cV, v_)

    base = AW_END
    s_wuq, s_wuq_l = base, base + NCH * J
    base += 2 * NCH * J
    s_wuk, s_wuk_l = base, base + NCH * L
    base += 2 * NCH * L
    s_wdq, s_wdq_l = base, base + 2 * C
    base += 4 * C
    s_A, s_A_l = base, base + 2 * L
    base += 4 * L
    s_wuv, s_wuv_l = base, base + NCH * L
    base += 2 * NCH * L
    s_wot, s_wot_l = base, base + NCH * C
    assert base + 2 * NCH * C <= PBCOLS

    v_ = load_split(awuq.rearrange("c p f -> p c f"), pbm(s_wuq, NCH * J), pbm(s_wuq_l, NCH * J),
                    wstage[:, 0:NCH * J], c=NCH, f=J)
    P.wait("g", cV, v_)
    v_ = load_split(awuk.rearrange("c p f -> p c f"), pbm(s_wuk, NCH * L), pbm(s_wuk_l, NCH * L),
                    wstage[:, 0:NCH * L], c=NCH, f=L)
    P.wait("g", cV, v_)
    v_ = load_split(awdq.rearrange("c p f -> p c f"), pbm(s_wdq, 2 * C), pbm(s_wdq_l, 2 * C),
                    wstage[:, 0:2 * C], c=2, f=C)
    P.wait("g", cV, v_)
    v_ = load_split(awuv.rearrange("c p f -> p c f"), pbm(s_wuv, NCH * L), pbm(s_wuv_l, NCH * L),
                    wstage[:, 0:NCH * L], c=NCH, f=L)
    P.wait("g", cV, v_)
    for qtr in range(4):
        cols = 2 * C
        v_ = load_split(awot[2 * qtr:2 * qtr + 2].rearrange("c p f -> p c f"),
                        pbm(s_wot + qtr * cols, cols), pbm(s_wot_l + qtr * cols, cols),
                        wstage[:, 0:cols], c=2, f=C)
        P.wait("g", cV, v_)

    P.wait("t", cV)
    for jc in range(2):
        bank_acquire(jc)
        for kc in range(NCH):
            mm3(PB[jc][:, 0:L],
                pbm(s_wuq + kc * J + jc * 128, 128), pbm(s_wuq_l + kc * J + jc * 128, 128),
                pbm(s_wuk + kc * L, L), pbm(s_wuk_l + kc * L, L),
                start=(kc == 0), stop=(kc == NCH - 1), inc=(kc == NCH - 1))
        sv, vv = split_psum(PB[jc][:, 0:L], pbm(s_A + jc * L, L), pbm(s_A_l + jc * L, L), cT.val)
        bank_consumed(jc, cS, cV)
    vA = cV.val

    P.wait("t", cV, vA)
    for cc in range(NCH):
        bk = 2 + (cc % 2)
        bank_acquire(bk)
        for jc in range(2):
            mm3(PB[bk][:, 0:L],
                pbm(s_wdq + jc * C + cc * 128, 128), pbm(s_wdq_l + jc * C + cc * 128, 128),
                pbm(s_A + jc * L, L), pbm(s_A_l + jc * L, L),
                start=(jc == 0), stop=(jc == 1), inc=(jc == 1))
        split_psum(PB[bk][:, 0:L], keff_hi[:, cc * L:(cc + 1) * L], keff_lo[:, cc * L:(cc + 1) * L], cT.val)
        bank_consumed(bk, cS, cV)

    for lc in range(2):
        for mb in range(2):
            bk = 4 + mb
            bank_acquire(bk)
            for kc in range(NCH):
                mm3(PB[bk][:, 0:512],
                    pbm(s_wuv + kc * L + lc * 128, 128), pbm(s_wuv_l + kc * L + lc * 128, 128),
                    pbm(s_wot + kc * C + mb * 512, 512), pbm(s_wot_l + kc * C + mb * 512, 512),
                    start=(kc == 0), stop=(kc == NCH - 1), inc=(kc == NCH - 1))
            split_psum(PB[bk][:, 0:512], veff_hi[:, lc * C + mb * 512: lc * C + (mb + 1) * 512],
                       veff_lo[:, lc * C + mb * 512: lc * C + (mb + 1) * 512], cT.val)
            bank_consumed(bk, cS, cV)

    def ln_stats(chunk_fn, twait=None, blk=0):
        A = 6 + (blk % 2)
        bank_acquire(A)
        bank_acquire(5)
        tch = {0: 0, 1: 0}
        xsqs = (xsq.ap(), xadd[:, 512:1024])
        for ch in range(NCH):
            src = chunk_fn(ch)
            xq = xsqs[ch % 2]
            if twait is not None:
                twait("s")
            P.wait("s", cT, tch[ch % 2])
            sv = P.emit("s", lambda E, src=src, xq=xq: E.activation(xq, src, AF.Square), inc=cS)
            if twait is not None:
                twait("t")
            P.wait("t", cS, sv)
            P.emit("t", lambda E, src=src, ch=ch, A=A: E.matmul(
                PB[A][0:1, 0:512], c_onesf[:, 0:1], src,
                start=(ch == 0), stop=(ch == NCH - 1)))
            P.emit("t", lambda E, xq=xq, ch=ch: E.matmul(
                PB[5][0:1, 0:512], c_onesf[:, 0:1], xq,
                start=(ch == 0), stop=(ch == NCH - 1)), inc=cT)
            tch[ch % 2] = cT.val
        tstat = cT.val
        P.wait("v", cT, tstat)
        P.emit("v", lambda E, A=A: E.tensor_scalar_mul(rowsb_m[:], PB[A][0:1, 0:512], 1.0 / C))
        P.emit("v", lambda E: E.tensor_tensor(rowsb_r[:], rowsb_m[:], rowsb_m[:], OP.mult))
        vv = P.emit("v", lambda E: E.scalar_tensor_tensor(
            rowsb_r[:], PB[5][0:1, 0:512], 1.0 / C, rowsb_r[:],
            op0=OP.mult, op1=OP.subtract), inc=cV)
        bank_consumed(A, cV)
        bank_consumed(5, cV)
        vv = P.emit("v", lambda E: E.tensor_scalar_add(rowsb_r[:], rowsb_r[:], EPS_LN), inc=cV)
        P.wait("s", cV, vv)
        sv = P.emit("s", lambda E: E.activation(rowsb_r[:], rowsb_r[:], AF.Sqrt), inc=cS)
        P.wait("v", cS, sv)
        P.emit("v", lambda E: E.reciprocal(rowsb_r[:], rowsb_r[:]))
        vv = P.emit("v", lambda E: E.tensor_tensor(rowsb_m[:], rowsb_m[:], rowsb_r[:],
                                                   OP.mult), inc=cV)
        bank_acquire(A); bank_acquire(5)
        P.wait("t", cV, vv)
        P.emit("t", lambda E, A=A: E.matmul(PB[A][:, 0:512], c_onesf[0:1, :], rowsb_r[0:1, :],
                                            start=True, stop=True))
        P.emit("t", lambda E: E.matmul(PB[5][:, 0:512], c_onesf[0:1, :], rowsb_m[0:1, :],
                                       start=True, stop=True), inc=cT)
        trep = cT.val
        P.wait("v", cT, trep)
        P.emit("v", lambda E, A=A: E.tensor_copy(rrep[:], PB[A][:, 0:512]))
        vv = P.emit("v", lambda E: E.tensor_copy(mrep[:], PB[5][:, 0:512]), inc=cV)
        bank_consumed(A, cV); bank_consumed(5, cV)
        return tstat, vv

    lnpre = [None]
    xtile_free = {"tv": (0, 0, 0)}

    def ln_prefetch(next_src):
        if next_src is None:
            return
        t_, s_, v_ = xtile_free["tv"]
        P.wait("q", cT, t_)
        P.wait("q", cS, s_)
        P.wait("q", cV, v_)
        lnpre[0] = QD.dma(xtile.ap().rearrange("p (c t) -> p c t", c=NCH, t=512),
                          next_src[:, :, 0:512].rearrange("c p t -> p c t"))

    def ln_pass(xsrc_b, sink, write_xn=None, gate=False):
        sink_t = 0
        lnbufs = (xtile.ap(), pool_f[:, 768:768 + NCH * 512])
        prev_stat = [0, 0]
        prev_aff = [0, 0]
        prev_xn_toks = []
        tgate = {}
        for blkq in range(NTB):
            t0 = blkq * 512
            a = blkq % 2
            xt = lnbufs[a]
            if blkq == 0 and lnpre[0] is not None:
                tq = lnpre[0]
                lnpre[0] = None
            else:
                P.wait("q", cT, prev_stat[a])
                P.wait("q", cV, prev_aff[a])
                tq = QD.dma(xt.rearrange("p (c t) -> p c t", c=NCH, t=512),
                            xsrc_b[:, :, t0:t0 + 512].rearrange("c p t -> p c t"))
            tstat, _ = ln_stats(lambda ch: xt[:, ch * 512:(ch + 1) * 512],
                                twait=lambda e: P.waitt(e, tq), blk=blkq)
            prev_stat[a] = tstat
            P.wait("v", cT, sink_t)
            P.wait("s", cT, sink_t)
            for tk in prev_xn_toks:
                P.waitt("v", tk)
                P.waitt("s", tk)
            for ch in range(NCH):
                a2 = ch % 2
                if gate and ch >= 2:
                    P.wait("v", cT, tgate.get(ch - 2, 0))
                P.emit("v", lambda E, ch=ch, a2=a2, xt=xt: E.tensor_tensor(
                    xnf[:, a2 * 512:(a2 + 1) * 512], xt[:, ch * 512:(ch + 1) * 512], rrep[:], OP.mult))
                vv = P.emit("v", lambda E, ch=ch, a2=a2: E.tensor_tensor(
                    xnf[:, a2 * 512:(a2 + 1) * 512], xnf[:, a2 * 512:(a2 + 1) * 512], mrep[:],
                    OP.subtract), inc=cV)
                if gate:
                    if ch == 0:
                        bank_acquire(4)
                    P.wait("t", cV, vv)
                    P.emit("t", lambda En, ch=ch: En.matmul(
                        PB[4][0:16, 0:512], wgf[:, ch * 16:(ch + 1) * 16],
                        xnf[:, (ch % 2) * 512:((ch % 2) + 1) * 512],
                        start=(ch == 0), stop=(ch == NCH - 1)),
                        inc=cT)
                    tgate[ch] = cT.val
                P.wait("s", cV, vv)
                sv = P.emit("s", lambda E, ch=ch, a2=a2: E.activation(
                    xn_hi[:, ch * 512:(ch + 1) * 512], xnf[:, a2 * 512:(a2 + 1) * 512], AF.Copy), inc=cS)
                P.wait("v", cS, sv)
                P.emit("v", lambda E, ch=ch, a2=a2: E.tensor_tensor(
                    xn_lo[:, ch * 512:(ch + 1) * 512], xnf[:, a2 * 512:(a2 + 1) * 512],
                    xn_hi[:, ch * 512:(ch + 1) * 512], OP.subtract), inc=cV)
            vsplit = cV.val
            prev_aff[a] = vsplit
            if write_xn is not None:
                hi_d, lo_d = write_xn
                P.wait("q", cV, vsplit)
                tk1 = QD.dma(hi_d[:, :, t0:t0 + 512].rearrange("c p t -> p c t"),
                             xn_hi.ap().rearrange("p (c t) -> p c t", c=NCH, t=512))
                tk2 = QD.dma(lo_d[:, :, t0:t0 + 512].rearrange("c p t -> p c t"),
                             xn_lo.ap().rearrange("p (c t) -> p c t", c=NCH, t=512))
                prev_xn_toks = [tk1, tk2]
            sink_t = sink(blkq, vsplit)
            if blkq == 2:
                xtile_free["tv"] = (cT.val, cS.val, cV.val)
        return sink_t

    xadd_st = [None, None]

    def attn_a(b, xsrc, xdst, nxt=None):
        A0 = AW_END
        ar_ckvT_h, ar_ckvT_l = A0, A0 + 2 * T
        ar_ckv_h, ar_ckv_l = A0 + 4 * T, A0 + 6 * T
        ar_q1_h, ar_q1_l = A0 + 8 * T, A0 + 10 * T
        ar_qlT_h, ar_qlT_l = A0 + 12 * T, A0 + 14 * T
        base_a = A0 + 16 * T
        ar_wuq_h, ar_wuq_l = base_a, base_a + 256
        ar_M_h, ar_M_l = base_a + 512, base_a + 1536
        ar_pt_h, ar_pt_l = base_a + 2560, base_a + 3584
        ar_ctx_h, ar_ctx_l = base_a + 4608, base_a + 5632
        assert base_a + 6656 <= 59904

        def pth_a(pb_):
            return ar_pt_h + pb_ * 512 if pb_ < 2 else 59904

        def ptl_a(pb_):
            return ar_pt_l + pb_ * 512 if pb_ < 2 else 60416
        pf_ybuf = 0
        pf_ctxf = 2048
        pf_rsum = 3072
        pf_rra = 3584
        pf_wuq = 4096

        def proj_sink(blkq, vsplit):
            t0 = blkq * 512
            P.wait("t", cV, vsplit)
            for lc in range(2):
                bk = lc
                bank_acquire(bk)
                for kc in range(NCH):
                    mm3(PB[bk][:, 0:512],
                        wdkvt_hi[:, kc * L + lc * 128: kc * L + (lc + 1) * 128],
                        wdkvt_lo[:, kc * L + lc * 128: kc * L + (lc + 1) * 128],
                        xn_hi[:, kc * 512:(kc + 1) * 512], xn_lo[:, kc * 512:(kc + 1) * 512],
                        start=(kc == 0), stop=(kc == NCH - 1), inc=(kc == NCH - 1))
                split_psum(PB[bk][:, 0:512], pbm(ar_ckvT_h + lc * T + t0, 512),
                           pbm(ar_ckvT_l + lc * T + t0, 512), cT.val)
                bank_consumed(bk, cS, cV)
            for jc in range(2):
                bk = 2 + jc
                bank_acquire(bk)
                for kc in range(NCH):
                    mm3(PB[bk][:, 0:512],
                        wdqt_hi[:, kc * J + jc * 128: kc * J + (jc + 1) * 128],
                        wdqt_lo[:, kc * J + jc * 128: kc * J + (jc + 1) * 128],
                        xn_hi[:, kc * 512:(kc + 1) * 512], xn_lo[:, kc * 512:(kc + 1) * 512],
                        start=(kc == 0), stop=(kc == NCH - 1), inc=(kc == NCH - 1))
                split_psum(PB[bk][:, 0:512], pbm(ar_q1_h + jc * T + t0, 512),
                           pbm(ar_q1_l + jc * T + t0, 512), cT.val)
                bank_consumed(bk, cS, cV)
            return cT.val

        ln_pass(xsrc, proj_sink)
        ln_prefetch(nxt)

        for lc in range(2):
            for kt in range(NKT):
                bk = 4 + (kt % 2)
                bank_acquire(bk)
                bkb = PB[bk].ap().bitcast(BF16)
                P.emit("t", lambda E, lc=lc, kt=kt, bkb=bkb: E.matmul(
                    bkb[:, 0:128], pbm(ar_ckvT_h + lc * T + kt * 128, 128), c_identb[:],
                    is_transpose=True, start=True, stop=False))
                P.emit("t", lambda E, lc=lc, kt=kt, bkb=bkb: E.matmul(
                    bkb[:, 128:256], pbm(ar_ckvT_l + lc * T + kt * 128, 128), c_identb[:],
                    is_transpose=True, start=False, stop=True), inc=cT)
                tv = cT.val
                P.wait("v", cT, tv)
                P.emit("v", lambda E, lc=lc, kt=kt, bkb=bkb: E.tensor_copy(
                    pbm(ar_ckv_h + kt * L + lc * 128, 128), bkb[:, 0:128]))
                vv = P.emit("v", lambda E, lc=lc, kt=kt, bkb=bkb: E.tensor_copy(
                    pbm(ar_ckv_l + kt * L + lc * 128, 128), bkb[:, 128:256]), inc=cV)
                bank_consumed(bk, cV)

        tqlat_prev = 0
        pend_tail_a = [None]

        def tail_a_recip(qb):
            rb_ = 6 + (qb % 2)
            P.wait("s", cT)
            rr = pfm(pf_rsum, 512)[0:1, :]
            P.emit("s", lambda E, rr=rr, rb_=rb_: E.activation(
                rr, PB[rb_][0:1, 0:512], AF.Ln))
            sv = P.emit("s", lambda E, rr=rr: E.activation(
                rr, rr, AF.Exp, scale=-1.0), inc=cS)
            bank_consumed(rb_, cS)
            return sv

        def emit_tail_a(qb, h, half, hh, sv):
            p_ = qb % 2
            cbs = (2 + 2 * p_, 3 + 2 * p_)
            rb_ = 6 + p_
            pj_ = h // 2
            rr = pfm(pf_rsum, 512)[0:1, :]
            bank_acquire(rb_)
            P.wait("t", cS, sv)
            P.emit("t", lambda E, rr=rr, rb_=rb_: E.matmul(
                PB[rb_][:, 0:512], c_onesf[0:1, :], rr, start=True, stop=True), inc=cT)
            trr = cT.val
            P.wait("s", cT, trr)
            rra = pfm(pf_rra, 512)
            sv = P.emit("s", lambda E, rra=rra, rb_=rb_: E.activation(
                rra, PB[rb_][:, 0:512], AF.Copy), inc=cS)
            bank_consumed(rb_, cS)
            for lc in range(2):
                cf = pfm(pf_ctxf + lc * 512, 512)
                chi = pbm(ar_ctx_h + lc * 512, 512)
                clo = pbm(ar_ctx_l + lc * 512, 512)
                P.wait("v", cS, sv)
                vv2 = P.emit("v", lambda E, cf=cf, cb=cbs[lc], rra=rra: E.tensor_tensor(
                    cf, PB[cb][:, 0:512], rra, OP.mult), inc=cV)
                P.wait("s", cV, vv2)
                sv = P.emit("s", lambda E, chi=chi, cf=cf: E.activation(chi, cf, AF.Copy), inc=cS)
                P.wait("v", cS, sv)
                P.emit("v", lambda E, clo=clo, cf=cf, chi=chi: E.tensor_tensor(
                    clo, cf, chi, OP.subtract), inc=cV)
                bank_consumed(cbs[lc], cV)
            vctx = cV.val
            P.wait("t", cV, vctx)
            bank_acquire(cbs[0])
            for lc in range(2):
                mm3(PB[cbs[0]][half:half + 64, 0:512],
                    veff_hi[:, lc * C + h * 64: lc * C + (h + 1) * 64],
                    veff_lo[:, lc * C + h * 64: lc * C + (h + 1) * 64],
                    pbm(ar_ctx_h + lc * 512, 512), pbm(ar_ctx_l + lc * 512, 512),
                    start=(lc == 0), stop=(lc == 1), tp=(0, half),
                    inc=(lc == 1))
            ty = cT.val
            if hh == 0:
                P.wait("v", cT, ty)
                vv = P.emit("v", lambda E, qb=qb, cb=cbs[0]: E.tensor_copy(
                    pool_f[0:64, pf_ybuf + qb * 512: pf_ybuf + (qb + 1) * 512],
                    PB[cb][0:64, 0:512]), inc=cV)
                bank_consumed(cbs[0], cV)
            else:
                t0 = qb * 512
                a = qb % 2
                if xadd_st[a] is not None:
                    P.waitt("q", xadd_st[a])
                tq = QD.dma(xadd[:, a * 512:(a + 1) * 512],
                            xsrc[pj_, :, t0:t0 + 512])
                P.wait("v", cT, ty)
                P.waitt("v", tq)
                P.emit("v", lambda E, a=a, qb=qb: E.tensor_tensor(
                    xadd[0:64, a * 512:(a + 1) * 512], xadd[0:64, a * 512:(a + 1) * 512],
                    pool_f[0:64, pf_ybuf + qb * 512: pf_ybuf + (qb + 1) * 512], OP.add))
                vv = P.emit("v", lambda E, a=a, cb=cbs[0]: E.tensor_tensor(
                    xadd[64:128, a * 512:(a + 1) * 512], xadd[64:128, a * 512:(a + 1) * 512],
                    PB[cb][64:128, 0:512], OP.add), inc=cV)
                bank_consumed(cbs[0], cV)
                P.wait("q", cV, vv)
                xadd_st[a] = QD.dma(xdst[pj_, :, t0:t0 + 512],
                                    xadd[:, a * 512:(a + 1) * 512])

        P.wait("g", cT)
        P.wait("g", cS)
        P.wait("g", cV)
        wuq_tok = [None, None]
        wuq_vwu = [0, 0]
        wuq_tok[0] = GD.dma(pool_f[:, pf_wuq:pf_wuq + J], awuq[0][:, :])
        for pj in range(8):
            sl = pj % 2
            stg = pool_f[:, pf_wuq + sl * J:pf_wuq + (sl + 1) * J]
            P.wait("s", cT, tqlat_prev)
            P.waitt("s", wuq_tok[sl])
            svw = P.emit("s", lambda E, stg=stg: E.activation(
                pbm(ar_wuq_h, J), stg, AF.Copy), inc=cS)
            P.wait("v", cS, svw)
            vwu = P.emit("v", lambda E, stg=stg: E.tensor_tensor(
                pbm(ar_wuq_l, J), stg, pbm(ar_wuq_h, J), OP.subtract), inc=cV)
            wuq_vwu[sl] = vwu
            if pj < 7:
                nsl = (pj + 1) % 2
                P.wait("g", cV, wuq_vwu[nsl])
                wuq_tok[nsl] = GD.dma(pool_f[:, pf_wuq + nsl * J:pf_wuq + (nsl + 1) * J],
                                      awuq[pj + 1][:, :])

            for hh in range(2):
                h = 2 * pj + hh
                half = hh * 64
                tp = (half, 0)
                P.wait("t", cV, vwu)
                P.wait("t", cT, tqlat_prev)
                for jc in range(2):
                    bk = 2 + jc
                    bank_acquire(bk)
                    for lc in range(2):
                        mm3(PB[bk][:, lc * 128:(lc + 1) * 128],
                            pbm(ar_wuq_h + jc * 128, 128)[half:half + 64, :],
                            pbm(ar_wuq_l + jc * 128, 128)[half:half + 64, :],
                            keff_hi[half:half + 64, pj * L + lc * 128: pj * L + (lc + 1) * 128],
                            keff_lo[half:half + 64, pj * L + lc * 128: pj * L + (lc + 1) * 128],
                            start=(lc == 0), stop=(lc == 1), tp=tp, inc=(lc == 1))
                    split_psum(PB[bk][:, 0:256], pbm(ar_M_h + jc * 256, 256),
                               pbm(ar_M_l + jc * 256, 256), cT.val)
                    bank_consumed(bk, cS, cV)
                vM = cV.val
                P.wait("t", cV, vM)
                for lc in range(2):
                    for tb in range(NTB):
                        bk = (tb % 2)
                        bank_acquire(bk)
                        for jc in range(2):
                            mm3(PB[bk][:, 0:512],
                                pbm(ar_M_h + jc * 256 + lc * 128, 128),
                                pbm(ar_M_l + jc * 256 + lc * 128, 128),
                                pbm(ar_q1_h + jc * T + tb * 512, 512),
                                pbm(ar_q1_l + jc * T + tb * 512, 512),
                                start=(jc == 0), stop=(jc == 1), inc=(jc == 1))
                        split_psum(PB[bk][:, 0:512], pbm(ar_qlT_h + lc * T + tb * 512, 512),
                                   pbm(ar_qlT_l + lc * T + tb * 512, 512), cT.val)
                        bank_consumed(bk, cS, cV)
                tqlat_prev = cT.val
                vql = cV.val
                P.wait("t", cV, vql)

                for qb in range(NTB):
                    nkts = 4 * qb + 4

                    p_ = qb % 2
                    cbs = (2 + 2 * p_, 3 + 2 * p_)
                    rb_ = 6 + p_

                    def consume_kt(ehi, elo, kt, lo, vv, gv, cbs=cbs, rb_=rb_, nkts=nkts):
                        P.wait("t", cV, vv)
                        P.wait("t", cG, gv)
                        if kt == 0:
                            bank_acquire(rb_); bank_acquire(cbs[0]); bank_acquire(cbs[1])
                        P.emit("t", lambda E, ehi=ehi, kt=kt, rb_=rb_, lo=lo: E.matmul(
                            PB[rb_][0:1, lo:512], c_onesb[:, 0:1], ehi[:, lo:512],
                            start=(kt == 0), stop=False))
                        P.emit("t", lambda E, elo=elo, kt=kt, rb_=rb_, lo=lo,
                               e=(kt == nkts - 1): E.matmul(
                            PB[rb_][0:1, lo:512], c_onesb[:, 0:1], elo[:, lo:512],
                            start=False, stop=e))
                        for lc in range(2):
                            mm3(PB[cbs[lc]][:, lo:512],
                                pbm(ar_ckv_h + kt * L + lc * 128, 128),
                                pbm(ar_ckv_l + kt * L + lc * 128, 128),
                                ehi[:, lo:512], elo[:, lo:512],
                                start=(kt == 0), stop=(kt == nkts - 1),
                                inc=(lc == 1))

                    pends = []
                    for kt in range(nkts):
                        sbk = kt % 2
                        pbuf = kt % 3
                        dj = kt - 4 * qb
                        lo = max(dj, 0) * 128
                        w = 512 - lo
                        bank_acquire(sbk)
                        for lc in range(2):
                            mm3(PB[sbk][:, lo:512],
                                pbm(ar_ckvT_h + lc * T + kt * 128, 128),
                                pbm(ar_ckvT_l + lc * T + kt * 128, 128),
                                pbm(ar_qlT_h + lc * T + qb * 512 + lo, w),
                                pbm(ar_qlT_l + lc * T + qb * 512 + lo, w),
                                start=(lc == 0), stop=(lc == 1 and dj < 0),
                                inc=(lc == 1 and dj < 0))
                        if dj >= 0:
                            P.emit("t", lambda E, sbk=sbk, lo=lo: E.matmul(
                                PB[sbk][:, lo:lo + 128], c_identb[:], c_ntri[:],
                                start=False, stop=True), inc=cT)
                        tsc = cT.val
                        ehi = pbm(pth_a(pbuf), 512)
                        elo = pbm(ptl_a(pbuf), 512)
                        ef = xnf[:, (kt % 2) * 512:((kt % 2) + 1) * 512]
                        P.wait("s", cT, tsc)
                        if len(pends) == 2:
                            P.wait("s", cV, pends[0][4])
                        sv2 = P.emit("s", lambda E, sbk=sbk, ef=ef, lo=lo: E.activation(
                            ef[:, lo:512], PB[sbk][:, lo:512], AF.Exp, scale=0.125), inc=cS)
                        bank_consumed(sbk, cS)
                        sv3 = P.emit("s", lambda E, ef=ef, ehi=ehi, lo=lo: E.activation(
                            ehi[:, lo:512], ef[:, lo:512], AF.Copy), inc=cS)
                        P.wait("v", cS, sv3)
                        vv = P.emit("v", lambda E, ef=ef, ehi=ehi, elo=elo, lo=lo: E.tensor_tensor(
                            elo[:, lo:512], ef[:, lo:512], ehi[:, lo:512], OP.subtract), inc=cV)
                        pends.append((ehi, elo, kt, lo, vv, 0))
                        if len(pends) == 3:
                            consume_kt(*pends.pop(0))
                    for p_ in pends:
                        consume_kt(*p_)
                    if pend_tail_a[0] is not None:
                        emit_tail_a(*pend_tail_a[0])
                    pend_tail_a[0] = (qb, h, half, hh, tail_a_recip(qb))
                emit_tail_a(*pend_tail_a[0])
                pend_tail_a[0] = None
        return

    def ffn(b, xsrc, xdst, nxt=None):
        fa_h, fa_hl = 0, 16384
        fa_w1 = 32768
        fa_w2 = 40960
        fa_mask = 49152
        fa_sel = 49408
        fa_z = 49664
        fa_selT = 49920
        fa_selrow = 51968
        fa_repm16 = 52992
        assert fa_repm16 + 16 * 512 <= PBCOLS
        pfa_lg = 0
        pfa_zf = 768
        assert pfa_zf + 4096 <= PFCOLS

        def ga(a, n):
            return xadd[:, a:a + n]
        ga_lgt, ga_cum, ga_off, ga_K, ga_g1 = 0, 16, 272, 288, 304

        def w1sl(sl, hl, kc):
            base = fa_w1 + sl * 4096 + hl * 2048
            return pbm(base + kc * FP, FP)

        def w2sl(sl, hl, fc):
            base = fa_w2 + sl * 4096 + hl * 2048
            return pbm(base + fc * C, C)

        P.emit("v", lambda E: E.memset(ga(ga_off, 16)[0:1, :], 0.0))

        def gate_sink(blkq, vsplit):
            t0b = blkq * 512
            tlg = cT.val
            P.wait("s", cT, tlg)
            lg = pool_f[0:16, pfa_lg:pfa_lg + 512]
            sv = P.emit("s", lambda E, lg=lg: E.activation(lg, PB[4][0:16, 0:512], AF.Identity,
                                                           bias=bg_col[:]), inc=cS)
            bank_consumed(4, cS)
            for tb in range(4):
                tile = 4 * blkq + tb
                lgt = ga(ga_lgt, 16)
                bank_acquire(5)
                P.wait("t", cS, sv)
                P.emit("t", lambda E, lg=lg, tb=tb: E.matmul(
                    PB[5][:, 0:16], lg[:, tb * 128:(tb + 1) * 128], c_ident16[:],
                    is_transpose=True, start=True, stop=True), inc=cT)
                ttr = cT.val
                P.wait("v", cT, ttr)
                P.emit("v", lambda E, lgt=lgt: E.tensor_copy(lgt, PB[5][:, 0:16]))
                bank_consumed(5, cV)
                g1 = ga(ga_g1, 16)
                g2 = ga(ga_g1 + 16, 16)
                m1 = ga(ga_g1 + 32, 1)
                P.emit("v", lambda E, lgt=lgt, m1=m1: E.tensor_reduce(
                    m1, lgt, axis=AX.X, op=OP.max))
                P.emit("v", lambda E, lgt=lgt, g1=g1, m1=m1: E.tensor_scalar(
                    g1, lgt, m1, None, op0=OP.is_ge))
                P.emit("v", lambda E, g1=g1: E.tensor_scalar_mul(g1, g1, 1e9))
                P.emit("v", lambda E, lgt=lgt, g1=g1, g2=g2: E.tensor_tensor(
                    g2, lgt, g1, OP.subtract))
                P.emit("v", lambda E, g2=g2, m1=m1: E.tensor_reduce(
                    m1, g2, axis=AX.X, op=OP.max))
                mk = pbm(fa_mask + tile * 16, 16)
                zb = pbm(fa_z, 16)
                P.emit("v", lambda E, lgt=lgt, m1=m1, mk=mk: E.tensor_scalar(
                    mk, lgt, m1, None, op0=OP.is_ge))
                vz = P.emit("v", lambda E, mk=mk, zb=zb: E.tensor_scalar(
                    zb, mk, -1.0, 1.0, op0=OP.mult, op1=OP.add))
                bank_acquire(5)
                bank_acquire(4)
                P.wait("t", cV, vz)
                P.emit("t", lambda E, zb=zb: E.matmul(PB[5][:, 0:16], c_tri[:], zb,
                                                      start=True, stop=False))
                P.emit("t", lambda E: E.matmul(PB[5][:, 0:16], c_onesf[0:1, :],
                                               ga(ga_off, 16)[0:1, :], start=False, stop=True))
                P.emit("t", lambda E, zb=zb: E.matmul(PB[4][0:1, 0:16], c_onesb[:, 0:1], zb,
                                                      start=True, stop=True), inc=cT)
                tcs = cT.val
                P.wait("v", cT, tcs)
                P.emit("v", lambda E, tile=tile: E.tensor_copy(ga(ga_cum + tile * 16, 16),
                                                               PB[5][:, 0:16]))
                vo = P.emit("v", lambda E: E.tensor_tensor(ga(ga_off, 16)[0:1, :],
                                                           ga(ga_off, 16)[0:1, :],
                                                           PB[4][0:1, 0:16], OP.add))
                bank_consumed(5, cV)
                bank_consumed(4, cV)
                P.wait("t", cV, vo)
            return cT.val

        ln_pass(xsrc, gate_sink, write_xn=(xnh_d[b], xnl_d[b]), gate=True)

        Kr = ga(ga_K, 16)
        vK = P.emit("v", lambda E: E.tensor_scalar_add(ga(ga_g1, 16)[0:1, 0:16],
                                                       ga(ga_off, 16)[0:1, :], -1024.0))
        bank_acquire(5)
        P.wait("t", cV, vK)
        P.emit("t", lambda E: E.matmul(PB[5][:, 0:16], c_onesf[0:1, :],
                                       ga(ga_g1, 16)[0:1, 0:16], start=True, stop=True), inc=cT)
        tKr = cT.val
        P.wait("v", cT, tKr)
        P.emit("v", lambda E, Kr=Kr: E.tensor_copy(Kr, PB[5][:, 0:16]))
        bank_consumed(5, cV)
        for tile in range(NKT):
            mk = pbm(fa_mask + tile * 16, 16)
            sl = pbm(fa_sel + tile * 16, 16)
            zb = pbm(fa_z, 16)
            u = ga(ga_g1, 16)
            P.emit("v", lambda E, mk=mk, zb=zb: E.tensor_scalar(
                zb, mk, -1.0, 1.0, op0=OP.mult, op1=OP.add))
            P.emit("v", lambda E, u=u, tile=tile, Kr=Kr: E.tensor_tensor(
                u, ga(ga_cum + tile * 16, 16), Kr, OP.is_le))
            P.emit("v", lambda E, u=u, zb=zb: E.tensor_tensor(u, u, zb, OP.mult))
            vs = P.emit("v", lambda E, sl=sl, u=u, mk=mk: E.tensor_tensor(sl, u, mk, OP.add))
            bank_acquire(5)
            P.wait("t", cV, vs)
            bkb = PB[5].ap().bitcast(BF16)
            P.emit("t", lambda E, sl=sl, bkb=bkb: E.matmul(bkb[0:16, 0:128], sl, c_identb[:],
                                                           is_transpose=True, start=True,
                                                           stop=True), inc=cT)
            tsT = cT.val
            P.wait("v", cT, tsT)
            P.emit("v", lambda E, tile=tile, bkb=bkb: E.tensor_copy(
                pool_b[0:16, fa_selT + tile * 128: fa_selT + (tile + 1) * 128],
                bkb[0:16, 0:128]), inc=cV)
            bank_consumed(5, cV)

        xst = None
        zaff_pend = None
        wstage_last = [cV.val]

        def emit_zaffine(ztb):
            nonlocal xst
            zt0 = ztb * 512
            if xst is not None:
                P.waitt("q", xst)
            txl = QD.dma(xtile.ap().rearrange("p (c t) -> p c t", c=NCH, t=512),
                         xsrc[:, :, zt0:zt0 + 512].rearrange("c p t -> p c t"))
            for dc in range(8):
                zf = pfm(pfa_zf + dc * 512, 512)
                P.emit("v", lambda E, zf=zf: E.tensor_tensor(zf, zf, rrep[:], OP.mult))
                P.emit("v", lambda E, zf=zf: E.tensor_tensor(zf, zf, mrep[:], OP.subtract))
                P.emit("v", lambda E, zf=zf, dc=dc: E.tensor_scalar(
                    zf, zf, gam_col[:, dc:dc + 1], bet_col[:, dc:dc + 1],
                    op0=OP.mult, op1=OP.add))
                P.waitt("v", txl)
                P.emit("v", lambda E, zf=zf, dc=dc: E.tensor_tensor(
                    xtile[:, dc * 512:(dc + 1) * 512], xtile[:, dc * 512:(dc + 1) * 512],
                    zf, OP.add), inc=cV)
            vfin = cV.val
            P.wait("q", cV, vfin)
            xst = QD.dma(xdst[:, :, zt0:zt0 + 512].rearrange("c p t -> p c t"),
                         xtile.ap().rearrange("p (c t) -> p c t", c=NCH, t=512))
        P.wait("q", cT)
        P.wait("q", cV)
        P.wait("q", cS)
        for c in QCT:
            P.wait("q", c)
        for tb in range(NTB):
            t0b = tb * 512
            th = QD.dma(xn_hi.ap().rearrange("p (c t) -> p c t", c=NCH, t=512),
                        xnh_d[b][:, :, t0b:t0b + 512].rearrange("c p t -> p c t"))
            tl = QD.dma(xn_lo.ap().rearrange("p (c t) -> p c t", c=NCH, t=512),
                        xnl_d[b][:, :, t0b:t0b + 512].rearrange("c p t -> p c t"))
            wtoks = {}

            def load_w1(e, wprev=None):
                sl = e % 2
                if e >= 1:
                    P.wait("g", cV, wtoks[e - 1])
                elif wprev is not None:
                    P.wait("g", cV, wprev)
                t1 = GD.dma(wstage.ap().rearrange("p (c f) -> p c f", c=NCH, f=FP),
                            fw1[e].rearrange("c p f -> p c f"))
                t2 = GD.dma(r3(pbm(fa_w1 + sl * 4096, 2048), NCH, FP),
                            fw1[e].rearrange("c p f -> p c f"))
                P.waitt("v", t1)
                P.waitt("v", t2)
                vv = P.emit("v", lambda E, sl=sl: E.tensor_tensor(
                    pbm(fa_w1 + sl * 4096 + 2048, 2048), wstage[:, 0:2048],
                    pbm(fa_w1 + sl * 4096, 2048), OP.subtract), inc=cV)
                wtoks[e] = vv

            pendr = None
            for e in range(E):
                rb = 2 + (e % 2)
                bank_acquire(rb)
                P.emit("t", lambda E, e=e, rb=rb, tb=tb: E.matmul(
                    PB[rb][:, 0:512], c_onehot[:, e * 128:(e + 1) * 128],
                    pool_b[0:16, fa_selT + tb * 512: fa_selT + (tb + 1) * 512],
                    start=True, stop=True), inc=cT)
                if pendr is not None:
                    pe, ptv, prb = pendr
                    P.wait("v", cT, ptv)
                    P.emit("v", lambda E, pe=pe, prb=prb: E.tensor_copy(
                        pbm(fa_repm16 + pe * 512, 512), PB[prb][:, 0:512]), inc=cV)
                    bank_consumed(prb, cV)
                pendr = (e, cT.val, rb)
            pe, ptv, prb = pendr
            P.wait("v", cT, ptv)
            P.emit("v", lambda E, pe=pe, prb=prb: E.tensor_copy(
                pbm(fa_repm16 + pe * 512, 512), PB[prb][:, 0:512]), inc=cV)
            bank_consumed(prb, cV)

            tproj = {}
            xnf_user = {0: 0, 1: 0}
            load_w1(0, wprev=wstage_last[0])
            for e in range(E):
                if e + 1 < E:
                    if e >= 1:
                        P.wait("g", cT, tproj[e - 1])
                    load_w1(e + 1)
                sl = e % 2
                P.wait("t", cV, wtoks[e])
                P.waitt("t", th)
                P.waitt("t", tl)
                for fc in range(2):
                    bk = (e * 2 + fc) % 4
                    bank_acquire(bk)
                    for kc in range(NCH):
                        mm3(PB[bk][:, 0:512],
                            w1sl(sl, 0, kc)[:, fc * 128:(fc + 1) * 128],
                            w1sl(sl, 1, kc)[:, fc * 128:(fc + 1) * 128],
                            xn_hi[:, kc * 512:(kc + 1) * 512], xn_lo[:, kc * 512:(kc + 1) * 512],
                            start=(kc == 0), stop=(kc == NCH - 1), inc=(kc == NCH - 1))
                    tproj[e] = cT.val
                    hh = pbm(fa_h + (e * 2 + fc) * 512, 512)
                    hl = pbm(fa_hl + (e * 2 + fc) * 512, 512)
                    a = fc
                    P.wait("s", cT, tproj[e])
                    P.wait("s", cV, xnf_user[a])
                    sv2 = P.emit("s", lambda E, a=a, bk=bk, e=e, fc=fc: E.activation(
                        xnf[:, a * 512:(a + 1) * 512], PB[bk][:, 0:512], AF.Gelu,
                        bias=b1_col[:, e * 2 + fc:e * 2 + fc + 1]), inc=cS)
                    P.wait("v", cS, sv2)
                    P.emit("v", lambda E, hh=hh, a=a: E.tensor_copy(
                        hh, xnf[:, a * 512:(a + 1) * 512]))
                    xnf_user[a] = P.emit("v", lambda E, hl=hl, a=a, hh=hh: E.tensor_tensor(
                        hl, xnf[:, a * 512:(a + 1) * 512], hh, OP.subtract), inc=cV)
                    bank_consumed(bk, cS, cV)
                for fc in range(2):
                    hh = pbm(fa_h + (e * 2 + fc) * 512, 512)
                    hl = pbm(fa_hl + (e * 2 + fc) * 512, 512)
                    P.emit("v", lambda E, hh=hh, e=e: E.tensor_tensor(
                        hh, hh, pbm(fa_repm16 + e * 512, 512), OP.mult))
                    P.emit("v", lambda E, hl=hl, e=e: E.tensor_tensor(
                        hl, hl, pbm(fa_repm16 + e * 512, 512), OP.mult), inc=cV)
            vhall = cV.val

            if zaff_pend is not None:
                emit_zaffine(zaff_pend)
                zaff_pend = None

            w2toks = {}

            def load_w2(e, wprev=None):
                sl = e % 2
                if e >= 1:
                    P.wait("g", cV, w2toks[e - 1])
                elif wprev is not None:
                    P.wait("g", cV, wprev)
                t1 = GD.dma(wstage.ap().rearrange("p (c f) -> p c f", c=2, f=C),
                            fw2[e].rearrange("c p f -> p c f"))
                t2 = GD.dma(r3(pbm(fa_w2 + sl * 4096, 2048), 2, C),
                            fw2[e].rearrange("c p f -> p c f"))
                P.waitt("v", t1)
                P.waitt("v", t2)
                vv = P.emit("v", lambda E, sl=sl: E.tensor_tensor(
                    pbm(fa_w2 + sl * 4096 + 2048, 2048), wstage[:, 0:2048],
                    pbm(fa_w2 + sl * 4096, 2048), OP.subtract), inc=cV)
                w2toks[e] = vv

            tw2 = {}
            load_w2(0, wprev=wtoks[E - 1])
            load_w2(1)
            wstage_last[0] = w2toks[1]
            P.wait("t", cV, vhall)
            for dc in range(8):
                bank_acquire(dc)
            for e in range(E):
                if e + 1 < E and e >= 1:
                    P.wait("g", cT, tw2[e - 1])
                    load_w2(e + 1)
                    wstage_last[0] = w2toks[e + 1]
                sl = e % 2
                P.wait("t", cV, w2toks[e])
                for dc in range(8):
                    if e == 0:
                        P.emit("t", lambda E, dc=dc, tb=tb: E.matmul(
                            PB[dc][:, 0:512], b2_hi[:, dc * 128:(dc + 1) * 128],
                            pool_b[0:16, fa_selT + tb * 512: fa_selT + (tb + 1) * 512],
                            start=True, stop=False))
                        P.emit("t", lambda E, dc=dc, tb=tb: E.matmul(
                            PB[dc][:, 0:512], b2_lo[:, dc * 128:(dc + 1) * 128],
                            pool_b[0:16, fa_selT + tb * 512: fa_selT + (tb + 1) * 512],
                            start=False, stop=False))
                    for fc in range(2):
                        hh = pbm(fa_h + (e * 2 + fc) * 512, 512)
                        hl = pbm(fa_hl + (e * 2 + fc) * 512, 512)
                        mm3(PB[dc][:, 0:512],
                            w2sl(sl, 0, fc)[:, dc * 128:(dc + 1) * 128],
                            w2sl(sl, 1, fc)[:, dc * 128:(dc + 1) * 128],
                            hh, hl,
                            start=False, stop=(e == E - 1 and fc == 1),
                            inc=(fc == 1))
                tw2[e] = cT.val
            tout = cT.val

            for dc in (2, 3, 0, 1, 4, 5, 6, 7):
                zf = pfm(pfa_zf + dc * 512, 512)
                P.waitt("v", th)
                P.waitt("v", tl)
                P.emit("v", lambda E, zf=zf, dc=dc: E.tensor_tensor(
                    zf, xn_hi[:, dc * 512:(dc + 1) * 512], xn_lo[:, dc * 512:(dc + 1) * 512],
                    OP.add))
                P.wait("v", cT, tout)
                P.emit("v", lambda E, zf=zf, dc=dc: E.tensor_tensor(
                    zf, zf, PB[dc][:, 0:512], OP.add), inc=cV)
                bank_consumed(dc, cV)
            vz = cV.val
            ln_stats(lambda ch: pfm(pfa_zf + ch * 512, 512),
                     twait=lambda e: P.wait(e, cV, vz), blk=tb)
            zaff_pend = tb
            P.wait("q", cT, tout)
            P.wait("q", cV, vz)
        if zaff_pend is not None:
            emit_zaffine(zaff_pend)
        if nxt is not None:
            if xst is not None:
                P.waitt("q", xst)
            lnpre[0] = QD.dma(xtile.ap().rearrange("p (c t) -> p c t", c=NCH, t=512),
                              nxt[:, :, 0:512].rearrange("c p t -> p c t"))
        return

    def attn_b(b, xsrc, xdst, nxt=None):
        ba_y, ba_yl = 0, 16384
        ba_w = 32768
        ba_q, ba_ql = 38912, 40960
        ba_k, ba_kl = 43008, 45056
        ba_vT, ba_vTl = 47104, 47616
        ba_vt, ba_vtl = 48128, 50176
        ba_pt, ba_ptl = 52224, 53248

        def pth(pb_):
            return ba_pt + pb_ * 512 if pb_ < 2 else 60416

        def ptl(pb_):
            return ba_ptl + pb_ * 512 if pb_ < 2 else 60928
        pfb_rs = 0
        pfb_ef = 768
        pfb_yf = 2816
        pfb_rr = 3328
        pfb_ef3 = 3840

        def noop_sink(blkq, vsplit):
            return 0

        ln_pass(xsrc, noop_sink, write_xn=(xnh_d[b], xnl_d[b]))
        ln_prefetch(nxt)
        for c in QCT:
            P.wait("q", c)

        def wsl(base, hl, kc):
            return pbm(base + hl * 1024 + kc * 128, 128)

        def load_wqkv(pj, wprev):
            slot = ba_w if pj % 2 == 0 else 54272
            toks = []
            for wi, wd in enumerate((bwq, bwk, bwv)):
                base = slot + wi * 2048
                if wprev is not None:
                    P.wait("g", cV, wprev)
                t1 = GD.dma(wstage[:, 0:1024].rearrange("p (k f) -> p k f", k=NCH, f=128),
                            wd[pj].rearrange("(k p) f -> p k f", p=128))
                t2 = GD.dma(r3(pbm(base, 1024), NCH, 128),
                            wd[pj].rearrange("(k p) f -> p k f", p=128))
                P.waitt("v", t1)
                P.waitt("v", t2)
                wprev = P.emit("v", lambda E, base=base: E.tensor_tensor(
                    pbm(base + 1024, 1024), wstage[:, 0:1024], pbm(base, 1024), OP.subtract),
                    inc=cV)
                toks.append(wprev)
            return toks

        P.wait("g", cV)
        P.wait("g", cT)
        wvtoks = load_wqkv(0, None)
        tproj_prev = 0
        for pj in range(8):
            slot = ba_w if pj % 2 == 0 else 54272
            for tb in range(NTB):
                t0b = tb * 512
                P.wait("q", cT, tproj_prev)
                th = QD.dma(xn_hi.ap().rearrange("p (c t) -> p c t", c=NCH, t=512),
                            xnh_d[b][:, :, t0b:t0b + 512].rearrange("c p t -> p c t"))
                tl = QD.dma(xn_lo.ap().rearrange("p (c t) -> p c t", c=NCH, t=512),
                            xnl_d[b][:, :, t0b:t0b + 512].rearrange("c p t -> p c t"))
                P.waitt("t", th)
                P.waitt("t", tl)
                P.wait("t", cV, wvtoks[2])
                svv = None
                for wi, (dsth, dstl) in enumerate(((ba_q, ba_ql), (ba_k, ba_kl), (ba_vT, ba_vTl))):
                    bk = wi
                    base = slot + wi * 2048
                    bank_acquire(bk)
                    for kc in range(NCH):
                        mm3(PB[bk][:, 0:512],
                            wsl(base, 0, kc), wsl(base, 1, kc),
                            xn_hi[:, kc * 512:(kc + 1) * 512], xn_lo[:, kc * 512:(kc + 1) * 512],
                            start=(kc == 0), stop=(kc == NCH - 1), inc=(kc == NCH - 1))
                    if wi < 2:
                        svv = split_psum(PB[bk][:, 0:512], pbm(dsth + t0b, 512),
                                         pbm(dstl + t0b, 512), cT.val)
                    else:
                        svv = split_psum(PB[bk][:, 0:512], pbm(ba_vT, 512), pbm(ba_vTl, 512),
                                         cT.val)
                    bank_consumed(bk, cS, cV)
                tproj_prev = cT.val
                P.wait("t", cS, svv[0])
                P.wait("t", cV, svv[1])
                for tt in range(4):
                    kt = 4 * tb + tt
                    bk = 3 + (tt % 2)
                    bank_acquire(bk)
                    bkb = PB[bk].ap().bitcast(BF16)
                    P.emit("t", lambda E, tt=tt, bkb=bkb: E.matmul(
                        bkb[:, 0:128], pbm(ba_vT + tt * 128, 128), c_identb[:],
                        is_transpose=True, start=True, stop=False))
                    P.emit("t", lambda E, tt=tt, bkb=bkb: E.matmul(
                        bkb[:, 128:256], pbm(ba_vTl + tt * 128, 128), c_identb[:],
                        is_transpose=True, start=False, stop=True), inc=cT)
                    tv = cT.val
                    P.wait("v", cT, tv)
                    P.emit("v", lambda E, kt=kt, bkb=bkb: E.tensor_copy(
                        pbm(ba_vt + kt * 128, 128), bkb[:, 0:128]))
                    vv = P.emit("v", lambda E, kt=kt, bkb=bkb: E.tensor_copy(
                        pbm(ba_vtl + kt * 128, 128), bkb[:, 128:256]), inc=cV)
                    bank_consumed(bk, cV)
            vproj = cV.val
            if pj + 1 < 8:
                wvtoks = load_wqkv(pj + 1, wvtoks[2])
            for hh in range(2):
                half = hh * 64
                tp = (half, 0)
                P.wait("t", cV, vproj)
                gk = 0

                def win(qb, kt):
                    rel = kt - 4 * qb
                    jlo = max(rel, 0)
                    jhi = min(rel + 1, 3)
                    return jlo * 128, (jhi + 1) * 128, rel

                def consume_bkt(ehi, elo, qb, kt, first, last, vv):
                    rb = 2 + (qb % 2)
                    yb = 6 + (qb % 2)
                    lo, hi, rel = win(qb, kt)
                    P.wait("t", cV, vv)
                    if first:
                        bank_acquire(rb)
                        bank_acquire(yb)
                        P.emit("t", lambda E, rb=rb, ehi=ehi: E.matmul(
                            PB[rb][0:1, 0:512], c_zerob[:, 0:1], ehi, start=True, stop=False))
                        P.emit("t", lambda E, yb=yb, ehi=ehi, half=half: E.matmul(
                            PB[yb][half:half + 64, 0:512], c_zerob[:, 0:64], ehi,
                            start=True, stop=False, tile_position=(0, half)))
                    P.emit("t", lambda E, ehi=ehi, rb=rb, lo=lo, hi=hi: E.matmul(
                        PB[rb][0:1, lo:hi], c_onesb[:, 0:1], ehi[:, lo:hi],
                        start=False, stop=False))
                    P.emit("t", lambda E, elo=elo, rb=rb, lo=lo, hi=hi, last=last: E.matmul(
                        PB[rb][0:1, lo:hi], c_onesb[:, 0:1], elo[:, lo:hi],
                        start=False, stop=last))
                    mm3(PB[yb][half:half + 64, lo:hi],
                        pbm(ba_vt + kt * 128, 128)[:, half:half + 64],
                        pbm(ba_vtl + kt * 128, 128)[:, half:half + 64],
                        ehi[:, lo:hi], elo[:, lo:hi],
                        start=False, stop=last, tp=(0, half), inc=last)

                def emit_tail(qb, sv):
                    yb = 6 + (qb % 2)
                    rs = pool_f[0:1, pfb_rs:pfb_rs + 512]
                    bank_acquire(4)
                    P.wait("t", cS, sv)
                    P.emit("t", lambda E, rs=rs, half=half: E.matmul(
                        PB[4][half:half + 64, 0:512], c_onesf[0:1, 0:64], rs,
                        start=True, stop=True, tile_position=(0, half)), inc=cT)
                    trr = cT.val
                    yh = pbm(ba_y + pj * 2048 + qb * 512, 512)[half:half + 64, :]
                    yl = pbm(ba_yl + pj * 2048 + qb * 512, 512)[half:half + 64, :]
                    yf = pool_f[half:half + 64, pfb_yf:pfb_yf + 512]
                    rrb = pool_f[half:half + 64, pfb_rr:pfb_rr + 512]
                    P.wait("s", cT, trr)
                    sv = P.emit("s", lambda E, rrb=rrb, half=half: E.activation(
                        rrb, PB[4][half:half + 64, 0:512], AF.Copy), inc=cS)
                    bank_consumed(4, cS)
                    P.wait("v", cS, sv)
                    vv = P.emit("v", lambda E, yf=yf, rrb=rrb, yb=yb, half=half: E.tensor_tensor(
                        yf, PB[yb][half:half + 64, 0:512], rrb, OP.mult), inc=cV)
                    bank_consumed(yb, cV)
                    P.wait("s", cV, vv)
                    sv = P.emit("s", lambda E, yh=yh, yf=yf: E.activation(yh, yf, AF.Copy), inc=cS)
                    P.wait("v", cS, sv)
                    P.emit("v", lambda E, yl=yl, yf=yf, yh=yh: E.tensor_tensor(
                        yl, yf, yh, OP.subtract), inc=cV)

                pends = []
                pend_tail = None
                efvv = {0: 0, 1: 0, 2: 0}

                def pop_one():
                    nonlocal pend_tail
                    ent = pends.pop(0)
                    consume_bkt(*ent)
                    if ent[5]:
                        if pend_tail is not None:
                            emit_tail(*pend_tail)
                        qb_ = ent[2]
                        rb = 2 + (qb_ % 2)
                        ty = cT.val
                        P.wait("s", cT, ty)
                        rs = pool_f[0:1, pfb_rs:pfb_rs + 512]
                        P.emit("s", lambda E, rs=rs, rb=rb: E.activation(
                            rs, PB[rb][0:1, 0:512], AF.Ln))
                        sv = P.emit("s", lambda E, rs=rs: E.activation(
                            rs, rs, AF.Exp, scale=-1.0), inc=cS)
                        bank_consumed(rb, cS)
                        pend_tail = (qb_, sv)
                for qb in range(NTB):
                    kts = [k for k in range(4 * qb - 1, 4 * qb + 4) if k >= 0]
                    for ki, kt in enumerate(kts):
                        lo, hi, rel = win(qb, kt)
                        w = hi - lo
                        sbk = gk % 2
                        pb3 = gk % 3
                        gk += 1
                        bank_acquire(sbk)
                        mm3(PB[sbk][:, lo:hi],
                            pbm(ba_k + kt * 128, 128)[half:half + 64, :],
                            pbm(ba_kl + kt * 128, 128)[half:half + 64, :],
                            pbm(ba_q + qb * 512 + lo, w)[half:half + 64, :],
                            pbm(ba_ql + qb * 512 + lo, w)[half:half + 64, :],
                            start=True, stop=False, tp=tp, inc=False)
                        mjs = [jj for jj in (rel, rel + 1) if 0 <= jj <= 3]
                        for mi, jj in enumerate(mjs):
                            msk = c_ntri if jj == rel else c_ntril
                            P.emit("t", lambda E, sbk=sbk, jj=jj, msk=msk,
                                   e=(mi == len(mjs) - 1): E.matmul(
                                PB[sbk][:, jj * 128:(jj + 1) * 128], c_identb[:], msk[:],
                                start=False, stop=e),
                                inc=cT if mi == len(mjs) - 1 else None)
                        tsc = cT.val
                        ehi = pbm(pth(pb3), 512)
                        elo = pbm(ptl(pb3), 512)
                        ef = pfm(pfb_ef + pb3 * 1024 if pb3 < 2 else pfb_ef3, 512)
                        P.wait("s", cT, tsc)
                        P.wait("s", cV, efvv[pb3])
                        sv2 = P.emit("s", lambda E, ef=ef, sbk=sbk, lo=lo, hi=hi: E.activation(
                            ef[:, lo:hi], PB[sbk][:, lo:hi], AF.Exp, scale=0.125), inc=cS)
                        bank_consumed(sbk, cS)
                        sv3 = P.emit("s", lambda E, ef=ef, ehi=ehi, lo=lo, hi=hi: E.activation(
                            ehi[:, lo:hi], ef[:, lo:hi], AF.Copy), inc=cS)
                        P.wait("v", cS, sv3)
                        vv = P.emit("v", lambda E, ef=ef, ehi=ehi, elo=elo, lo=lo, hi=hi:
                                    E.tensor_tensor(elo[:, lo:hi], ef[:, lo:hi], ehi[:, lo:hi],
                                                    OP.subtract), inc=cV)
                        efvv[pb3] = vv
                        pends.append((ehi, elo, qb, kt, ki == 0, ki == len(kts) - 1, vv))
                        if len(pends) == 3:
                            pop_one()
                while pends:
                    pop_one()
                emit_tail(*pend_tail)
        P.wait("g", cV, wvtoks[2])
        P.wait("g", cT, tproj_prev)
        wotoks = []
        for pj in range(8):
            base = ba_w + pj * 2048
            if pj >= 1:
                P.wait("g", cV, wotoks[pj - 1])
            t1 = GD.dma(wstage[:, 0:1024], bwot[pj][:, :])
            t2 = GD.dma(pbm(base, 1024), bwot[pj][:, :])
            P.waitt("v", t1)
            P.waitt("v", t2)
            vv = P.emit("v", lambda E, base=base: E.tensor_tensor(
                pbm(base + 1024, 1024), wstage[:, 0:1024], pbm(base, 1024), OP.subtract), inc=cV)
            wotoks.append(vv)
        vwo = cV.val
        P.wait("t", cV, vwo)
        xst = None
        for fc in range(NCH):
            for tb in range(NTB):
                t0b = tb * 512
                bk = (fc * NTB + tb) % 2
                bank_acquire(bk)
                for pj in range(8):
                    mm3(PB[bk][:, 0:512],
                        pbm(ba_w + pj * 2048 + fc * 128, 128),
                        pbm(ba_w + pj * 2048 + 1024 + fc * 128, 128),
                        pbm(ba_y + pj * 2048 + t0b, 512), pbm(ba_yl + pj * 2048 + t0b, 512),
                        start=(pj == 0), stop=(pj == 7), inc=(pj == 7))
                to = cT.val
                a = (fc * NTB + tb) % 2
                if xst is not None:
                    P.waitt("q", xst)
                txl = QD.dma(xadd[:, a * 512:(a + 1) * 512], xsrc[fc, :, t0b:t0b + 512])
                P.waitt("v", txl)
                P.wait("v", cT, to)
                vv = P.emit("v", lambda E, a=a, bk=bk: E.tensor_tensor(
                    xadd[:, a * 512:(a + 1) * 512], xadd[:, a * 512:(a + 1) * 512],
                    PB[bk][:, 0:512], OP.add), inc=cV)
                bank_consumed(bk, cV)
                P.wait("q", cV, vv)
                xst = QD.dma(xdst[fc, :, t0b:t0b + 512], xadd[:, a * 512:(a + 1) * 512])
        return

    srcs = [xin, xres, xres, xres]
    dsts = [xres, xres, xres, xres]
    dsts[N_SUBLAYERS - 1] = out_fm

    seq = [(attn_a, b, srcs[0][b], dsts[0][b]) for b in range(B)]
    if N_SUBLAYERS > 1:
        seq += [(ffn, b, srcs[1][b], dsts[1][b]) for b in range(B)]
    if N_SUBLAYERS > 2:
        seq += [(attn_b, b, srcs[2][b], dsts[2][b]) for b in range(B)]
    if N_SUBLAYERS > 3:
        seq += [(ffn, b, srcs[3][b], dsts[3][b]) for b in range(B)]
    for i, (fn, b, s_, d_) in enumerate(seq):
        fn(b, s_, d_, nxt=(seq[i + 1][2] if i + 1 < len(seq) else None))
        stage_barrier()

    for c in QCT + WCT:
        P.wait("q", c)
        P.wait("g", c)

    with nc.Block() as blk:
        blk.sync(lambda E: [f(E) for f in P.q["q"]])
        blk.tensor(lambda E: [f(E) for f in P.q["t"]])
        blk.vector(lambda E: [f(E) for f in P.q["v"]])
        blk.scalar(lambda E: [f(E) for f in P.q["s"]])
        blk.gpsimd(lambda E: [f(E) for f in P.q["g"]])

    ctx.close()
    return nc


def host_inputs(inputs, core):
    x = inputs["L0_attn_a_x"]
    m = {}
    xb = np.ascontiguousarray(x[2 * core:2 * core + 2].transpose(0, 2, 1))
    m["xin"] = xb.reshape(B, NCH, 128, T)
    m["awdqt"] = np.ascontiguousarray(inputs["L0_attn_a_W_dq"].T).reshape(NCH, 128, J)
    m["awdq"] = inputs["L0_attn_a_W_dq"].reshape(2, 128, C)
    m["awuqt"] = np.ascontiguousarray(inputs["L0_attn_a_W_uq"].T).reshape(2, 128, C)
    m["awuq"] = inputs["L0_attn_a_W_uq"].reshape(NCH, 128, J)
    m["awuk"] = inputs["L0_attn_a_W_uk"].reshape(NCH, 128, L)
    m["awdkvt"] = np.ascontiguousarray(inputs["L0_attn_a_W_dkv"].T).reshape(NCH, 128, L)
    m["awuv"] = inputs["L0_attn_a_W_uv"].reshape(NCH, 128, L)
    m["awot"] = np.ascontiguousarray(inputs["L0_attn_a_W_o"].T).reshape(NCH, 128, C)
    m["bwq"] = np.ascontiguousarray(inputs["L0_attn_b_Wq"].T.reshape(C, NCH, 128).transpose(1, 0, 2))
    m["bwk"] = np.ascontiguousarray(inputs["L0_attn_b_Wk"].T.reshape(C, NCH, 128).transpose(1, 0, 2))
    m["bwv"] = np.ascontiguousarray(inputs["L0_attn_b_Wv"].T.reshape(C, NCH, 128).transpose(1, 0, 2))
    m["bwot"] = np.ascontiguousarray(inputs["L0_attn_b_Wo"].T).reshape(NCH, 128, C)
    m["fwg"] = inputs["L0_ffn_wg"].reshape(NCH, 128, E)
    m["fbg"] = inputs["L0_ffn_bg"].reshape(16, 1)
    m["fw1"] = inputs["L0_ffn_w1"].reshape(E, NCH, 128, FP)
    m["fb1"] = inputs["L0_ffn_b1"]
    m["fw2"] = inputs["L0_ffn_w2"].reshape(E, 2, 128, C)
    m["fb2"] = inputs["L0_ffn_b2"]
    m["fgam"] = inputs["L0_ffn_gamma"].reshape(NCH, 128)
    m["fbet"] = inputs["L0_ffn_beta"].reshape(NCH, 128)
    m["ident16"] = np.eye(16, dtype=np.float32)
    import ml_dtypes
    m["identb"] = np.eye(128, dtype=ml_dtypes.bfloat16)
    ii = np.arange(128)
    m["tri_incl"] = (ii[None, :] >= ii[:, None]).astype(ml_dtypes.bfloat16)
    m["ntri"] = (((ii[None, :] >= ii[:, None]).astype(np.float32) - 1.0)
                 * 1e9).astype(ml_dtypes.bfloat16)
    m["ntril"] = (((ii[None, :] < ii[:, None]).astype(np.float32) - 1.0)
                  * 1e9).astype(ml_dtypes.bfloat16)
    m["onesf"] = np.ones((128, 128), dtype=np.float32)
    m["onesb"] = np.ones((128, 128), dtype=ml_dtypes.bfloat16)
    oh = np.zeros((16, 2048), dtype=ml_dtypes.bfloat16)
    for e in range(16):
        oh[e, e * 128:(e + 1) * 128] = 1
    m["onehot"] = oh
    m["zerob"] = np.zeros((128, 64), dtype=ml_dtypes.bfloat16)
    return m


_nc_cache = {}


def kernel(**inputs):
    key = N_SUBLAYERS
    if key not in _nc_cache:
        _nc_cache[key] = build_program()
    nc = _nc_cache[key]
    in_maps = [host_inputs(inputs, core) for core in range(8)]
    res = run_bass_kernel_spmd(nc, in_maps, core_ids=list(range(8)))
    outs = []
    for core in range(8):
        o = res.results[core]["out_fm"].reshape(B, C, T)
        outs.append(np.ascontiguousarray(o.transpose(0, 2, 1)))
    return np.concatenate(outs, axis=0)


# version 22
# speedup vs baseline: 1.0266x; 1.0099x over previous
import numpy as np
from contextlib import ExitStack
import concourse.bass as bass
import concourse.mybir as mybir
from concourse.bass_utils import run_bass_kernel_spmd

F32 = mybir.dt.float32
BF16 = mybir.dt.bfloat16
AF = mybir.ActivationFunctionType
OP = mybir.AluOpType
AX = mybir.AxisListType

N_SUBLAYERS = 4

B = 2
T = 2048
C = 1024
NCH = 8
L = 256
J = 256
NH = 16
HS = 64
E = 16
FP = 256
NTB = 4
NKT = 16
EPS_LN = 1e-5


class Ctr:
    def __init__(self, sem, name):
        self.sem = sem
        self.name = name
        self.val = 0


class Prog:
    ENGS = ("t", "v", "s", "g", "q")

    def __init__(self):
        self.q = {e: [] for e in self.ENGS}
        self.waited = {}
        self.own = {}

    def emit(self, eng, f, inc=None, n=1):
        own = self.own.get(eng)
        if own is not None:
            self.wait(eng, own)
            if inc is None:
                inc = own
        if inc is not None:
            inc.val += n
            sem, amount = inc.sem, n

            def g(En):
                f(En).then_inc(sem, amount)
        else:
            def g(En):
                f(En)
        self.q[eng].append(g)
        return inc.val if inc is not None else None

    def wait(self, eng, ctr, v=None):
        if v is None:
            v = ctr.val
        assert v <= ctr.val, f"wait {ctr.name}>={v} but only {ctr.val} emitted"
        if v <= 0:
            return
        k = (eng, ctr.name)
        if self.waited.get(k, 0) >= v:
            return
        self.waited[k] = v
        sem = ctr.sem
        self.q[eng].append(lambda En: En.wait_ge(sem, v))

    def waitt(self, eng, tok):
        ctr, v = tok
        self.wait(eng, ctr, v)


class DmaQ:

    def __init__(self, P, eng, ctrs):
        self.P, self.eng, self.ctrs = P, eng, ctrs
        self.i = 0

    def dma(self, out, in_, **kw):
        c = self.ctrs[self.i % len(self.ctrs)]
        self.i += 1
        self.P.wait(self.eng, c)
        c.val += 16
        sem, v = c.sem, c.val

        def g(En):
            En.dma_start(out=out, in_=in_, **kw).then_inc(sem, 16)
        self.P.q[self.eng].append(g)
        return (c, v)


def build_program():
    nc = bass.Bass("TRN2", target_bir_lowering=False)

    xin = nc.dram_tensor("xin", [B, NCH, 128, T], F32, kind="ExternalInput").ap()
    out_fm = nc.dram_tensor("out_fm", [B, NCH, 128, T], F32, kind="ExternalOutput").ap()

    awdqt = nc.dram_tensor("awdqt", [NCH, 128, J], F32, kind="ExternalInput").ap()
    awdq = nc.dram_tensor("awdq", [2, 128, C], F32, kind="ExternalInput").ap()
    awuqt = nc.dram_tensor("awuqt", [2, 128, C], F32, kind="ExternalInput").ap()
    awuq = nc.dram_tensor("awuq", [NCH, 128, J], F32, kind="ExternalInput").ap()
    awuk = nc.dram_tensor("awuk", [NCH, 128, L], F32, kind="ExternalInput").ap()
    awdkvt = nc.dram_tensor("awdkvt", [NCH, 128, L], F32, kind="ExternalInput").ap()
    awuv = nc.dram_tensor("awuv", [NCH, 128, L], F32, kind="ExternalInput").ap()
    awot = nc.dram_tensor("awot", [NCH, 128, C], F32, kind="ExternalInput").ap()

    bwq = nc.dram_tensor("bwq", [NCH, C, 128], F32, kind="ExternalInput").ap()
    bwk = nc.dram_tensor("bwk", [NCH, C, 128], F32, kind="ExternalInput").ap()
    bwv = nc.dram_tensor("bwv", [NCH, C, 128], F32, kind="ExternalInput").ap()
    bwot = nc.dram_tensor("bwot", [NCH, 128, C], F32, kind="ExternalInput").ap()

    fwg = nc.dram_tensor("fwg", [NCH, 128, E], F32, kind="ExternalInput").ap()
    fbg = nc.dram_tensor("fbg", [16, 1], F32, kind="ExternalInput").ap()
    fw1 = nc.dram_tensor("fw1", [E, NCH, 128, FP], F32, kind="ExternalInput").ap()
    fb1 = nc.dram_tensor("fb1", [E, FP], F32, kind="ExternalInput").ap()
    fw2 = nc.dram_tensor("fw2", [E, 2, 128, C], F32, kind="ExternalInput").ap()
    fb2 = nc.dram_tensor("fb2", [E, C], F32, kind="ExternalInput").ap()
    fgam = nc.dram_tensor("fgam", [NCH, 128], F32, kind="ExternalInput").ap()
    fbet = nc.dram_tensor("fbet", [NCH, 128], F32, kind="ExternalInput").ap()

    ident16 = nc.dram_tensor("ident16", [16, 16], F32, kind="ExternalInput").ap()
    identb = nc.dram_tensor("identb", [128, 128], BF16, kind="ExternalInput").ap()
    tri_incl = nc.dram_tensor("tri_incl", [128, 128], BF16, kind="ExternalInput").ap()
    ntri = nc.dram_tensor("ntri", [128, 128], BF16, kind="ExternalInput").ap()
    ntril = nc.dram_tensor("ntril", [128, 128], BF16, kind="ExternalInput").ap()
    onesf = nc.dram_tensor("onesf", [128, 128], F32, kind="ExternalInput").ap()
    onesb = nc.dram_tensor("onesb", [128, 128], BF16, kind="ExternalInput").ap()
    onehot = nc.dram_tensor("onehot", [16, 2048], BF16, kind="ExternalInput").ap()
    zerob = nc.dram_tensor("zerob", [128, 64], BF16, kind="ExternalInput").ap()

    xres = nc.dram_tensor("xres", [B, NCH, 128, T], F32).ap()
    xnh_d = nc.dram_tensor("xnh_d", [B, NCH, 128, T], BF16).ap()
    xnl_d = nc.dram_tensor("xnl_d", [B, NCH, 128, T], BF16).ap()

    ctx = ExitStack()

    def sbuf(name, shape, dt=F32):
        return ctx.enter_context(nc.sbuf_tensor(name, shape, dt))

    c_ident16 = sbuf("c_ident16", [16, 16])
    c_identb = sbuf("c_identb", [128, 128], BF16)
    c_tri = sbuf("c_tri", [128, 128], BF16)
    c_ntri = sbuf("c_ntri", [128, 128], BF16)
    c_ntril = sbuf("c_ntril", [128, 128], BF16)
    c_onesf = sbuf("c_onesf", [128, 128])
    c_onesb = sbuf("c_onesb", [128, 128], BF16)
    c_onehot = sbuf("c_onehot", [16, 2048], BF16)
    c_zerob = sbuf("c_zerob", [128, 64], BF16)
    gam_col = sbuf("gam_col", [128, NCH])
    bet_col = sbuf("bet_col", [128, NCH])
    bg_col = sbuf("bg_col", [16, 1])
    b1_col = sbuf("b1_col", [128, 2 * E])
    b2_hi = sbuf("b2_hi", [16, C], BF16)
    b2_lo = sbuf("b2_lo", [16, C], BF16)
    wgf = sbuf("wgf", [128, NCH * E])

    wstage = sbuf("wstage", [128, 2 * C])

    xtile = sbuf("xtile", [128, NCH * 512])
    xsq = sbuf("xsq", [128, 512])
    xnf = sbuf("xnf", [128, 2 * 512])
    xn_hi = sbuf("xn_hi", [128, NCH * 512], BF16)
    xn_lo = sbuf("xn_lo", [128, NCH * 512], BF16)
    rrep = sbuf("rrep", [128, 512])
    mrep = sbuf("mrep", [128, 512])
    rowsb_r = sbuf("rowsb_r", [1, 512])
    rowsb_m = sbuf("rowsb_m", [1, 512])
    xadd = sbuf("xadd", [128, 2 * 512])

    PBCOLS = 61440
    PFCOLS = 4864
    pool_b = sbuf("pool_b", [128, PBCOLS], BF16)
    pool_f = sbuf("pool_f", [128, PFCOLS])

    AW = 0
    aw_wdqt_h, aw_wdqt_l = 0, 2048
    aw_wuqt_h, aw_wuqt_l = 4096, 6144
    aw_wdkvt_h, aw_wdkvt_l = 8192, 10240
    aw_keff_h, aw_keff_l = 12288, 14336
    aw_veff_h, aw_veff_l = 16384, 18432
    AW_END = 20480

    PB = [ctx.enter_context(nc.psum_tensor(f"pb{i}", [128, 512], F32)) for i in range(8)]

    cT = Ctr(ctx.enter_context(nc.semaphore("cT")), "cT")
    cV = Ctr(ctx.enter_context(nc.semaphore("cV")), "cV")
    cS = Ctr(ctx.enter_context(nc.semaphore("cS")), "cS")
    cG = Ctr(ctx.enter_context(nc.semaphore("cG")), "cG")
    QCT = [Ctr(ctx.enter_context(nc.semaphore(f"cQ{i}")), f"cQ{i}") for i in range(8)]
    WCT = [Ctr(ctx.enter_context(nc.semaphore(f"cW{i}")), f"cW{i}") for i in range(8)]
    ALLC = [cT, cV, cS, cG] + QCT + WCT

    P = Prog()
    P.own = {"v": cV, "s": cS, "g": cG}
    QD = DmaQ(P, "q", QCT)
    GD = DmaQ(P, "g", WCT)

    def pbm(a, n):
        return pool_b[:, a:a + n]

    def pfm(a, n):
        return pool_f[:, a:a + n]

    wdqt_hi = pbm(aw_wdqt_h, 2048); wdqt_lo = pbm(aw_wdqt_l, 2048)
    wuqt_hi = pbm(aw_wuqt_h, 2048); wuqt_lo = pbm(aw_wuqt_l, 2048)
    wdkvt_hi = pbm(aw_wdkvt_h, 2048); wdkvt_lo = pbm(aw_wdkvt_l, 2048)
    keff_hi = pbm(aw_keff_h, 2048); keff_lo = pbm(aw_keff_l, 2048)
    veff_hi = pbm(aw_veff_h, 2048); veff_lo = pbm(aw_veff_l, 2048)

    bank_deps = [[] for _ in range(8)]

    def bank_acquire(i):
        for (c, v) in bank_deps[i]:
            P.wait("t", c, v)
        bank_deps[i] = []

    def bank_consumed(i, *ctrs):
        bank_deps[i] = [(c, c.val) for c in ctrs]

    def stage_barrier():
        for e in Prog.ENGS:
            for c in ALLC:
                P.wait(e, c)

    grp_open = {}
    import traceback as _tb

    def gtrack(out, start, stop):
        name = out.tensor.name if hasattr(out, "tensor") else "?"
        if start:
            assert not grp_open.get(name), f"group restart on open {name}"
            grp_open[name] = True
        else:
            assert grp_open.get(name), f"accumulate into unopened group {name}"
        if stop:
            grp_open[name] = False

    class TEng:

        def __init__(self, E):
            self.E = E

        def matmul(self, out, lh, rh, **kw):
            return self.E.matmul(out, lh, rh, **kw)

    _orig_emit = P.emit

    def emit_t_checked(eng, f, inc=None, n=1):
        if eng == "t":
            class Rec:
                def __init__(self):
                    self.calls = []

                def matmul(self, out, lh, rh, start=None, stop=None, **kw):
                    self.calls.append((out, bool(start), bool(stop)))
                    class _I:
                        def then_inc(self, *a):
                            return self
                    return _I()

                def __getattr__(self, k):
                    def f2(*a, **kw):
                        class _I:
                            def then_inc(self, *a2):
                                return self
                        return _I()
                    return f2
            r = Rec()
            try:
                f(r)
                for (out, st, sp) in r.calls:
                    if hasattr(out, "tensor") and "pb" in str(getattr(out.tensor, "name", "")):
                        gtrack(out, st, sp)
            except AssertionError:
                raise
            except Exception:
                pass
        return _orig_emit(eng, f, inc=inc, n=n)

    P.emit = emit_t_checked

    def mm3(out, lh, ll, rh, rl, start, stop, tp=None, inc=True):
        kw = {} if tp is None else {"tile_position": tp}
        P.emit("t", lambda E: E.matmul(out, lh, rh, start=start, stop=False, **kw))
        P.emit("t", lambda E: E.matmul(out, lh, rl, start=False, stop=False, **kw))
        return P.emit("t", lambda E: E.matmul(out, ll, rh, start=False, stop=stop, **kw),
                      inc=cT if inc else None)

    def split_psum(psum_ap, hi_ap, lo_ap, tval):
        P.wait("s", cT, tval)
        sv = P.emit("s", lambda E: E.activation(hi_ap, psum_ap, AF.Copy), inc=cS)
        P.wait("v", cT, tval)
        P.wait("v", cS, sv)
        vv = P.emit("v", lambda E: E.tensor_tensor(lo_ap, psum_ap, hi_ap, OP.subtract), inc=cV)
        return sv, vv

    def r3(sb_ap, c, f):
        return sb_ap.rearrange("p (c f) -> p c f", c=c, f=f)

    def load_split(dram_re, hi_ap, lo_ap, stage_ap, c=None, f=None):
        if c is not None:
            t1 = GD.dma(r3(hi_ap, c, f), dram_re)
            t2 = GD.dma(r3(stage_ap, c, f), dram_re)
        else:
            t1 = GD.dma(hi_ap, dram_re)
            t2 = GD.dma(stage_ap, dram_re)
        P.waitt("v", t1)
        P.waitt("v", t2)
        return P.emit("v", lambda E: E.tensor_tensor(lo_ap, stage_ap, hi_ap, OP.subtract), inc=cV)

    GD.dma(c_ident16[:], ident16[:, :])
    GD.dma(c_identb[:], identb[:, :])
    GD.dma(c_tri[:], tri_incl[:, :])
    GD.dma(c_ntri[:], ntri[:, :])
    GD.dma(c_ntril[:], ntril[:, :])
    GD.dma(c_onesf[:], onesf[:, :])
    GD.dma(c_onesb[:], onesb[:, :])
    GD.dma(c_onehot[:], onehot[:, :])
    GD.dma(c_zerob[:], zerob[:, :])
    GD.dma(gam_col[:], fgam.rearrange("c p -> p c"), allow_slow_non_contiguous=True)
    GD.dma(bet_col[:], fbet.rearrange("c p -> p c"), allow_slow_non_contiguous=True)
    GD.dma(bg_col[:], fbg[:, :])
    GD.dma(b1_col.ap().rearrange("p (e c) -> p e c", e=E, c=2), fb1.rearrange("e (c p) -> p e c", p=128), allow_slow_non_contiguous=True)
    v_ = load_split(fb2[:, :], b2_hi[:], b2_lo[:], wstage[0:16, 0:C])
    P.wait("g", cV, v_)
    GD.dma(wgf.ap().rearrange("p (c f) -> p c f", c=NCH, f=E), fwg.rearrange("c p f -> p c f"))

    v_ = load_split(awdqt.rearrange("c p f -> p c f"), wdqt_hi, wdqt_lo, wstage[:, 0:NCH * J], c=NCH, f=J)
    P.wait("g", cV, v_)
    v_ = load_split(awdkvt.rearrange("c p f -> p c f"), wdkvt_hi, wdkvt_lo, wstage[:, 0:NCH * L], c=NCH, f=L)
    P.wait("g", cV, v_)

    base = AW_END
    s_wuq, s_wuq_l = base, base + NCH * J
    base += 2 * NCH * J
    s_wuk, s_wuk_l = base, base + NCH * L
    base += 2 * NCH * L
    s_wdq, s_wdq_l = base, base + 2 * C
    base += 4 * C
    s_A, s_A_l = base, base + 2 * L
    base += 4 * L
    s_wuv, s_wuv_l = base, base + NCH * L
    base += 2 * NCH * L
    s_wot, s_wot_l = base, base + NCH * C
    assert base + 2 * NCH * C <= PBCOLS

    v_ = load_split(awuq.rearrange("c p f -> p c f"), pbm(s_wuq, NCH * J), pbm(s_wuq_l, NCH * J),
                    wstage[:, 0:NCH * J], c=NCH, f=J)
    P.wait("g", cV, v_)
    v_ = load_split(awuk.rearrange("c p f -> p c f"), pbm(s_wuk, NCH * L), pbm(s_wuk_l, NCH * L),
                    wstage[:, 0:NCH * L], c=NCH, f=L)
    P.wait("g", cV, v_)
    v_ = load_split(awdq.rearrange("c p f -> p c f"), pbm(s_wdq, 2 * C), pbm(s_wdq_l, 2 * C),
                    wstage[:, 0:2 * C], c=2, f=C)
    P.wait("g", cV, v_)
    v_ = load_split(awuv.rearrange("c p f -> p c f"), pbm(s_wuv, NCH * L), pbm(s_wuv_l, NCH * L),
                    wstage[:, 0:NCH * L], c=NCH, f=L)
    P.wait("g", cV, v_)
    for qtr in range(4):
        cols = 2 * C
        v_ = load_split(awot[2 * qtr:2 * qtr + 2].rearrange("c p f -> p c f"),
                        pbm(s_wot + qtr * cols, cols), pbm(s_wot_l + qtr * cols, cols),
                        wstage[:, 0:cols], c=2, f=C)
        P.wait("g", cV, v_)

    P.wait("t", cV)
    for jc in range(2):
        bank_acquire(jc)
        for kc in range(NCH):
            mm3(PB[jc][:, 0:L],
                pbm(s_wuq + kc * J + jc * 128, 128), pbm(s_wuq_l + kc * J + jc * 128, 128),
                pbm(s_wuk + kc * L, L), pbm(s_wuk_l + kc * L, L),
                start=(kc == 0), stop=(kc == NCH - 1), inc=(kc == NCH - 1))
        sv, vv = split_psum(PB[jc][:, 0:L], pbm(s_A + jc * L, L), pbm(s_A_l + jc * L, L), cT.val)
        bank_consumed(jc, cS, cV)
    vA = cV.val

    P.wait("t", cV, vA)
    for cc in range(NCH):
        bk = 2 + (cc % 2)
        bank_acquire(bk)
        for jc in range(2):
            mm3(PB[bk][:, 0:L],
                pbm(s_wdq + jc * C + cc * 128, 128), pbm(s_wdq_l + jc * C + cc * 128, 128),
                pbm(s_A + jc * L, L), pbm(s_A_l + jc * L, L),
                start=(jc == 0), stop=(jc == 1), inc=(jc == 1))
        split_psum(PB[bk][:, 0:L], keff_hi[:, cc * L:(cc + 1) * L], keff_lo[:, cc * L:(cc + 1) * L], cT.val)
        bank_consumed(bk, cS, cV)

    for lc in range(2):
        for mb in range(2):
            bk = 4 + mb
            bank_acquire(bk)
            for kc in range(NCH):
                mm3(PB[bk][:, 0:512],
                    pbm(s_wuv + kc * L + lc * 128, 128), pbm(s_wuv_l + kc * L + lc * 128, 128),
                    pbm(s_wot + kc * C + mb * 512, 512), pbm(s_wot_l + kc * C + mb * 512, 512),
                    start=(kc == 0), stop=(kc == NCH - 1), inc=(kc == NCH - 1))
            split_psum(PB[bk][:, 0:512], veff_hi[:, lc * C + mb * 512: lc * C + (mb + 1) * 512],
                       veff_lo[:, lc * C + mb * 512: lc * C + (mb + 1) * 512], cT.val)
            bank_consumed(bk, cS, cV)

    def ln_stats(chunk_fn, twait=None, blk=0):
        A = 6 + (blk % 2)
        bank_acquire(A)
        bank_acquire(5)
        tch = {0: 0, 1: 0}
        xsqs = (xsq.ap(), xadd[:, 512:1024])
        for ch in range(NCH):
            src = chunk_fn(ch)
            xq = xsqs[ch % 2]
            if twait is not None:
                twait("s")
            P.wait("s", cT, tch[ch % 2])
            sv = P.emit("s", lambda E, src=src, xq=xq: E.activation(xq, src, AF.Square), inc=cS)
            if twait is not None:
                twait("t")
            P.wait("t", cS, sv)
            P.emit("t", lambda E, src=src, ch=ch, A=A: E.matmul(
                PB[A][0:1, 0:512], c_onesf[:, 0:1], src,
                start=(ch == 0), stop=(ch == NCH - 1)))
            P.emit("t", lambda E, xq=xq, ch=ch: E.matmul(
                PB[5][0:1, 0:512], c_onesf[:, 0:1], xq,
                start=(ch == 0), stop=(ch == NCH - 1)), inc=cT)
            tch[ch % 2] = cT.val
        tstat = cT.val
        P.wait("v", cT, tstat)
        P.emit("v", lambda E, A=A: E.tensor_scalar_mul(rowsb_m[:], PB[A][0:1, 0:512], 1.0 / C))
        P.emit("v", lambda E: E.tensor_tensor(rowsb_r[:], rowsb_m[:], rowsb_m[:], OP.mult))
        vv = P.emit("v", lambda E: E.scalar_tensor_tensor(
            rowsb_r[:], PB[5][0:1, 0:512], 1.0 / C, rowsb_r[:],
            op0=OP.mult, op1=OP.subtract), inc=cV)
        bank_consumed(A, cV)
        bank_consumed(5, cV)
        vv = P.emit("v", lambda E: E.tensor_scalar_add(rowsb_r[:], rowsb_r[:], EPS_LN), inc=cV)
        P.wait("s", cV, vv)
        sv = P.emit("s", lambda E: E.activation(rowsb_r[:], rowsb_r[:], AF.Sqrt), inc=cS)
        P.wait("v", cS, sv)
        P.emit("v", lambda E: E.reciprocal(rowsb_r[:], rowsb_r[:]))
        vv = P.emit("v", lambda E: E.tensor_tensor(rowsb_m[:], rowsb_m[:], rowsb_r[:],
                                                   OP.mult), inc=cV)
        bank_acquire(A); bank_acquire(5)
        P.wait("t", cV, vv)
        P.emit("t", lambda E, A=A: E.matmul(PB[A][:, 0:512], c_onesf[0:1, :], rowsb_r[0:1, :],
                                            start=True, stop=True))
        P.emit("t", lambda E: E.matmul(PB[5][:, 0:512], c_onesf[0:1, :], rowsb_m[0:1, :],
                                       start=True, stop=True), inc=cT)
        trep = cT.val
        P.wait("v", cT, trep)
        P.emit("v", lambda E, A=A: E.tensor_copy(rrep[:], PB[A][:, 0:512]))
        vv = P.emit("v", lambda E: E.tensor_copy(mrep[:], PB[5][:, 0:512]), inc=cV)
        bank_consumed(A, cV); bank_consumed(5, cV)
        return tstat, vv

    lnpre = [None]
    xtile_free = {"tv": (0, 0, 0)}

    def ln_prefetch(next_src):
        if next_src is None:
            return
        t_, s_, v_ = xtile_free["tv"]
        P.wait("q", cT, t_)
        P.wait("q", cS, s_)
        P.wait("q", cV, v_)
        lnpre[0] = QD.dma(xtile.ap().rearrange("p (c t) -> p c t", c=NCH, t=512),
                          next_src[:, :, 0:512].rearrange("c p t -> p c t"))

    def ln_pass(xsrc_b, sink, write_xn=None, gate=False):
        sink_t = 0
        lnbufs = (xtile.ap(), pool_f[:, 768:768 + NCH * 512])
        prev_stat = [0, 0]
        prev_aff = [0, 0]
        prev_xn_toks = []
        tgate = {}
        for blkq in range(NTB):
            t0 = blkq * 512
            a = blkq % 2
            xt = lnbufs[a]
            if blkq == 0 and lnpre[0] is not None:
                tq = lnpre[0]
                lnpre[0] = None
            else:
                P.wait("q", cT, prev_stat[a])
                P.wait("q", cV, prev_aff[a])
                tq = QD.dma(xt.rearrange("p (c t) -> p c t", c=NCH, t=512),
                            xsrc_b[:, :, t0:t0 + 512].rearrange("c p t -> p c t"))
            tstat, _ = ln_stats(lambda ch: xt[:, ch * 512:(ch + 1) * 512],
                                twait=lambda e: P.waitt(e, tq), blk=blkq)
            prev_stat[a] = tstat
            P.wait("v", cT, sink_t)
            P.wait("s", cT, sink_t)
            for tk in prev_xn_toks:
                P.waitt("v", tk)
                P.waitt("s", tk)
            for ch in range(NCH):
                a2 = ch % 2
                if gate and ch >= 2:
                    P.wait("v", cT, tgate.get(ch - 2, 0))
                P.emit("v", lambda E, ch=ch, a2=a2, xt=xt: E.tensor_tensor(
                    xnf[:, a2 * 512:(a2 + 1) * 512], xt[:, ch * 512:(ch + 1) * 512], rrep[:], OP.mult))
                vv = P.emit("v", lambda E, ch=ch, a2=a2: E.tensor_tensor(
                    xnf[:, a2 * 512:(a2 + 1) * 512], xnf[:, a2 * 512:(a2 + 1) * 512], mrep[:],
                    OP.subtract), inc=cV)
                if gate:
                    if ch == 0:
                        bank_acquire(4)
                    P.wait("t", cV, vv)
                    P.emit("t", lambda En, ch=ch: En.matmul(
                        PB[4][0:16, 0:512], wgf[:, ch * 16:(ch + 1) * 16],
                        xnf[:, (ch % 2) * 512:((ch % 2) + 1) * 512],
                        start=(ch == 0), stop=(ch == NCH - 1)),
                        inc=cT)
                    tgate[ch] = cT.val
                P.wait("s", cV, vv)
                sv = P.emit("s", lambda E, ch=ch, a2=a2: E.activation(
                    xn_hi[:, ch * 512:(ch + 1) * 512], xnf[:, a2 * 512:(a2 + 1) * 512], AF.Copy), inc=cS)
                P.wait("v", cS, sv)
                P.emit("v", lambda E, ch=ch, a2=a2: E.tensor_tensor(
                    xn_lo[:, ch * 512:(ch + 1) * 512], xnf[:, a2 * 512:(a2 + 1) * 512],
                    xn_hi[:, ch * 512:(ch + 1) * 512], OP.subtract), inc=cV)
            vsplit = cV.val
            prev_aff[a] = vsplit
            if write_xn is not None:
                hi_d, lo_d = write_xn
                P.wait("q", cV, vsplit)
                tk1 = QD.dma(hi_d[:, :, t0:t0 + 512].rearrange("c p t -> p c t"),
                             xn_hi.ap().rearrange("p (c t) -> p c t", c=NCH, t=512))
                tk2 = QD.dma(lo_d[:, :, t0:t0 + 512].rearrange("c p t -> p c t"),
                             xn_lo.ap().rearrange("p (c t) -> p c t", c=NCH, t=512))
                prev_xn_toks = [tk1, tk2]
            sink_t = sink(blkq, vsplit)
            if blkq == 2:
                xtile_free["tv"] = (cT.val, cS.val, cV.val)
        return sink_t

    xadd_st = [None, None]

    def attn_a(b, xsrc, xdst, nxt=None):
        A0 = AW_END
        ar_ckvT_h, ar_ckvT_l = A0, A0 + 2 * T
        ar_ckv_h, ar_ckv_l = A0 + 4 * T, A0 + 6 * T
        ar_q1_h, ar_q1_l = A0 + 8 * T, A0 + 10 * T
        ar_qlT_h, ar_qlT_l = A0 + 12 * T, A0 + 14 * T
        base_a = A0 + 16 * T
        ar_wuq_h, ar_wuq_l = base_a, base_a + 256
        ar_M_h, ar_M_l = base_a + 512, base_a + 1536
        ar_pt_h, ar_pt_l = base_a + 2560, base_a + 3584
        ar_ctx_h, ar_ctx_l = base_a + 4608, base_a + 5632
        assert base_a + 6656 <= 59904

        def pth_a(pb_):
            return ar_pt_h + pb_ * 512 if pb_ < 2 else 59904

        def ptl_a(pb_):
            return ar_pt_l + pb_ * 512 if pb_ < 2 else 60416
        pf_ybuf = 0
        pf_ctxf = 2048
        pf_rsum = 3072
        pf_rra = 3584
        pf_wuq = 4096

        def proj_sink(blkq, vsplit):
            t0 = blkq * 512
            P.wait("t", cV, vsplit)
            for lc in range(2):
                bk = lc
                bank_acquire(bk)
                for kc in range(NCH):
                    mm3(PB[bk][:, 0:512],
                        wdkvt_hi[:, kc * L + lc * 128: kc * L + (lc + 1) * 128],
                        wdkvt_lo[:, kc * L + lc * 128: kc * L + (lc + 1) * 128],
                        xn_hi[:, kc * 512:(kc + 1) * 512], xn_lo[:, kc * 512:(kc + 1) * 512],
                        start=(kc == 0), stop=(kc == NCH - 1), inc=(kc == NCH - 1))
                split_psum(PB[bk][:, 0:512], pbm(ar_ckvT_h + lc * T + t0, 512),
                           pbm(ar_ckvT_l + lc * T + t0, 512), cT.val)
                bank_consumed(bk, cS, cV)
            for jc in range(2):
                bk = 2 + jc
                bank_acquire(bk)
                for kc in range(NCH):
                    mm3(PB[bk][:, 0:512],
                        wdqt_hi[:, kc * J + jc * 128: kc * J + (jc + 1) * 128],
                        wdqt_lo[:, kc * J + jc * 128: kc * J + (jc + 1) * 128],
                        xn_hi[:, kc * 512:(kc + 1) * 512], xn_lo[:, kc * 512:(kc + 1) * 512],
                        start=(kc == 0), stop=(kc == NCH - 1), inc=(kc == NCH - 1))
                split_psum(PB[bk][:, 0:512], pbm(ar_q1_h + jc * T + t0, 512),
                           pbm(ar_q1_l + jc * T + t0, 512), cT.val)
                bank_consumed(bk, cS, cV)
            return cT.val

        ln_pass(xsrc, proj_sink)
        ln_prefetch(nxt)

        for lc in range(2):
            for kt in range(NKT):
                bk = 4 + (kt % 2)
                bank_acquire(bk)
                bkb = PB[bk].ap().bitcast(BF16)
                P.emit("t", lambda E, lc=lc, kt=kt, bkb=bkb: E.matmul(
                    bkb[:, 0:128], pbm(ar_ckvT_h + lc * T + kt * 128, 128), c_identb[:],
                    is_transpose=True, start=True, stop=False))
                P.emit("t", lambda E, lc=lc, kt=kt, bkb=bkb: E.matmul(
                    bkb[:, 128:256], pbm(ar_ckvT_l + lc * T + kt * 128, 128), c_identb[:],
                    is_transpose=True, start=False, stop=True), inc=cT)
                tv = cT.val
                P.wait("v", cT, tv)
                P.emit("v", lambda E, lc=lc, kt=kt, bkb=bkb: E.tensor_copy(
                    pbm(ar_ckv_h + kt * L + lc * 128, 128), bkb[:, 0:128]))
                vv = P.emit("v", lambda E, lc=lc, kt=kt, bkb=bkb: E.tensor_copy(
                    pbm(ar_ckv_l + kt * L + lc * 128, 128), bkb[:, 128:256]), inc=cV)
                bank_consumed(bk, cV)

        tqlat_prev = 0
        pend_tail_a = [None]

        def tail_a_recip(qb):
            rb_ = 6 + (qb % 2)
            P.wait("s", cT)
            rr = pfm(pf_rsum, 512)[0:1, :]
            P.emit("s", lambda E, rr=rr, rb_=rb_: E.activation(
                rr, PB[rb_][0:1, 0:512], AF.Ln))
            sv = P.emit("s", lambda E, rr=rr: E.activation(
                rr, rr, AF.Exp, scale=-1.0), inc=cS)
            bank_consumed(rb_, cS)
            return sv

        def emit_tail_a(qb, h, half, hh, sv):
            p_ = qb % 2
            cbs = (2 + 2 * p_, 3 + 2 * p_)
            rb_ = 6 + p_
            pj_ = h // 2
            rr = pfm(pf_rsum, 512)[0:1, :]
            bank_acquire(rb_)
            P.wait("t", cS, sv)
            P.emit("t", lambda E, rr=rr, rb_=rb_: E.matmul(
                PB[rb_][:, 0:512], c_onesf[0:1, :], rr, start=True, stop=True), inc=cT)
            trr = cT.val
            P.wait("s", cT, trr)
            rra = pfm(pf_rra, 512)
            sv = P.emit("s", lambda E, rra=rra, rb_=rb_: E.activation(
                rra, PB[rb_][:, 0:512], AF.Copy), inc=cS)
            bank_consumed(rb_, cS)
            for lc in range(2):
                cf = pfm(pf_ctxf + lc * 512, 512)
                chi = pbm(ar_ctx_h + lc * 512, 512)
                clo = pbm(ar_ctx_l + lc * 512, 512)
                P.wait("v", cS, sv)
                vv2 = P.emit("v", lambda E, cf=cf, cb=cbs[lc], rra=rra: E.tensor_tensor(
                    cf, PB[cb][:, 0:512], rra, OP.mult), inc=cV)
                P.wait("s", cV, vv2)
                sv = P.emit("s", lambda E, chi=chi, cf=cf: E.activation(chi, cf, AF.Copy), inc=cS)
                P.wait("v", cS, sv)
                P.emit("v", lambda E, clo=clo, cf=cf, chi=chi: E.tensor_tensor(
                    clo, cf, chi, OP.subtract), inc=cV)
                bank_consumed(cbs[lc], cV)
            vctx = cV.val
            P.wait("t", cV, vctx)
            bank_acquire(cbs[0])
            for lc in range(2):
                mm3(PB[cbs[0]][half:half + 64, 0:512],
                    veff_hi[:, lc * C + h * 64: lc * C + (h + 1) * 64],
                    veff_lo[:, lc * C + h * 64: lc * C + (h + 1) * 64],
                    pbm(ar_ctx_h + lc * 512, 512), pbm(ar_ctx_l + lc * 512, 512),
                    start=(lc == 0), stop=(lc == 1), tp=(0, half),
                    inc=(lc == 1))
            ty = cT.val
            if hh == 0:
                P.wait("v", cT, ty)
                vv = P.emit("v", lambda E, qb=qb, cb=cbs[0]: E.tensor_copy(
                    pool_f[0:64, pf_ybuf + qb * 512: pf_ybuf + (qb + 1) * 512],
                    PB[cb][0:64, 0:512]), inc=cV)
                bank_consumed(cbs[0], cV)
            else:
                t0 = qb * 512
                a = qb % 2
                if xadd_st[a] is not None:
                    P.waitt("q", xadd_st[a])
                tq = QD.dma(xadd[:, a * 512:(a + 1) * 512],
                            xsrc[pj_, :, t0:t0 + 512])
                P.wait("v", cT, ty)
                P.waitt("v", tq)
                P.emit("v", lambda E, a=a, qb=qb: E.tensor_tensor(
                    xadd[0:64, a * 512:(a + 1) * 512], xadd[0:64, a * 512:(a + 1) * 512],
                    pool_f[0:64, pf_ybuf + qb * 512: pf_ybuf + (qb + 1) * 512], OP.add))
                vv = P.emit("v", lambda E, a=a, cb=cbs[0]: E.tensor_tensor(
                    xadd[64:128, a * 512:(a + 1) * 512], xadd[64:128, a * 512:(a + 1) * 512],
                    PB[cb][64:128, 0:512], OP.add), inc=cV)
                bank_consumed(cbs[0], cV)
                P.wait("q", cV, vv)
                xadd_st[a] = QD.dma(xdst[pj_, :, t0:t0 + 512],
                                    xadd[:, a * 512:(a + 1) * 512])

        P.wait("g", cT)
        P.wait("g", cS)
        P.wait("g", cV)
        wuq_tok = [None, None]
        wuq_vwu = [0, 0]
        wuq_tok[0] = GD.dma(pool_f[:, pf_wuq:pf_wuq + J], awuq[0][:, :])
        for pj in range(8):
            sl = pj % 2
            stg = pool_f[:, pf_wuq + sl * J:pf_wuq + (sl + 1) * J]
            P.wait("s", cT, tqlat_prev)
            P.waitt("s", wuq_tok[sl])
            svw = P.emit("s", lambda E, stg=stg: E.activation(
                pbm(ar_wuq_h, J), stg, AF.Copy), inc=cS)
            P.wait("v", cS, svw)
            vwu = P.emit("v", lambda E, stg=stg: E.tensor_tensor(
                pbm(ar_wuq_l, J), stg, pbm(ar_wuq_h, J), OP.subtract), inc=cV)
            wuq_vwu[sl] = vwu
            if pj < 7:
                nsl = (pj + 1) % 2
                P.wait("g", cV, wuq_vwu[nsl])
                wuq_tok[nsl] = GD.dma(pool_f[:, pf_wuq + nsl * J:pf_wuq + (nsl + 1) * J],
                                      awuq[pj + 1][:, :])

            for hh in range(2):
                h = 2 * pj + hh
                half = hh * 64
                tp = (half, 0)
                P.wait("t", cV, vwu)
                P.wait("t", cT, tqlat_prev)
                for jc in range(2):
                    bk = 2 + jc
                    bank_acquire(bk)
                    for lc in range(2):
                        mm3(PB[bk][:, lc * 128:(lc + 1) * 128],
                            pbm(ar_wuq_h + jc * 128, 128)[half:half + 64, :],
                            pbm(ar_wuq_l + jc * 128, 128)[half:half + 64, :],
                            keff_hi[half:half + 64, pj * L + lc * 128: pj * L + (lc + 1) * 128],
                            keff_lo[half:half + 64, pj * L + lc * 128: pj * L + (lc + 1) * 128],
                            start=(lc == 0), stop=(lc == 1), tp=tp, inc=(lc == 1))
                    split_psum(PB[bk][:, 0:256], pbm(ar_M_h + jc * 256, 256),
                               pbm(ar_M_l + jc * 256, 256), cT.val)
                    bank_consumed(bk, cS, cV)
                vM = cV.val
                P.wait("t", cV, vM)
                for lc in range(2):
                    for tb in range(NTB):
                        bk = (tb % 2)
                        bank_acquire(bk)
                        for jc in range(2):
                            mm3(PB[bk][:, 0:512],
                                pbm(ar_M_h + jc * 256 + lc * 128, 128),
                                pbm(ar_M_l + jc * 256 + lc * 128, 128),
                                pbm(ar_q1_h + jc * T + tb * 512, 512),
                                pbm(ar_q1_l + jc * T + tb * 512, 512),
                                start=(jc == 0), stop=(jc == 1), inc=(jc == 1))
                        split_psum(PB[bk][:, 0:512], pbm(ar_qlT_h + lc * T + tb * 512, 512),
                                   pbm(ar_qlT_l + lc * T + tb * 512, 512), cT.val)
                        bank_consumed(bk, cS, cV)
                tqlat_prev = cT.val
                vql = cV.val
                P.wait("t", cV, vql)

                for qb in range(NTB):
                    nkts = 4 * qb + 4

                    p_ = qb % 2
                    cbs = (2 + 2 * p_, 3 + 2 * p_)
                    rb_ = 6 + p_

                    def consume_kt(ehi, elo, kt, lo, vv, gv, cbs=cbs, rb_=rb_, nkts=nkts):
                        P.wait("t", cV, vv)
                        P.wait("t", cG, gv)
                        if kt == 0:
                            bank_acquire(rb_); bank_acquire(cbs[0]); bank_acquire(cbs[1])
                        P.emit("t", lambda E, ehi=ehi, kt=kt, rb_=rb_, lo=lo: E.matmul(
                            PB[rb_][0:1, lo:512], c_onesb[:, 0:1], ehi[:, lo:512],
                            start=(kt == 0), stop=False))
                        P.emit("t", lambda E, elo=elo, kt=kt, rb_=rb_, lo=lo,
                               e=(kt == nkts - 1): E.matmul(
                            PB[rb_][0:1, lo:512], c_onesb[:, 0:1], elo[:, lo:512],
                            start=False, stop=e))
                        for lc in range(2):
                            mm3(PB[cbs[lc]][:, lo:512],
                                pbm(ar_ckv_h + kt * L + lc * 128, 128),
                                pbm(ar_ckv_l + kt * L + lc * 128, 128),
                                ehi[:, lo:512], elo[:, lo:512],
                                start=(kt == 0), stop=(kt == nkts - 1),
                                inc=(lc == 1))

                    pends = []
                    for kt in range(nkts):
                        sbk = kt % 2
                        pbuf = kt % 3
                        dj = kt - 4 * qb
                        lo = max(dj, 0) * 128
                        w = 512 - lo
                        bank_acquire(sbk)
                        for lc in range(2):
                            mm3(PB[sbk][:, lo:512],
                                pbm(ar_ckvT_h + lc * T + kt * 128, 128),
                                pbm(ar_ckvT_l + lc * T + kt * 128, 128),
                                pbm(ar_qlT_h + lc * T + qb * 512 + lo, w),
                                pbm(ar_qlT_l + lc * T + qb * 512 + lo, w),
                                start=(lc == 0), stop=(lc == 1 and dj < 0),
                                inc=(lc == 1 and dj < 0))
                        if dj >= 0:
                            P.emit("t", lambda E, sbk=sbk, lo=lo: E.matmul(
                                PB[sbk][:, lo:lo + 128], c_identb[:], c_ntri[:],
                                start=False, stop=True), inc=cT)
                        tsc = cT.val
                        ehi = pbm(pth_a(pbuf), 512)
                        elo = pbm(ptl_a(pbuf), 512)
                        ef = xnf[:, (kt % 2) * 512:((kt % 2) + 1) * 512]
                        P.wait("s", cT, tsc)
                        if len(pends) == 2:
                            P.wait("s", cV, pends[0][4])
                        sv2 = P.emit("s", lambda E, sbk=sbk, ef=ef, lo=lo: E.activation(
                            ef[:, lo:512], PB[sbk][:, lo:512], AF.Exp, scale=0.125), inc=cS)
                        bank_consumed(sbk, cS)
                        sv3 = P.emit("s", lambda E, ef=ef, ehi=ehi, lo=lo: E.activation(
                            ehi[:, lo:512], ef[:, lo:512], AF.Copy), inc=cS)
                        P.wait("v", cS, sv3)
                        vv = P.emit("v", lambda E, ef=ef, ehi=ehi, elo=elo, lo=lo: E.tensor_tensor(
                            elo[:, lo:512], ef[:, lo:512], ehi[:, lo:512], OP.subtract), inc=cV)
                        pends.append((ehi, elo, kt, lo, vv, 0))
                        if len(pends) == 3:
                            consume_kt(*pends.pop(0))
                    for p_ in pends:
                        consume_kt(*p_)
                    if pend_tail_a[0] is not None:
                        emit_tail_a(*pend_tail_a[0])
                    pend_tail_a[0] = (qb, h, half, hh, tail_a_recip(qb))
                emit_tail_a(*pend_tail_a[0])
                pend_tail_a[0] = None
        return

    def ffn(b, xsrc, xdst, nxt=None):
        fa_h, fa_hl = 0, 16384
        fa_w1 = 32768
        fa_w2 = 40960
        fa_mask = 49152
        fa_sel = 49408
        fa_z = 49664
        fa_selT = 49920
        fa_selrow = 51968
        fa_repm16 = 52992
        assert fa_repm16 + 16 * 512 <= PBCOLS
        pfa_lg = 0
        pfa_zf = 768
        assert pfa_zf + 4096 <= PFCOLS

        def ga(a, n):
            return xadd[:, a:a + n]
        ga_lgt, ga_cum, ga_off, ga_K, ga_g1 = 0, 16, 272, 288, 304

        def w1sl(sl, hl, kc):
            base = fa_w1 + sl * 4096 + hl * 2048
            return pbm(base + kc * FP, FP)

        def w2sl(sl, hl, fc):
            base = fa_w2 + sl * 4096 + hl * 2048
            return pbm(base + fc * C, C)

        P.emit("v", lambda E: E.memset(ga(ga_off, 16)[0:1, :], 0.0))

        def gate_sink(blkq, vsplit):
            t0b = blkq * 512
            tlg = cT.val
            P.wait("s", cT, tlg)
            lg = pool_f[0:16, pfa_lg:pfa_lg + 512]
            sv = P.emit("s", lambda E, lg=lg: E.activation(lg, PB[4][0:16, 0:512], AF.Identity,
                                                           bias=bg_col[:]), inc=cS)
            bank_consumed(4, cS)
            for tb in range(4):
                tile = 4 * blkq + tb
                lgt = ga(ga_lgt, 16)
                bank_acquire(5)
                P.wait("t", cS, sv)
                P.emit("t", lambda E, lg=lg, tb=tb: E.matmul(
                    PB[5][:, 0:16], lg[:, tb * 128:(tb + 1) * 128], c_ident16[:],
                    is_transpose=True, start=True, stop=True), inc=cT)
                ttr = cT.val
                P.wait("v", cT, ttr)
                P.emit("v", lambda E, lgt=lgt: E.tensor_copy(lgt, PB[5][:, 0:16]))
                bank_consumed(5, cV)
                g1 = ga(ga_g1, 16)
                g2 = ga(ga_g1 + 16, 16)
                m1 = ga(ga_g1 + 32, 1)
                P.emit("v", lambda E, lgt=lgt, m1=m1: E.tensor_reduce(
                    m1, lgt, axis=AX.X, op=OP.max))
                P.emit("v", lambda E, lgt=lgt, g1=g1, m1=m1: E.tensor_scalar(
                    g1, lgt, m1, None, op0=OP.is_ge))
                P.emit("v", lambda E, g1=g1: E.tensor_scalar_mul(g1, g1, 1e9))
                P.emit("v", lambda E, lgt=lgt, g1=g1, g2=g2: E.tensor_tensor(
                    g2, lgt, g1, OP.subtract))
                P.emit("v", lambda E, g2=g2, m1=m1: E.tensor_reduce(
                    m1, g2, axis=AX.X, op=OP.max))
                mk = pbm(fa_mask + tile * 16, 16)
                zb = pbm(fa_z, 16)
                P.emit("v", lambda E, lgt=lgt, m1=m1, mk=mk: E.tensor_scalar(
                    mk, lgt, m1, None, op0=OP.is_ge))
                vz = P.emit("v", lambda E, mk=mk, zb=zb: E.tensor_scalar(
                    zb, mk, -1.0, 1.0, op0=OP.mult, op1=OP.add))
                bank_acquire(5)
                bank_acquire(4)
                P.wait("t", cV, vz)
                P.emit("t", lambda E, zb=zb: E.matmul(PB[5][:, 0:16], c_tri[:], zb,
                                                      start=True, stop=False))
                P.emit("t", lambda E: E.matmul(PB[5][:, 0:16], c_onesf[0:1, :],
                                               ga(ga_off, 16)[0:1, :], start=False, stop=True))
                P.emit("t", lambda E, zb=zb: E.matmul(PB[4][0:1, 0:16], c_onesb[:, 0:1], zb,
                                                      start=True, stop=True), inc=cT)
                tcs = cT.val
                P.wait("v", cT, tcs)
                P.emit("v", lambda E, tile=tile: E.tensor_copy(ga(ga_cum + tile * 16, 16),
                                                               PB[5][:, 0:16]))
                vo = P.emit("v", lambda E: E.tensor_tensor(ga(ga_off, 16)[0:1, :],
                                                           ga(ga_off, 16)[0:1, :],
                                                           PB[4][0:1, 0:16], OP.add))
                bank_consumed(5, cV)
                bank_consumed(4, cV)
                P.wait("t", cV, vo)
            return cT.val

        ln_pass(xsrc, gate_sink, write_xn=(xnh_d[b], xnl_d[b]), gate=True)

        Kr = ga(ga_K, 16)
        vK = P.emit("v", lambda E: E.tensor_scalar_add(ga(ga_g1, 16)[0:1, 0:16],
                                                       ga(ga_off, 16)[0:1, :], -1024.0))
        bank_acquire(5)
        P.wait("t", cV, vK)
        P.emit("t", lambda E: E.matmul(PB[5][:, 0:16], c_onesf[0:1, :],
                                       ga(ga_g1, 16)[0:1, 0:16], start=True, stop=True), inc=cT)
        tKr = cT.val
        P.wait("v", cT, tKr)
        P.emit("v", lambda E, Kr=Kr: E.tensor_copy(Kr, PB[5][:, 0:16]))
        bank_consumed(5, cV)
        for tile in range(NKT):
            mk = pbm(fa_mask + tile * 16, 16)
            sl = pbm(fa_sel + tile * 16, 16)
            zb = pbm(fa_z, 16)
            u = ga(ga_g1, 16)
            P.emit("v", lambda E, mk=mk, zb=zb: E.tensor_scalar(
                zb, mk, -1.0, 1.0, op0=OP.mult, op1=OP.add))
            P.emit("v", lambda E, u=u, tile=tile, Kr=Kr: E.tensor_tensor(
                u, ga(ga_cum + tile * 16, 16), Kr, OP.is_le))
            P.emit("v", lambda E, u=u, zb=zb: E.tensor_tensor(u, u, zb, OP.mult))
            vs = P.emit("v", lambda E, sl=sl, u=u, mk=mk: E.tensor_tensor(sl, u, mk, OP.add))
            bank_acquire(5)
            P.wait("t", cV, vs)
            bkb = PB[5].ap().bitcast(BF16)
            P.emit("t", lambda E, sl=sl, bkb=bkb: E.matmul(bkb[0:16, 0:128], sl, c_identb[:],
                                                           is_transpose=True, start=True,
                                                           stop=True), inc=cT)
            tsT = cT.val
            P.wait("v", cT, tsT)
            P.emit("v", lambda E, tile=tile, bkb=bkb: E.tensor_copy(
                pool_b[0:16, fa_selT + tile * 128: fa_selT + (tile + 1) * 128],
                bkb[0:16, 0:128]), inc=cV)
            bank_consumed(5, cV)

        xst = None
        zaff_pend = None
        wstage_last = [cV.val]

        def emit_zaffine(ztb):
            nonlocal xst
            zt0 = ztb * 512
            if xst is not None:
                P.waitt("q", xst)
            txl = QD.dma(xtile.ap().rearrange("p (c t) -> p c t", c=NCH, t=512),
                         xsrc[:, :, zt0:zt0 + 512].rearrange("c p t -> p c t"))
            for dc in range(8):
                zf = pfm(pfa_zf + dc * 512, 512)
                P.emit("v", lambda E, zf=zf: E.tensor_tensor(zf, zf, rrep[:], OP.mult))
                P.emit("v", lambda E, zf=zf: E.tensor_tensor(zf, zf, mrep[:], OP.subtract))
                P.emit("v", lambda E, zf=zf, dc=dc: E.tensor_scalar(
                    zf, zf, gam_col[:, dc:dc + 1], bet_col[:, dc:dc + 1],
                    op0=OP.mult, op1=OP.add))
                P.waitt("v", txl)
                P.emit("v", lambda E, zf=zf, dc=dc: E.tensor_tensor(
                    xtile[:, dc * 512:(dc + 1) * 512], xtile[:, dc * 512:(dc + 1) * 512],
                    zf, OP.add), inc=cV)
            vfin = cV.val
            P.wait("q", cV, vfin)
            xst = QD.dma(xdst[:, :, zt0:zt0 + 512].rearrange("c p t -> p c t"),
                         xtile.ap().rearrange("p (c t) -> p c t", c=NCH, t=512))
        P.wait("q", cT)
        P.wait("q", cV)
        P.wait("q", cS)
        for c in QCT:
            P.wait("q", c)
        for tb in range(NTB):
            t0b = tb * 512
            th = QD.dma(xn_hi.ap().rearrange("p (c t) -> p c t", c=NCH, t=512),
                        xnh_d[b][:, :, t0b:t0b + 512].rearrange("c p t -> p c t"))
            tl = QD.dma(xn_lo.ap().rearrange("p (c t) -> p c t", c=NCH, t=512),
                        xnl_d[b][:, :, t0b:t0b + 512].rearrange("c p t -> p c t"))
            wtoks = {}

            def load_w1(e, wprev=None):
                sl = e % 2
                if e >= 1:
                    P.wait("g", cV, wtoks[e - 1])
                elif wprev is not None:
                    P.wait("g", cV, wprev)
                t1 = GD.dma(wstage.ap().rearrange("p (c f) -> p c f", c=NCH, f=FP),
                            fw1[e].rearrange("c p f -> p c f"))
                t2 = GD.dma(r3(pbm(fa_w1 + sl * 4096, 2048), NCH, FP),
                            fw1[e].rearrange("c p f -> p c f"))
                P.waitt("v", t1)
                P.waitt("v", t2)
                vv = P.emit("v", lambda E, sl=sl: E.tensor_tensor(
                    pbm(fa_w1 + sl * 4096 + 2048, 2048), wstage[:, 0:2048],
                    pbm(fa_w1 + sl * 4096, 2048), OP.subtract), inc=cV)
                wtoks[e] = vv

            pendr = None
            for e in range(E):
                rb = 2 + (e % 2)
                bank_acquire(rb)
                P.emit("t", lambda E, e=e, rb=rb, tb=tb: E.matmul(
                    PB[rb][:, 0:512], c_onehot[:, e * 128:(e + 1) * 128],
                    pool_b[0:16, fa_selT + tb * 512: fa_selT + (tb + 1) * 512],
                    start=True, stop=True), inc=cT)
                if pendr is not None:
                    pe, ptv, prb = pendr
                    P.wait("v", cT, ptv)
                    P.emit("v", lambda E, pe=pe, prb=prb: E.tensor_copy(
                        pbm(fa_repm16 + pe * 512, 512), PB[prb][:, 0:512]), inc=cV)
                    bank_consumed(prb, cV)
                pendr = (e, cT.val, rb)
            pe, ptv, prb = pendr
            P.wait("v", cT, ptv)
            P.emit("v", lambda E, pe=pe, prb=prb: E.tensor_copy(
                pbm(fa_repm16 + pe * 512, 512), PB[prb][:, 0:512]), inc=cV)
            bank_consumed(prb, cV)

            tproj = {}
            xnf_user = {0: 0, 1: 0}
            load_w1(0, wprev=wstage_last[0])
            for e in range(E):
                if e + 1 < E:
                    if e >= 1:
                        P.wait("g", cT, tproj[e - 1])
                    load_w1(e + 1)
                sl = e % 2
                P.wait("t", cV, wtoks[e])
                P.waitt("t", th)
                P.waitt("t", tl)
                for fc in range(2):
                    bk = (e * 2 + fc) % 4
                    bank_acquire(bk)
                    for kc in range(NCH):
                        mm3(PB[bk][:, 0:512],
                            w1sl(sl, 0, kc)[:, fc * 128:(fc + 1) * 128],
                            w1sl(sl, 1, kc)[:, fc * 128:(fc + 1) * 128],
                            xn_hi[:, kc * 512:(kc + 1) * 512], xn_lo[:, kc * 512:(kc + 1) * 512],
                            start=(kc == 0), stop=(kc == NCH - 1), inc=(kc == NCH - 1))
                    tproj[e] = cT.val
                    hh = pbm(fa_h + (e * 2 + fc) * 512, 512)
                    hl = pbm(fa_hl + (e * 2 + fc) * 512, 512)
                    a = fc
                    P.wait("s", cT, tproj[e])
                    P.wait("s", cV, xnf_user[a])
                    sv2 = P.emit("s", lambda E, a=a, bk=bk, e=e, fc=fc: E.activation(
                        xnf[:, a * 512:(a + 1) * 512], PB[bk][:, 0:512], AF.Gelu,
                        bias=b1_col[:, e * 2 + fc:e * 2 + fc + 1]), inc=cS)
                    P.wait("v", cS, sv2)
                    P.emit("v", lambda E, hh=hh, a=a: E.tensor_copy(
                        hh, xnf[:, a * 512:(a + 1) * 512]))
                    xnf_user[a] = P.emit("v", lambda E, hl=hl, a=a, hh=hh: E.tensor_tensor(
                        hl, xnf[:, a * 512:(a + 1) * 512], hh, OP.subtract), inc=cV)
                    bank_consumed(bk, cS)
                for fc in range(2):
                    hh = pbm(fa_h + (e * 2 + fc) * 512, 512)
                    hl = pbm(fa_hl + (e * 2 + fc) * 512, 512)
                    P.emit("v", lambda E, hh=hh, e=e: E.tensor_tensor(
                        hh, hh, pbm(fa_repm16 + e * 512, 512), OP.mult))
                    P.emit("v", lambda E, hl=hl, e=e: E.tensor_tensor(
                        hl, hl, pbm(fa_repm16 + e * 512, 512), OP.mult), inc=cV)
            vhall = cV.val

            if zaff_pend is not None:
                emit_zaffine(zaff_pend)
                zaff_pend = None

            w2toks = {}

            def load_w2(e, wprev=None):
                sl = e % 2
                if e >= 1:
                    P.wait("g", cV, w2toks[e - 1])
                elif wprev is not None:
                    P.wait("g", cV, wprev)
                t1 = GD.dma(wstage.ap().rearrange("p (c f) -> p c f", c=2, f=C),
                            fw2[e].rearrange("c p f -> p c f"))
                t2 = GD.dma(r3(pbm(fa_w2 + sl * 4096, 2048), 2, C),
                            fw2[e].rearrange("c p f -> p c f"))
                P.waitt("v", t1)
                P.waitt("v", t2)
                vv = P.emit("v", lambda E, sl=sl: E.tensor_tensor(
                    pbm(fa_w2 + sl * 4096 + 2048, 2048), wstage[:, 0:2048],
                    pbm(fa_w2 + sl * 4096, 2048), OP.subtract), inc=cV)
                w2toks[e] = vv

            tw2 = {}
            load_w2(0, wprev=wtoks[E - 1])
            load_w2(1)
            wstage_last[0] = w2toks[1]
            P.wait("t", cV, vhall)
            for dc in range(8):
                bank_acquire(dc)
            for e in range(E):
                if e + 1 < E and e >= 1:
                    P.wait("g", cT, tw2[e - 1])
                    load_w2(e + 1)
                    wstage_last[0] = w2toks[e + 1]
                sl = e % 2
                P.wait("t", cV, w2toks[e])
                for dc in range(8):
                    if e == 0:
                        P.emit("t", lambda E, dc=dc, tb=tb: E.matmul(
                            PB[dc][:, 0:512], b2_hi[:, dc * 128:(dc + 1) * 128],
                            pool_b[0:16, fa_selT + tb * 512: fa_selT + (tb + 1) * 512],
                            start=True, stop=False))
                        P.emit("t", lambda E, dc=dc, tb=tb: E.matmul(
                            PB[dc][:, 0:512], b2_lo[:, dc * 128:(dc + 1) * 128],
                            pool_b[0:16, fa_selT + tb * 512: fa_selT + (tb + 1) * 512],
                            start=False, stop=False))
                    for fc in range(2):
                        hh = pbm(fa_h + (e * 2 + fc) * 512, 512)
                        hl = pbm(fa_hl + (e * 2 + fc) * 512, 512)
                        mm3(PB[dc][:, 0:512],
                            w2sl(sl, 0, fc)[:, dc * 128:(dc + 1) * 128],
                            w2sl(sl, 1, fc)[:, dc * 128:(dc + 1) * 128],
                            hh, hl,
                            start=False, stop=(e == E - 1 and fc == 1),
                            inc=(fc == 1))
                tw2[e] = cT.val
            tout = cT.val

            for dc in (2, 3, 0, 1, 4, 5, 6, 7):
                zf = pfm(pfa_zf + dc * 512, 512)
                P.waitt("v", th)
                P.waitt("v", tl)
                P.emit("v", lambda E, zf=zf, dc=dc: E.tensor_tensor(
                    zf, xn_hi[:, dc * 512:(dc + 1) * 512], xn_lo[:, dc * 512:(dc + 1) * 512],
                    OP.add))
                P.wait("v", cT, tout)
                P.emit("v", lambda E, zf=zf, dc=dc: E.tensor_tensor(
                    zf, zf, PB[dc][:, 0:512], OP.add), inc=cV)
                bank_consumed(dc, cV)
            vz = cV.val
            ln_stats(lambda ch: pfm(pfa_zf + ch * 512, 512),
                     twait=lambda e: P.wait(e, cV, vz), blk=tb)
            zaff_pend = tb
            P.wait("q", cT, tout)
            P.wait("q", cV, vz)
        if zaff_pend is not None:
            emit_zaffine(zaff_pend)
        if nxt is not None:
            if xst is not None:
                P.waitt("q", xst)
            lnpre[0] = QD.dma(xtile.ap().rearrange("p (c t) -> p c t", c=NCH, t=512),
                              nxt[:, :, 0:512].rearrange("c p t -> p c t"))
        return

    def attn_b(b, xsrc, xdst, nxt=None):
        ba_y, ba_yl = 0, 16384
        ba_w = 32768
        ba_q, ba_ql = 38912, 40960
        ba_k, ba_kl = 43008, 45056
        ba_vT, ba_vTl = 47104, 47616
        ba_vt, ba_vtl = 48128, 50176
        ba_pt, ba_ptl = 52224, 53248

        def pth(pb_):
            return ba_pt + pb_ * 512 if pb_ < 2 else 60416

        def ptl(pb_):
            return ba_ptl + pb_ * 512 if pb_ < 2 else 60928
        pfb_rs = 0
        pfb_ef = 768
        pfb_yf = 2816
        pfb_rr = 3328
        pfb_ef3 = 3840

        def noop_sink(blkq, vsplit):
            return 0

        ln_pass(xsrc, noop_sink, write_xn=(xnh_d[b], xnl_d[b]))
        ln_prefetch(nxt)
        for c in QCT:
            P.wait("q", c)

        def wsl(base, hl, kc):
            return pbm(base + hl * 1024 + kc * 128, 128)

        def load_wqkv(pj, wprev):
            slot = ba_w if pj % 2 == 0 else 54272
            toks = []
            for wi, wd in enumerate((bwq, bwk, bwv)):
                base = slot + wi * 2048
                if wprev is not None:
                    P.wait("g", cV, wprev)
                t1 = GD.dma(wstage[:, 0:1024].rearrange("p (k f) -> p k f", k=NCH, f=128),
                            wd[pj].rearrange("(k p) f -> p k f", p=128))
                t2 = GD.dma(r3(pbm(base, 1024), NCH, 128),
                            wd[pj].rearrange("(k p) f -> p k f", p=128))
                P.waitt("v", t1)
                P.waitt("v", t2)
                wprev = P.emit("v", lambda E, base=base: E.tensor_tensor(
                    pbm(base + 1024, 1024), wstage[:, 0:1024], pbm(base, 1024), OP.subtract),
                    inc=cV)
                toks.append(wprev)
            return toks

        P.wait("g", cV)
        P.wait("g", cT)
        wvtoks = load_wqkv(0, None)
        tproj_prev = 0
        for pj in range(8):
            slot = ba_w if pj % 2 == 0 else 54272
            for tb in range(NTB):
                t0b = tb * 512
                P.wait("q", cT, tproj_prev)
                th = QD.dma(xn_hi.ap().rearrange("p (c t) -> p c t", c=NCH, t=512),
                            xnh_d[b][:, :, t0b:t0b + 512].rearrange("c p t -> p c t"))
                tl = QD.dma(xn_lo.ap().rearrange("p (c t) -> p c t", c=NCH, t=512),
                            xnl_d[b][:, :, t0b:t0b + 512].rearrange("c p t -> p c t"))
                P.waitt("t", th)
                P.waitt("t", tl)
                P.wait("t", cV, wvtoks[2])
                svv = None
                for wi, (dsth, dstl) in enumerate(((ba_q, ba_ql), (ba_k, ba_kl), (ba_vT, ba_vTl))):
                    bk = wi
                    base = slot + wi * 2048
                    bank_acquire(bk)
                    for kc in range(NCH):
                        mm3(PB[bk][:, 0:512],
                            wsl(base, 0, kc), wsl(base, 1, kc),
                            xn_hi[:, kc * 512:(kc + 1) * 512], xn_lo[:, kc * 512:(kc + 1) * 512],
                            start=(kc == 0), stop=(kc == NCH - 1), inc=(kc == NCH - 1))
                    if wi < 2:
                        svv = split_psum(PB[bk][:, 0:512], pbm(dsth + t0b, 512),
                                         pbm(dstl + t0b, 512), cT.val)
                    else:
                        svv = split_psum(PB[bk][:, 0:512], pbm(ba_vT, 512), pbm(ba_vTl, 512),
                                         cT.val)
                    bank_consumed(bk, cS, cV)
                tproj_prev = cT.val
                P.wait("t", cS, svv[0])
                P.wait("t", cV, svv[1])
                for tt in range(4):
                    kt = 4 * tb + tt
                    bk = 3 + (tt % 2)
                    bank_acquire(bk)
                    bkb = PB[bk].ap().bitcast(BF16)
                    P.emit("t", lambda E, tt=tt, bkb=bkb: E.matmul(
                        bkb[:, 0:128], pbm(ba_vT + tt * 128, 128), c_identb[:],
                        is_transpose=True, start=True, stop=False))
                    P.emit("t", lambda E, tt=tt, bkb=bkb: E.matmul(
                        bkb[:, 128:256], pbm(ba_vTl + tt * 128, 128), c_identb[:],
                        is_transpose=True, start=False, stop=True), inc=cT)
                    tv = cT.val
                    P.wait("v", cT, tv)
                    P.emit("v", lambda E, kt=kt, bkb=bkb: E.tensor_copy(
                        pbm(ba_vt + kt * 128, 128), bkb[:, 0:128]))
                    vv = P.emit("v", lambda E, kt=kt, bkb=bkb: E.tensor_copy(
                        pbm(ba_vtl + kt * 128, 128), bkb[:, 128:256]), inc=cV)
                    bank_consumed(bk, cV)
            vproj = cV.val
            if pj + 1 < 8:
                wvtoks = load_wqkv(pj + 1, wvtoks[2])
            for hh in range(2):
                half = hh * 64
                tp = (half, 0)
                P.wait("t", cV, vproj)
                gk = 0

                def win(qb, kt):
                    rel = kt - 4 * qb
                    jlo = max(rel, 0)
                    jhi = min(rel + 1, 3)
                    return jlo * 128, (jhi + 1) * 128, rel

                def consume_bkt(ehi, elo, qb, kt, first, last, vv):
                    rb = 2 + (qb % 2)
                    yb = 6 + (qb % 2)
                    lo, hi, rel = win(qb, kt)
                    P.wait("t", cV, vv)
                    if first:
                        bank_acquire(rb)
                        bank_acquire(yb)
                        P.emit("t", lambda E, rb=rb, ehi=ehi: E.matmul(
                            PB[rb][0:1, 0:512], c_zerob[:, 0:1], ehi, start=True, stop=False))
                        P.emit("t", lambda E, yb=yb, ehi=ehi, half=half: E.matmul(
                            PB[yb][half:half + 64, 0:512], c_zerob[:, 0:64], ehi,
                            start=True, stop=False, tile_position=(0, half)))
                    P.emit("t", lambda E, ehi=ehi, rb=rb, lo=lo, hi=hi: E.matmul(
                        PB[rb][0:1, lo:hi], c_onesb[:, 0:1], ehi[:, lo:hi],
                        start=False, stop=False))
                    P.emit("t", lambda E, elo=elo, rb=rb, lo=lo, hi=hi, last=last: E.matmul(
                        PB[rb][0:1, lo:hi], c_onesb[:, 0:1], elo[:, lo:hi],
                        start=False, stop=last))
                    mm3(PB[yb][half:half + 64, lo:hi],
                        pbm(ba_vt + kt * 128, 128)[:, half:half + 64],
                        pbm(ba_vtl + kt * 128, 128)[:, half:half + 64],
                        ehi[:, lo:hi], elo[:, lo:hi],
                        start=False, stop=last, tp=(0, half), inc=last)

                def emit_tail(qb, sv):
                    yb = 6 + (qb % 2)
                    rs = pool_f[0:1, pfb_rs:pfb_rs + 512]
                    bank_acquire(4)
                    P.wait("t", cS, sv)
                    P.emit("t", lambda E, rs=rs, half=half: E.matmul(
                        PB[4][half:half + 64, 0:512], c_onesf[0:1, 0:64], rs,
                        start=True, stop=True, tile_position=(0, half)), inc=cT)
                    trr = cT.val
                    yh = pbm(ba_y + pj * 2048 + qb * 512, 512)[half:half + 64, :]
                    yl = pbm(ba_yl + pj * 2048 + qb * 512, 512)[half:half + 64, :]
                    yf = pool_f[half:half + 64, pfb_yf:pfb_yf + 512]
                    rrb = pool_f[half:half + 64, pfb_rr:pfb_rr + 512]
                    P.wait("s", cT, trr)
                    sv = P.emit("s", lambda E, rrb=rrb, half=half: E.activation(
                        rrb, PB[4][half:half + 64, 0:512], AF.Copy), inc=cS)
                    bank_consumed(4, cS)
                    P.wait("v", cS, sv)
                    vv = P.emit("v", lambda E, yf=yf, rrb=rrb, yb=yb, half=half: E.tensor_tensor(
                        yf, PB[yb][half:half + 64, 0:512], rrb, OP.mult), inc=cV)
                    bank_consumed(yb, cV)
                    P.wait("s", cV, vv)
                    sv = P.emit("s", lambda E, yh=yh, yf=yf: E.activation(yh, yf, AF.Copy), inc=cS)
                    P.wait("v", cS, sv)
                    P.emit("v", lambda E, yl=yl, yf=yf, yh=yh: E.tensor_tensor(
                        yl, yf, yh, OP.subtract), inc=cV)

                pends = []
                pend_tail = None
                efvv = {0: 0, 1: 0, 2: 0}

                def pop_one():
                    nonlocal pend_tail
                    ent = pends.pop(0)
                    consume_bkt(*ent)
                    if ent[5]:
                        if pend_tail is not None:
                            emit_tail(*pend_tail)
                        qb_ = ent[2]
                        rb = 2 + (qb_ % 2)
                        ty = cT.val
                        P.wait("s", cT, ty)
                        rs = pool_f[0:1, pfb_rs:pfb_rs + 512]
                        P.emit("s", lambda E, rs=rs, rb=rb: E.activation(
                            rs, PB[rb][0:1, 0:512], AF.Ln))
                        sv = P.emit("s", lambda E, rs=rs: E.activation(
                            rs, rs, AF.Exp, scale=-1.0), inc=cS)
                        bank_consumed(rb, cS)
                        pend_tail = (qb_, sv)
                for qb in range(NTB):
                    kts = [k for k in range(4 * qb - 1, 4 * qb + 4) if k >= 0]
                    for ki, kt in enumerate(kts):
                        lo, hi, rel = win(qb, kt)
                        w = hi - lo
                        sbk = gk % 2
                        pb3 = gk % 3
                        gk += 1
                        bank_acquire(sbk)
                        mm3(PB[sbk][:, lo:hi],
                            pbm(ba_k + kt * 128, 128)[half:half + 64, :],
                            pbm(ba_kl + kt * 128, 128)[half:half + 64, :],
                            pbm(ba_q + qb * 512 + lo, w)[half:half + 64, :],
                            pbm(ba_ql + qb * 512 + lo, w)[half:half + 64, :],
                            start=True, stop=False, tp=tp, inc=False)
                        mjs = [jj for jj in (rel, rel + 1) if 0 <= jj <= 3]
                        for mi, jj in enumerate(mjs):
                            msk = c_ntri if jj == rel else c_ntril
                            P.emit("t", lambda E, sbk=sbk, jj=jj, msk=msk,
                                   e=(mi == len(mjs) - 1): E.matmul(
                                PB[sbk][:, jj * 128:(jj + 1) * 128], c_identb[:], msk[:],
                                start=False, stop=e),
                                inc=cT if mi == len(mjs) - 1 else None)
                        tsc = cT.val
                        ehi = pbm(pth(pb3), 512)
                        elo = pbm(ptl(pb3), 512)
                        ef = pfm(pfb_ef + pb3 * 1024 if pb3 < 2 else pfb_ef3, 512)
                        P.wait("s", cT, tsc)
                        P.wait("s", cV, efvv[pb3])
                        sv2 = P.emit("s", lambda E, ef=ef, sbk=sbk, lo=lo, hi=hi: E.activation(
                            ef[:, lo:hi], PB[sbk][:, lo:hi], AF.Exp, scale=0.125), inc=cS)
                        bank_consumed(sbk, cS)
                        sv3 = P.emit("s", lambda E, ef=ef, ehi=ehi, lo=lo, hi=hi: E.activation(
                            ehi[:, lo:hi], ef[:, lo:hi], AF.Copy), inc=cS)
                        P.wait("v", cS, sv3)
                        vv = P.emit("v", lambda E, ef=ef, ehi=ehi, elo=elo, lo=lo, hi=hi:
                                    E.tensor_tensor(elo[:, lo:hi], ef[:, lo:hi], ehi[:, lo:hi],
                                                    OP.subtract), inc=cV)
                        efvv[pb3] = vv
                        pends.append((ehi, elo, qb, kt, ki == 0, ki == len(kts) - 1, vv))
                        if len(pends) == 3:
                            pop_one()
                while pends:
                    pop_one()
                emit_tail(*pend_tail)
        P.wait("g", cV, wvtoks[2])
        P.wait("g", cT, tproj_prev)
        wotoks = []
        for pj in range(8):
            base = ba_w + pj * 2048
            if pj >= 1:
                P.wait("g", cV, wotoks[pj - 1])
            t1 = GD.dma(wstage[:, 0:1024], bwot[pj][:, :])
            t2 = GD.dma(pbm(base, 1024), bwot[pj][:, :])
            P.waitt("v", t1)
            P.waitt("v", t2)
            vv = P.emit("v", lambda E, base=base: E.tensor_tensor(
                pbm(base + 1024, 1024), wstage[:, 0:1024], pbm(base, 1024), OP.subtract), inc=cV)
            wotoks.append(vv)
        vwo = cV.val
        P.wait("t", cV, vwo)
        xst = None
        for fc in range(NCH):
            for tb in range(NTB):
                t0b = tb * 512
                bk = (fc * NTB + tb) % 2
                bank_acquire(bk)
                for pj in range(8):
                    mm3(PB[bk][:, 0:512],
                        pbm(ba_w + pj * 2048 + fc * 128, 128),
                        pbm(ba_w + pj * 2048 + 1024 + fc * 128, 128),
                        pbm(ba_y + pj * 2048 + t0b, 512), pbm(ba_yl + pj * 2048 + t0b, 512),
                        start=(pj == 0), stop=(pj == 7), inc=(pj == 7))
                to = cT.val
                a = (fc * NTB + tb) % 2
                if xst is not None:
                    P.waitt("q", xst)
                txl = QD.dma(xadd[:, a * 512:(a + 1) * 512], xsrc[fc, :, t0b:t0b + 512])
                P.waitt("v", txl)
                P.wait("v", cT, to)
                vv = P.emit("v", lambda E, a=a, bk=bk: E.tensor_tensor(
                    xadd[:, a * 512:(a + 1) * 512], xadd[:, a * 512:(a + 1) * 512],
                    PB[bk][:, 0:512], OP.add), inc=cV)
                bank_consumed(bk, cV)
                P.wait("q", cV, vv)
                xst = QD.dma(xdst[fc, :, t0b:t0b + 512], xadd[:, a * 512:(a + 1) * 512])
        return

    srcs = [xin, xres, xres, xres]
    dsts = [xres, xres, xres, xres]
    dsts[N_SUBLAYERS - 1] = out_fm

    seq = [(attn_a, b, srcs[0][b], dsts[0][b]) for b in range(B)]
    if N_SUBLAYERS > 1:
        seq += [(ffn, b, srcs[1][b], dsts[1][b]) for b in range(B)]
    if N_SUBLAYERS > 2:
        seq += [(attn_b, b, srcs[2][b], dsts[2][b]) for b in range(B)]
    if N_SUBLAYERS > 3:
        seq += [(ffn, b, srcs[3][b], dsts[3][b]) for b in range(B)]
    for i, (fn, b, s_, d_) in enumerate(seq):
        fn(b, s_, d_, nxt=(seq[i + 1][2] if i + 1 < len(seq) else None))
        stage_barrier()

    for c in QCT + WCT:
        P.wait("q", c)
        P.wait("g", c)

    with nc.Block() as blk:
        blk.sync(lambda E: [f(E) for f in P.q["q"]])
        blk.tensor(lambda E: [f(E) for f in P.q["t"]])
        blk.vector(lambda E: [f(E) for f in P.q["v"]])
        blk.scalar(lambda E: [f(E) for f in P.q["s"]])
        blk.gpsimd(lambda E: [f(E) for f in P.q["g"]])

    ctx.close()
    return nc


def host_inputs(inputs, core):
    x = inputs["L0_attn_a_x"]
    m = {}
    xb = np.ascontiguousarray(x[2 * core:2 * core + 2].transpose(0, 2, 1))
    m["xin"] = xb.reshape(B, NCH, 128, T)
    m["awdqt"] = np.ascontiguousarray(inputs["L0_attn_a_W_dq"].T).reshape(NCH, 128, J)
    m["awdq"] = inputs["L0_attn_a_W_dq"].reshape(2, 128, C)
    m["awuqt"] = np.ascontiguousarray(inputs["L0_attn_a_W_uq"].T).reshape(2, 128, C)
    m["awuq"] = inputs["L0_attn_a_W_uq"].reshape(NCH, 128, J)
    m["awuk"] = inputs["L0_attn_a_W_uk"].reshape(NCH, 128, L)
    m["awdkvt"] = np.ascontiguousarray(inputs["L0_attn_a_W_dkv"].T).reshape(NCH, 128, L)
    m["awuv"] = inputs["L0_attn_a_W_uv"].reshape(NCH, 128, L)
    m["awot"] = np.ascontiguousarray(inputs["L0_attn_a_W_o"].T).reshape(NCH, 128, C)
    m["bwq"] = np.ascontiguousarray(inputs["L0_attn_b_Wq"].T.reshape(C, NCH, 128).transpose(1, 0, 2))
    m["bwk"] = np.ascontiguousarray(inputs["L0_attn_b_Wk"].T.reshape(C, NCH, 128).transpose(1, 0, 2))
    m["bwv"] = np.ascontiguousarray(inputs["L0_attn_b_Wv"].T.reshape(C, NCH, 128).transpose(1, 0, 2))
    m["bwot"] = np.ascontiguousarray(inputs["L0_attn_b_Wo"].T).reshape(NCH, 128, C)
    m["fwg"] = inputs["L0_ffn_wg"].reshape(NCH, 128, E)
    m["fbg"] = inputs["L0_ffn_bg"].reshape(16, 1)
    m["fw1"] = inputs["L0_ffn_w1"].reshape(E, NCH, 128, FP)
    m["fb1"] = inputs["L0_ffn_b1"]
    m["fw2"] = inputs["L0_ffn_w2"].reshape(E, 2, 128, C)
    m["fb2"] = inputs["L0_ffn_b2"]
    m["fgam"] = inputs["L0_ffn_gamma"].reshape(NCH, 128)
    m["fbet"] = inputs["L0_ffn_beta"].reshape(NCH, 128)
    m["ident16"] = np.eye(16, dtype=np.float32)
    import ml_dtypes
    m["identb"] = np.eye(128, dtype=ml_dtypes.bfloat16)
    ii = np.arange(128)
    m["tri_incl"] = (ii[None, :] >= ii[:, None]).astype(ml_dtypes.bfloat16)
    m["ntri"] = (((ii[None, :] >= ii[:, None]).astype(np.float32) - 1.0)
                 * 1e9).astype(ml_dtypes.bfloat16)
    m["ntril"] = (((ii[None, :] < ii[:, None]).astype(np.float32) - 1.0)
                  * 1e9).astype(ml_dtypes.bfloat16)
    m["onesf"] = np.ones((128, 128), dtype=np.float32)
    m["onesb"] = np.ones((128, 128), dtype=ml_dtypes.bfloat16)
    oh = np.zeros((16, 2048), dtype=ml_dtypes.bfloat16)
    for e in range(16):
        oh[e, e * 128:(e + 1) * 128] = 1
    m["onehot"] = oh
    m["zerob"] = np.zeros((128, 64), dtype=ml_dtypes.bfloat16)
    return m


_nc_cache = {}


def kernel(**inputs):
    key = N_SUBLAYERS
    if key not in _nc_cache:
        _nc_cache[key] = build_program()
    nc = _nc_cache[key]
    in_maps = [host_inputs(inputs, core) for core in range(8)]
    res = run_bass_kernel_spmd(nc, in_maps, core_ids=list(range(8)))
    outs = []
    for core in range(8):
        o = res.results[core]["out_fm"].reshape(B, C, T)
        outs.append(np.ascontiguousarray(o.transpose(0, 2, 1)))
    return np.concatenate(outs, axis=0)
